# version 20
# baseline (speedup 1.0000x reference)
import contextlib
import numpy as np
import concourse.bass as bass
import concourse.mybir as mybir
from concourse.bass_utils import run_bass_kernel_spmd

F32 = mybir.dt.float32
BF16 = mybir.dt.bfloat16
AF = mybir.ActivationFunctionType
ALU = mybir.AluOpType
AX = mybir.AxisListType

NCORES = 8
T = 4096
HID = 1024
FC = 8
TILE = 512
NT = T // TILE
C = 128
NCH = TILE // C
SC = 32
NSUB = C // SC
NH = 4
DK = 128
DV = 256
DH = 4096
FT = 1024
NFT = T // FT
SLABW = 2048
N_ASLAB = 17
N_FSLAB = 176
RING = 6
EPS = 1e-5



def _lhsT_chunks(w, kparts, oparts):
    O, K = w.shape
    noc, nkc = O // oparts, K // kparts
    t = w.reshape(noc, oparts, nkc, kparts)
    return np.ascontiguousarray(t.transpose(0, 2, 3, 1))


def pack_attn_slabs(p):
    slabs = np.zeros((N_ASLAB, 128, SLABW), np.float32)
    s0 = []
    wx0 = p["W_x0"]
    for n in range(5):
        for k in range(8):
            s0.append(wx0[n * 32:(n + 1) * 32, k * 128:(k + 1) * 128].T)
    ww1 = p["W_w1"]
    for k in range(8):
        s0.append(ww1[:, k * 128:(k + 1) * 128].T)
    col = 0
    for ch in s0:
        w = ch.shape[1]
        slabs[0, :, col:col + w] = ch
        col += w
    assert col == 1792

    def fill(weight, base_slab):
        chs = _lhsT_chunks(weight, 128, 128)
        noc, nkc = chs.shape[0], chs.shape[1]
        si, col = base_slab, 0
        for oc in range(noc):
            for kc in range(nkc):
                slabs[si, :, col:col + 128] = chs[oc, kc]
                col += 128
                if col == SLABW:
                    si += 1
                    col = 0

    fill(p["W_r"], 1)
    fill(p["W_k"], 3)
    wv = p["W_v"]
    si, col = 5, 0
    for half in range(2):
        for k in range(8):
            blk = wv[half * 512:(half + 1) * 512, k * 128:(k + 1) * 128].T
            slabs[si, :, col:col + 512] = blk
            col += 512
            if col == SLABW:
                si += 1
                col = 0
    fill(p["W_g"], 9)
    fill(p["W_o"], 13)
    return slabs


def pack_ffn_slabs(p):
    slabs = np.zeros((N_FSLAB, 128, SLABW), np.float32)
    w1 = _lhsT_chunks(p["w1_w"], 128, 128)
    w2 = _lhsT_chunks(p["w2_w"], 128, 128)
    for o in range(32):
        slabs[o, :, 0:1024] = w1[o].transpose(1, 0, 2).reshape(128, 1024)
        slabs[o, :, 1024:2048] = w2[o].transpose(1, 0, 2).reshape(128, 1024)
    h1 = _lhsT_chunks(p["h1_w"], 128, 128)
    h2 = _lhsT_chunks(p["h2_w"], 128, 128)
    for o in range(32):
        slabs[32 + 2 * o] = h1[o, 0:16].transpose(1, 0, 2).reshape(128, 2048)
        slabs[32 + 2 * o + 1] = h1[o, 16:32].transpose(1, 0, 2).reshape(128, 2048)
        slabs[96 + 2 * o] = h2[o, 0:16].transpose(1, 0, 2).reshape(128, 2048)
        slabs[96 + 2 * o + 1] = h2[o, 16:32].transpose(1, 0, 2).reshape(128, 2048)
    w3 = _lhsT_chunks(p["w3_w"], 128, 128)
    for o in range(8):
        slabs[160 + 2 * o] = w3[o, 0:16].transpose(1, 0, 2).reshape(128, 2048)
        slabs[160 + 2 * o + 1] = w3[o, 16:32].transpose(1, 0, 2).reshape(128, 2048)
    return slabs


def pack_awx2(p):
    out = np.zeros((32, 5 * 8 * 128), np.float32)
    wx2 = p["W_x2"]
    i = 0
    for n in range(5):
        for fc in range(8):
            out[:, i * 128:(i + 1) * 128] = \
                wx2[fc * 128:(fc + 1) * 128, n * 32:(n + 1) * 32].T
            i += 1
    return out


def pack_aww2(p):
    out = np.zeros((64, 4 * 128), np.float32)
    ww2 = p["W_w2"]
    for oc in range(4):
        out[:, oc * 128:(oc + 1) * 128] = ww2[oc * 128:(oc + 1) * 128, :].T
    return out


AV_MU0, AV_XB, AV_BW2, AV_U, AV_GNW, AV_GNB, AV_N = 0, 8, 48, 52, 56, 64, 72


def pack_avecs(p):
    out = np.zeros((128, AV_N), np.float32)
    out[:, AV_MU0:AV_MU0 + 8] = p["mu0"].reshape(8, 128).T
    for n in range(5):
        out[:, AV_XB + n * 8:AV_XB + (n + 1) * 8] = p["x_bias"][n].reshape(8, 128).T
    out[:, AV_BW2:AV_BW2 + 4] = p["b_w2"].reshape(4, 128).T
    out[:, AV_U:AV_U + 4] = p["bonus"].T
    out[:, AV_GNW:AV_GNW + 8] = p["gn_weight"].reshape(8, 128).T
    out[:, AV_GNB:AV_GNB + 8] = p["gn_bias"].reshape(8, 128).T
    return out


FS_W1, FS_W2, FS_H1, FS_H2, FS_W3, FS_N = 0, 32, 64, 96, 128, 136


def pack_fscales(p):
    out = np.zeros((128, FS_N), np.float32)
    out[:, FS_W1:FS_W1 + 32] = p["w1_s"].reshape(32, 128).T
    out[:, FS_W2:FS_W2 + 32] = p["w2_s"].reshape(32, 128).T
    out[:, FS_H1:FS_H1 + 32] = p["h1_s"].reshape(32, 128).T
    out[:, FS_H2:FS_H2 + 32] = p["h2_s"].reshape(32, 128).T
    out[:, FS_W3:FS_W3 + 8] = p["w3_s"].reshape(8, 128).T
    return out



class Sem:
    def __init__(self, handle):
        self.h = handle
        self.n = 0


class EP:
    def __init__(self, name):
        self.name = name
        self.cmds = []
        self.selfsem = None

    def ins(self, meth, *args, inc=None, inck=1, **kw):
        chain = self.selfsem is not None and meth != "dma_start"
        if chain and self.selfsem.n > 0:
            self.cmds.append(("wait_ge", (self.selfsem.h, self.selfsem.n), {},
                              None, 0))
        if inc is None and chain:
            inc = self.selfsem
            inck = 1
        if inc is not None:
            inc.n += inck
            self.cmds.append((meth, args, kw, inc.h, inck))
        else:
            self.cmds.append((meth, args, kw, None, 0))

    def wait(self, sem, val):
        if val > 0:
            self.cmds.append(("wait_ge", (sem.h, val), {}, None, 0))

    def wait_m(self, mark):
        if mark is not None:
            self.wait(mark[0], mark[1])

    def run(self, e):
        for idx, (meth, args, kw, inch, inck) in enumerate(self.cmds):
            try:
                r = getattr(e, meth)(*args, **kw)
            except Exception:
                print(f"FAILED {self.name} cmd#{idx} {meth} "
                      f"args={[getattr(a, 'shape', a) for a in args]}")
                raise
            if inch is not None:
                r.then_inc(inch, inck)


class Bank:
    def __init__(self, ps):
        self.ps = ps
        self.wmark = None
        self.rmark = None


def build_nc(debug_dump=(), nt=NT, nft=NFT, do_mix=True, do_scan=True, do_epi=True, epi_mode=4):
    nc = bass.Bass("TRN2", target_bir_lowering=False, debug=False)
    dd = set(debug_dump)

    xT = nc.dram_tensor("xT", [HID, T], F32, kind="ExternalInput").ap()
    aslab = nc.dram_tensor("aslab", [N_ASLAB, 128, SLABW], F32, kind="ExternalInput").ap()
    fslab = nc.dram_tensor("fslab", [N_FSLAB, 128, SLABW], F32, kind="ExternalInput").ap()
    awx2 = nc.dram_tensor("awx2", [32, 5 * 8 * 128], F32, kind="ExternalInput").ap()
    aww2 = nc.dram_tensor("aww2", [64, 4 * 128], F32, kind="ExternalInput").ap()
    avecs = nc.dram_tensor("avecs", [128, AV_N], F32, kind="ExternalInput").ap()
    fscales = nc.dram_tensor("fscales", [128, FS_N], F32, kind="ExternalInput").ap()
    yT = nc.dram_tensor("yT", [HID, T], F32, kind="ExternalOutput").ap()
    wpa = nc.dram_tensor("wpa", [N_ASLAB, 128, SLABW], BF16, kind="Internal").ap()
    wpf = nc.dram_tensor("wpf", [N_FSLAB, 128, SLABW], BF16, kind="Internal").ap()
    x2kind = "ExternalOutput" if "x2s" in dd else "Internal"
    x2s = nc.dram_tensor("x2s", [NT, 128, FC, TILE], F32, kind=x2kind).ap()

    xTr = xT.rearrange("(fc p) t -> p fc t", p=128)
    yTr = yT.rearrange("(fc p) t -> p fc t", p=128)

    TE, VE, SE, GE, SY = EP("tensor"), EP("vector"), EP("scalar"), EP("gpsimd"), EP("sync")
    globals()["_last_eps"] = {"TE": TE, "VE": VE, "SE": SE, "GE": GE, "SY": SY}

    es = contextlib.ExitStack()
    with es:
        E = es.enter_context

        S_T = Sem(E(nc.semaphore("s_t")))
        S_V = Sem(E(nc.semaphore("s_v")))
        S_S = Sem(E(nc.semaphore("s_s")))
        S_P = Sem(E(nc.semaphore("s_p")))
        S_PH = Sem(E(nc.semaphore("s_ph")))
        VE.selfsem = S_V
        SE.selfsem = S_S
        GE.selfsem = S_P

        all_dma_sems = []

        def new_dsem(name):
            sm = Sem(E(nc.semaphore(name)))
            all_dma_sems.append(sm)
            return sm

        def dma(q, sem, out, in_, **kw):
            ep = SY if q == "sy" else GE
            ep.ins("dma_start", out=out, in_=in_, inc=sem, inck=16, **kw)
            return (sem, sem.n)

        ring = E(nc.sbuf_tensor("ring", [128, RING, SLABW], BF16))
        ones_f = E(nc.sbuf_tensor("ones_f", [128, 128], F32))
        ones_b = E(nc.sbuf_tensor("ones_b", [128, 128], BF16))
        id_b = E(nc.sbuf_tensor("id_b", [128, 128], BF16))
        mask_su = E(nc.sbuf_tensor("mask_su", [128, 128], F32))
        ones1_b = E(nc.sbuf_tensor("ones1_b", [128, 1], BF16))
        one1_b = E(nc.sbuf_tensor("one1_b", [1, 1], BF16))
        eps_col = E(nc.sbuf_tensor("eps_col", [128, 1], F32))
        vec_sb = E(nc.sbuf_tensor("vec_sb", [128, AV_N], F32))
        fsc_sb = E(nc.sbuf_tensor("fsc_sb", [128, FS_N], F32))
        dummy_sb = E(nc.sbuf_tensor("dummy_sb", [128, 4], F32))

        banks = [Bank(E(nc.psum_tensor(f"pb{i}", [128, 512], F32))) for i in range(4)]
        ps_m = Bank(E(nc.psum_tensor("ps_m", [128, 256], F32)))
        ps_o = Bank(E(nc.psum_tensor("ps_o", [128, 256], F32)))
        ps_kv = Bank(E(nc.psum_tensor("ps_kv", [128, 256], F32)))
        ps_t7 = Bank(E(nc.psum_tensor("ps_t7", [128, 128], BF16)))

        def tp_view(bank):
            if bank is ps_t7:
                return ps_t7.ps[:]
            return bank.ps[:, 0:64].bitcast(BF16)

        hoist_done = {}

        VE.ins("memset", ones_b[:], 1.0)
        VE.ins("memset", ones1_b[:], 1.0)
        VE.ins("memset", one1_b[:], 1.0)
        VE.ins("memset", mask_su[:], 1.0, inc=S_V)
        M_CONST_VB = (S_V, S_V.n)
        GE.wait_m(M_CONST_VB)
        GE.ins("affine_select", id_b[:], ones_b[:], [[1, 128]], ALU.is_equal, 0.0,
               base=0, channel_multiplier=-1, inc=S_P)
        GE.ins("affine_select", mask_su[:], mask_su[:], [[1, 128]], ALU.is_gt, 0.0,
               base=0, channel_multiplier=-1, inc=S_P)
        M_CONST_GP = (S_P, S_P.n)

        sem_misc = new_dsem("d_misc")
        t_vec = dma("sy", sem_misc, vec_sb[:], avecs[:])
        t_fsc = dma("sy", sem_misc, fsc_sb[:], fscales[:])
        t_vec = t_fsc

        slot_free = [None] * RING
        slab_seq = [0]
        sem_ring = [new_dsem(f"d_ring{j}") for j in range(RING)]

        def load_slab(src_ap):
            i = slab_seq[0]
            slot = i % RING
            SY.wait_m(slot_free[slot])
            tgt = dma("sy", sem_ring[slot], ring[:, slot, :], src_ap)
            slab_seq[0] += 1
            return slot, tgt

        esA = contextlib.ExitStack()
        A = esA.enter_context
        pdb = A(nc.sbuf_tensor("pdb", [128, 2, SLABW], BF16))
        pd_srcs = [(aslab[i], wpa[i]) for i in range(N_ASLAB)]
        pd_srcs += [(fslab[i], wpf[i]) for i in range(N_FSLAB)]
        NPD = len(pd_srcs)
        sem_pdl = [new_dsem(f"d_pdl{j}") for j in range(2)]
        sem_pds = [new_dsem(f"d_pds{j}") for j in range(2)]
        pd_load = [None] * NPD
        pd_store = [None] * NPD
        t_awx2 = t_aww2 = None
        awx2_sb = A(nc.sbuf_tensor("awx2_sb", [32, 5 * 8 * 128], BF16))
        aww2_sb = A(nc.sbuf_tensor("aww2_sb", [64, 4 * 128], BF16))
        sem_aw = new_dsem("d_aw")
        sem_aw2 = new_dsem("d_aw2")
        t_awx2 = dma("gp", sem_aw, awx2_sb[:], awx2[:])
        t_aww2 = dma("gp", sem_aw2, aww2_sb[:], aww2[:])
        t0_slabs = {}
        sem_t0 = [new_dsem(f"d_t0s{j}") for j in range(3)]
        for seg in range(3):
            slot = slab_seq[0] % RING
            tgt = dma("gp", sem_t0[seg], ring[:, slot, :], aslab[seg])
            t0_slabs[seg] = (slot, tgt)
            slab_seq[0] += 1
        for i in range(NPD):
            j = i % 2
            if i >= 2:
                GE.wait_m(pd_store[i - 2])
            pd_load[i] = dma("gp", sem_pdl[j], pdb[:, j, :], pd_srcs[i][0])
            if i >= 1:
                GE.wait_m(pd_load[i - 1])
                pd_store[i - 1] = dma("gp", sem_pds[(i - 1) % 2],
                                      pd_srcs[i - 1][1], pdb[:, (i - 1) % 2, :])
        GE.wait_m(pd_load[NPD - 1])
        pd_store[NPD - 1] = dma("gp", sem_pds[(NPD - 1) % 2],
                                pd_srcs[NPD - 1][1], pdb[:, (NPD - 1) % 2, :])
        def pd_gate(upto):
            marks = {}
            for i in range(upto):
                sm, v = pd_store[i]
                marks[sm] = max(marks.get(sm, 0), v)
            return list(marks.items())
        t_pd_attn = pd_gate(N_ASLAB)
        t_pd_ffn = pd_gate(NPD)

        xbuf = A(nc.sbuf_tensor("xbuf", [128, 2, FC, TILE], F32))
        lnb = A(nc.sbuf_tensor("lnb", [128, 2, FC, TILE + 8], BF16))
        sgxsq = A(nc.sbuf_tensor("sgxsq", [128, FC, TILE], BF16))
        scrA = A(nc.sbuf_tensor("scrA", [128, 2048], F32))
        delta = scrA[:].bitcast(BF16).rearrange("p (a b) -> p a b", a=FC)
        ics4 = scrA[:, 0:512].rearrange("p (a b) -> p a b", a=NH)
        cs4 = scrA[:, 512:1024].rearrange("p (a b) -> p a b", a=NH)
        arg4 = [scrA[:, 1024 + i * 512:1536 + i * 512]
                .rearrange("p (a b) -> p a b", a=NH) for i in range(2)]
        scrB = A(nc.sbuf_tensor("scrB", [128, 2048], F32))
        xi0v = scrB[:].bitcast(BF16).rearrange("p (a b) -> p a b", a=FC)
        xi1 = A(nc.sbuf_tensor("xi1", [128, FC, TILE], BF16))

        def xi_v(pipe):
            return xi0v if pipe == 0 else xi1[:]

        exp4 = [scrB[:, i * 512:(i + 1) * 512]
                .rearrange("p (a b) -> p a b", a=NH) for i in range(2)]
        _scrBb = scrB[:, 1024:2048].bitcast(BF16)
        rg4 = _scrBb[:, 0:512].rearrange("p (a b) -> p a b", a=NH)
        rl4 = _scrBb[:, 512:1024].rearrange("p (a b) -> p a b", a=NH)
        khf4 = _scrBb[:, 1024:1536].rearrange("p (a b) -> p a b", a=NH)
        kf4 = _scrBb[:, 1536:2048].rearrange("p (a b) -> p a b", a=NH)
        tmpm = A(nc.sbuf_tensor("tmpm", [128, FC, TILE], BF16))
        cenb = tmpm[:].rearrange("p a (b c) -> p (a b) c", c=DV)
        xpN = A(nc.sbuf_tensor("xpN", [32, 5, TILE], BF16))
        th_sb = A(nc.sbuf_tensor("th_sb", [64, TILE], BF16))
        r_sb = A(nc.sbuf_tensor("r_sb", [128, NH, TILE], BF16))
        k_sb = A(nc.sbuf_tensor("k_sb", [128, NH, TILE], BF16))
        g_sb = A(nc.sbuf_tensor("g_sb", [128, FC, TILE], BF16))
        w_sb = A(nc.sbuf_tensor("w_sb", [128, NH, TILE], F32))
        o2_sb = A(nc.sbuf_tensor("o2_sb", [128, FC, TILE], BF16))
        mean_t = A(nc.sbuf_tensor("mean_t", [128, TILE], F32))
        var_t = A(nc.sbuf_tensor("var_t", [128, TILE], F32))
        cen_s = A(nc.sbuf_tensor("cen_s", [128, TILE], F32))
        td4 = A(nc.sbuf_tensor("td4", [128, NH, C], BF16))
        kht4 = A(nc.sbuf_tensor("kht4", [128, NH, C], BF16))
        vtt = A(nc.sbuf_tensor("vtt", [128, NCH, NH, DV], BF16))
        mm4 = A(nc.sbuf_tensor("mm4", [128, NH, C], BF16))
        drow4 = A(nc.sbuf_tensor("drow4", [1, NH * C], BF16))
        od1 = A(nc.sbuf_tensor("od1", [128, 2 * C], F32))
        osb4 = A(nc.sbuf_tensor("osb4", [128, NH, 2 * C], F32))
        dcol4 = A(nc.sbuf_tensor("dcol4", [128, NH], F32))
        sum4 = A(nc.sbuf_tensor("sum4", [128, NH], F32))
        mean4 = A(nc.sbuf_tensor("mean4", [128, NH], F32))
        sumv16 = A(nc.sbuf_tensor("sumv16", [128, NCH * NH], F32))
        S_sb = A(nc.sbuf_tensor("S_sb", [128, NH, DV], F32))
        Sb_sb = A(nc.sbuf_tensor("Sb_sb", [128, NH, DV], BF16))
        varg = A(nc.sbuf_tensor("varg", [128, NCH * NH], F32))
        stdg = A(nc.sbuf_tensor("stdg", [128, NCH * NH], F32))
        onb4 = A(nc.sbuf_tensor("onb4", [128, NH, 2 * C], BF16))
        ofb4 = A(nc.sbuf_tensor("ofb4", [128, FC, C], BF16))

        VE.ins("memset", ones_f[:], 1.0)
        VE.ins("memset", eps_col[:], EPS)
        VE.ins("memset", S_sb[:], 0.0)
        VE.ins("memset", Sb_sb[:], 0.0)
        VE.ins("memset", lnb[:, 0, :, 7:8], 0.0, inc=S_V)
        M_CONST_V = (S_V, S_V.n)

        x_load_t = [None] * (NT + 1)
        stats_pre = [None] * (NT + 1)
        std_ready = [None] * (NT + 1)
        ln_ready = [None] * (NT + 1)
        xsq_read = [None] * (NT + 1)
        o2_read = [None] * NT
        sg_write_ok = [None] * NT
        w_last_read = [None] * NT
        v_last_read = [None] * NT
        x2_mark = [None] * NT
        x2_store_t = [None] * NT
        projdone = {}
        proj_mm_done = {}
        th_read = [None] * NT
        xp_read = [None] * NT
        o2v_prev = [None]
        m_scan_vread = [None]
        preloaded = [None]
        m_scan_sread = [None]
        m_ogrp = [None]
        m_tp_done = [None]
        m_kv_all = [None]
        m_od_all = [None]
        m_sbf_all = [M_CONST_V]
        m_supd_all = [None]
        m_scrB_free = [None]
        m_dmm = [None]
        m_sc_read = [None]
        m_ep_tp = [None]
        m_of_read = [None]
        arg_sread = [None, None]
        exp_vread = [None, None]

        def emit_ln_stats(t, is_first=False):
            xb = xbuf[:, t % 2]
            SE.wait_m(x_load_t[t])
            if t >= 2:
                SE.wait_m(o2_read[t - 2])
                SE.wait_m(x2_mark[t - 2])
            for fc in range(FC):
                SE.ins("activation", sgxsq[:, fc, :], xb[:, fc, :], AF.Square,
                       inc=(S_S if fc == FC - 1 else None))
            sq_mark = S_S.n
            b_sum, b_sumsq = banks[0], banks[1]
            TE.wait_m(b_sum.rmark)
            TE.wait_m(x_load_t[t])
            if is_first:
                TE.wait_m(M_CONST_V)
                TE.wait_m(M_CONST_GP)
            for fc in range(FC):
                TE.ins("matmul", b_sum.ps[:], ones_f[:], xb[:, fc, :],
                       start=(fc == 0), stop=(fc == FC - 1),
                       inc=(S_T if fc == FC - 1 else None))
            b_sum.wmark = (S_T, S_T.n)
            TE.wait_m(b_sumsq.rmark)
            TE.wait(S_S, sq_mark)
            for fc in range(FC):
                TE.ins("matmul", b_sumsq.ps[:], ones_b[:], sgxsq[:, fc, :],
                       start=(fc == 0), stop=(fc == FC - 1),
                       inc=(S_T if fc == FC - 1 else None))
            b_sumsq.wmark = (S_T, S_T.n)
            xsq_read[t] = (S_T, S_T.n)
            VE.wait_m(b_sum.wmark)
            VE.ins("tensor_scalar", mean_t[:], b_sum.ps[:], 1.0 / HID, None,
                   ALU.mult, inc=S_V)
            b_sum.rmark = (S_V, S_V.n)
            VE.wait_m(b_sumsq.wmark)
            VE.ins("tensor_scalar", var_t[:], b_sumsq.ps[:], 1.0 / HID, None,
                   ALU.mult, inc=S_V)
            b_sumsq.rmark = (S_V, S_V.n)
            VE.wait_m(ln_ready[t - 1] if t >= 1 else None)
            VE.ins("tensor_tensor", cen_s[:], mean_t[:], mean_t[:], ALU.mult)
            VE.ins("tensor_tensor", var_t[:], var_t[:], cen_s[:], ALU.subtract,
                   inc=S_V)
            stats_pre[t] = (S_V, S_V.n)

        def emit_ln_sqrt(t):
            SE.wait_m(stats_pre[t])
            SE.ins("activation", var_t[:], var_t[:], AF.Sqrt, bias=eps_col[:],
                   inc=S_S)
            VE.wait(S_S, S_S.n)
            VE.ins("reciprocal", var_t[:], var_t[:], inc=S_V)
            std_ready[t] = (S_V, S_V.n)

        def emit_ln_normalize(t):
            xb = xbuf[:, t % 2]
            lb = lnb[:, t % 2]
            XE, XS = aux_of(t)
            XE.wait_m(std_ready[t])
            XE.wait_m(m_scan_sread[0])
            XE.wait_m(m_scan_vread[0])
            if t >= 2:
                XE.wait_m(projdone.get((t - 2, "lnb_free")))
            for fc in range(FC):
                XE.ins("tensor_tensor", cen_s[:], xb[:, fc, :], mean_t[:],
                       ALU.subtract)
                XE.ins("tensor_tensor", lb[:, fc, 8:TILE + 8], cen_s[:], var_t[:],
                       ALU.mult)
            if t > 0:
                pb = lnb[:, (t - 1) % 2]
                for fc in range(FC):
                    XE.ins("tensor_copy", lb[:, fc, 7:8], pb[:, fc, TILE + 7:TILE + 8])
            for fc in range(FC):
                XE.ins("tensor_tensor", delta[:, fc, :], lb[:, fc, 7:TILE + 7],
                       lb[:, fc, 8:TILE + 8], ALU.subtract,
                       inc=(XS if fc == FC - 1 else None))
            ln_ready[t] = (XS, XS.n)

        sem_x = [new_dsem("d_x0"), new_dsem("d_x1")]
        sem_x2w = [new_dsem("d_x2w0"), new_dsem("d_x2w1")]
        x_load_t[0] = dma("sy", sem_x[0], xbuf[:, 0], xTr[:, :, 0:TILE])
        emit_ln_stats(0, is_first=True)
        emit_ln_sqrt(0)
        for sm, v in t_pd_attn:
            SY.wait(sm, v)

        def seg0_xp_lhsT(slot, n, k):
            return ring[:, slot, (n * 8 + k) * 32:(n * 8 + k + 1) * 32]

        def seg0_ww1_lhsT(slot, k):
            return ring[:, slot, 1280 + k * 64: 1280 + (k + 1) * 64]

        def seg_128(slot, idx):
            return ring[:, slot, idx * 128:(idx + 1) * 128]

        proj_plan = [("w", 1), ("r", 0), ("k", 2), ("v", 3), ("g", 4)]
        proj_out = {"r": (4, r_sb, "V"), "k": (4, k_sb, "V"),
                    "g": (8, g_sb, "S")}
        proj_segs = {"r": (1, 2), "k": (3, 2), "v": (5, 4), "g": (9, 4)}

        def xp_slice(n):
            return xpN[:, n, :]

        def aux_of(t):
            return (GE, S_P) if t >= 3 else (VE, S_V)

        def emit_mix_front(t, slot0, tgt0):
            emit_ln_normalize(t)
            lb = lnb[:, t % 2]
            VE.wait_m(ln_ready[t])
            if t == 0:
                VE.wait_m(t_vec)
            VE.wait_m(hoist_done.get((t - 1, "xp_mm")))
            VE.wait_m(m_scrB_free[0])
            for fc in range(FC):
                VE.ins("scalar_tensor_tensor", xi_v(0)[:, fc, :],
                       delta[:, fc, :],
                       vec_sb[:, AV_MU0 + fc:AV_MU0 + fc + 1],
                       lb[:, fc, 8:TILE + 8], ALU.mult, ALU.add,
                       inc=(S_V if fc == FC - 1 else None))
            xi0_mark = (S_V, S_V.n)

            TE.wait_m(tgt0)
            TE.wait_m(xi0_mark)
            if t >= 1:
                SE.wait_m(xp_read[t - 1])
            for n in range(5):
                b = banks[2] if n % 2 == 0 else banks[3]
                TE.wait_m(b.rmark)
                for k in range(FC):
                    TE.ins("matmul", b.ps[0:32, :], seg0_xp_lhsT(slot0, n, k),
                           xi_v(0)[:, k, :], start=(k == 0), stop=(k == FC - 1),
                           inc=(S_T if k == FC - 1 else None))
                b.wmark = (S_T, S_T.n)
                if n == 4:
                    hoist_done[(t, "xp_mm")] = (S_T, S_T.n)
                SE.wait_m(b.wmark)
                SE.ins("activation", xpN[:, n, :], b.ps[0:32, :], AF.Tanh,
                       inc=S_S)
                b.rmark = (S_S, S_S.n)
            hoist_done[(t, "xp")] = (S_S, S_S.n)

            if t == 0:
                TE.wait_m(t_awx2)
                TE.wait_m(t_aww2)
            TE.wait(S_S, S_S.n)
            mu_idx = 1
            for fc in range(FC):
                b = banks[fc % 2]
                TE.wait_m(b.rmark)
                TE.ins("matmul", b.ps[:],
                       awx2_sb[:, (mu_idx * 8 + fc) * 128:(mu_idx * 8 + fc + 1) * 128],
                       xp_slice(mu_idx), start=True, stop=True, inc=S_T)
                b.wmark = (S_T, S_T.n)
                VE.wait_m(b.wmark)
                if fc == 0:
                    VE.wait_m(proj_mm_done.get((t - 1, 3)))
                VE.ins("scalar_tensor_tensor", xi1[:, fc, :], b.ps[:],
                       vec_sb[:, AV_XB + mu_idx * 8 + fc:AV_XB + mu_idx * 8 + fc + 1],
                       delta[:, fc, :], ALU.add, ALU.mult, inc=S_V)
                b.rmark = (S_V, S_V.n)
            VE.ins("tensor_tensor", xi1[:], xi1[:], lb[:, :, 8:TILE + 8],
                   ALU.add, inc=S_V)
            xiw_mark = S_V.n

            TE.wait(S_V, xiw_mark)
            b = banks[2]
            TE.wait_m(b.rmark)
            for k in range(FC):
                TE.ins("matmul", b.ps[0:64, :], seg0_ww1_lhsT(slot0, k),
                       xi1[:, k, :], start=(k == 0), stop=(k == FC - 1),
                       inc=(S_T if k == FC - 1 else None))
            b.wmark = (S_T, S_T.n)
            proj_mm_done[(t, 0)] = (S_T, S_T.n)
            slot_free[slot0] = (S_T, S_T.n)
            SE.wait_m(b.wmark)
            if t >= 1:
                SE.wait_m(th_read[t - 1])
            SE.ins("activation", th_sb[:], b.ps[0:64, :], AF.Tanh, inc=S_S)
            b.rmark = (S_S, S_S.n)
            th_mark = S_S.n
            TE.wait(S_S, th_mark)
            for oc in range(4):
                b = banks[3] if oc % 2 else banks[2]
                TE.wait_m(b.rmark)
                TE.ins("matmul", b.ps[:],
                       aww2_sb[:, oc * 128:(oc + 1) * 128], th_sb[:],
                       start=True, stop=True, inc=S_T)
                b.wmark = (S_T, S_T.n)
                if oc == 3:
                    th_read[t] = (S_T, S_T.n)
                SE.wait_m(b.wmark)
                if oc == 0 and t >= 1:
                    SE.wait_m(w_last_read[t - 1])
                SE.ins("activation", w_sb[:, oc, :], b.ps[:], AF.Exp,
                       bias=vec_sb[:, AV_BW2 + oc:AV_BW2 + oc + 1], inc=S_S)
                b.rmark = (S_S, S_S.n)
            projdone[(t, "w")] = (S_S, S_S.n)

        for t in range(nt):
            tp = t % 2
            if t + 1 < nt:
                if t >= 1 and x2_store_t[t - 1] is not None:
                    SY.wait_m(x2_store_t[t - 1])
                x_load_t[t + 1] = dma("sy", sem_x[(t + 1) % 2],
                                      xbuf[:, (t + 1) % 2],
                                      xTr[:, :, (t + 1) * TILE:(t + 2) * TILE])

            if preloaded[0] is not None:
                tile_slabs = preloaded[0]
                preloaded[0] = None
            elif t == 0:
                tile_slabs = dict(t0_slabs)
            else:
                tile_slabs = {}

            def load_segs(a, b):
                for seg in range(a, b):
                    tile_slabs[seg] = load_slab(wpa[seg])

            if 0 not in tile_slabs:
                load_segs(0, 1)
            if 1 not in tile_slabs:
                load_segs(1, 3)

            if t == 0 and do_mix:
                emit_mix_front(0, tile_slabs[0][0], tile_slabs[0][1])
            lb = lnb[:, tp]

            xi_mark = {}
            tmp_free = [None]

            def emit_mus(ni):
                name, mu_idx = proj_plan[ni]
                pipe = (ni + 1) % 2
                TE.wait_m(hoist_done[(t, "xp")])
                xi_free_wait = {1: hoist_done.get((t, "xp_mm")),
                                2: proj_mm_done.get((t, 0)),
                                3: proj_mm_done.get((t, 1)),
                                4: proj_mm_done.get((t, 2))}[ni]
                first_v_wait = tmp_free[0]
                if ni == 1 and t >= 1:
                    first_v_wait2 = o2v_prev[0]
                else:
                    first_v_wait2 = None
                for fc in range(FC):
                    b = banks[fc % 2]
                    TE.wait_m(b.rmark)
                    TE.ins("matmul", b.ps[:],
                           awx2_sb[:, (mu_idx * 8 + fc) * 128:(mu_idx * 8 + fc + 1) * 128],
                           xp_slice(mu_idx), start=True, stop=True, inc=S_T)
                    b.wmark = (S_T, S_T.n)
                    VE.wait_m(b.wmark)
                    if fc == 0 and first_v_wait is not None:
                        VE.wait_m(first_v_wait)
                    if fc == 0 and first_v_wait2 is not None:
                        VE.wait_m(first_v_wait2)
                    VE.ins("scalar_tensor_tensor", tmpm[:, fc, :], b.ps[:],
                           vec_sb[:, AV_XB + mu_idx * 8 + fc:AV_XB + mu_idx * 8 + fc + 1],
                           delta[:, fc, :], ALU.add, ALU.mult, inc=S_V)
                    b.rmark = (S_V, S_V.n)
                    if fc == FC - 1 and ni == 4:
                        xp_read[t] = (S_T, S_T.n)
                XE, XS = aux_of(t)
                XE.wait(S_V, S_V.n)
                XE.wait_m(xi_free_wait)
                XE.ins("tensor_tensor", xi_v(pipe)[:],
                       tmpm[:].rearrange("p a b -> p a b"),
                       lb[:, :, 8:TILE + 8], ALU.add, inc=XS)
                xi_mark[ni] = (XS, XS.n)
                tmp_free[0] = (XS, XS.n)

            def emit_proj(ni):
                name, mu_idx = proj_plan[ni]
                pipe = (ni + 1) % 2
                TE.wait_m(xi_mark[ni])
                if name == "w":
                    b = banks[2]
                    TE.wait_m(b.rmark)
                    for k in range(FC):
                        TE.ins("matmul", b.ps[0:64, :], seg0_ww1_lhsT(slot0, k),
                               xi_v(pipe)[:, k, :], start=(k == 0),
                               stop=(k == FC - 1),
                               inc=(S_T if k == FC - 1 else None))
                    b.wmark = (S_T, S_T.n)
                    proj_mm_done[(t, ni)] = (S_T, S_T.n)
                    SE.wait_m(b.wmark)
                    if t >= 1:
                        SE.wait_m(th_read[t - 1])
                    SE.ins("activation", th_sb[:], b.ps[0:64, :], AF.Tanh,
                           inc=S_S)
                    b.rmark = (S_S, S_S.n)
                    th_mark = S_S.n
                    if t == 0:
                        TE.wait_m(t_aww2)
                    TE.wait(S_S, th_mark)
                    for oc in range(4):
                        b = banks[3] if oc % 2 else banks[2]
                        TE.wait_m(b.rmark)
                        TE.ins("matmul", b.ps[:],
                               aww2_sb[:, oc * 128:(oc + 1) * 128], th_sb[:],
                               start=True, stop=True, inc=S_T)
                        b.wmark = (S_T, S_T.n)
                        if oc == 3:
                            th_read[t] = (S_T, S_T.n)
                        SE.wait_m(b.wmark)
                        if oc == 0 and t >= 1:
                            SE.wait_m(w_last_read[t - 1])
                        SE.ins("activation", w_sb[:, oc, :], b.ps[:],
                               AF.Exp,
                               bias=vec_sb[:, AV_BW2 + oc:AV_BW2 + oc + 1],
                               inc=S_S)
                        b.rmark = (S_S, S_S.n)
                    projdone[(t, "w")] = (S_S, S_S.n)
                    slot_free[tile_slabs[0][0]] = (S_T, th_read[t][1])
                elif name == "v":
                    for tc in range(NCH):
                        for half in range(2):
                            b = banks[3] if (2 * tc + half) % 2 else banks[2]
                            TE.wait_m(b.rmark)
                            for k in range(FC):
                                seg = 5 + half * 2 + k // 4
                                slot, tgt = tile_slabs[seg]
                                TE.wait_m(tgt)
                                TE.ins("matmul", b.ps[:],
                                       xi_v(pipe)[:, k, tc * C:(tc + 1) * C],
                                       ring[:, slot, (k % 4) * 512:(k % 4 + 1) * 512],
                                       start=(k == 0), stop=(k == FC - 1),
                                       inc=(S_T if k == FC - 1 else None))
                            b.wmark = (S_T, S_T.n)
                            if tc == NCH - 1 and half == 1:
                                proj_mm_done[(t, ni)] = (S_T, S_T.n)
                                for seg in range(5, 9):
                                    slot_free[tile_slabs[seg][0]] = (S_T, S_T.n)
                            SE.wait_m(b.wmark)
                            if tc == 0 and half == 0 and t >= 1:
                                SE.wait_m(v_last_read[t - 1])
                            for hh in range(2):
                                SE.ins("activation",
                                       vtt[:, tc, 2 * half + hh, :],
                                       b.ps[:, hh * DV:(hh + 1) * DV], AF.Copy,
                                       inc=S_S)
                            b.rmark = (S_S, S_S.n)
                    projdone[(t, "v")] = (S_S, S_S.n)
                else:
                    n_oc, out_sb, evac = proj_out[name]
                    base_seg, nseg = proj_segs[name]
                    oc_per_seg = n_oc // nseg
                    for oc in range(n_oc):
                        seg = base_seg + oc // oc_per_seg
                        slot, tgt = tile_slabs[seg]
                        oin = oc % oc_per_seg
                        TE.wait_m(tgt)
                        b = banks[3] if oc % 2 else banks[2]
                        TE.wait_m(b.rmark)
                        for k in range(FC):
                            TE.ins("matmul", b.ps[:], seg_128(slot, oin * FC + k),
                                   xi_v(pipe)[:, k, :], start=(k == 0),
                                   stop=(k == FC - 1),
                                   inc=(S_T if k == FC - 1 else None))
                        b.wmark = (S_T, S_T.n)
                        if oin == oc_per_seg - 1:
                            slot_free[slot] = (S_T, S_T.n)
                        if oc == n_oc - 1:
                            proj_mm_done[(t, ni)] = (S_T, S_T.n)
                        ee, se = (VE, S_V) if evac == "V" else (SE, S_S)
                        ee.wait_m(b.wmark)
                        if oc == 0 and evac == "S" and name == "v" and t >= 1:
                            ee.wait_m(v_last_read[t - 1])
                        if ee is VE:
                            ee.ins("tensor_copy", out_sb[:, oc, :], b.ps[:],
                                   inc=se)
                        else:
                            ee.ins("activation", out_sb[:, oc, :], b.ps[:],
                                   AF.Copy, inc=se)
                        b.rmark = (se, se.n)
                    projdone[(t, name)] = (se, se.n)

            if do_mix:
                emit_mus(1)
                emit_proj(1)
                load_segs(3, 5)
                emit_mus(2)
                emit_proj(2)
                load_segs(5, 9)
                emit_mus(3)
                emit_proj(3)
                load_segs(9, 13)
                emit_mus(4)
                emit_proj(4)
                load_segs(13, 17)
                projdone[(t, "lnb_free")] = xi_mark[4]
            else:
                projdone[(t, "lnb_free")] = (S_V, ln_ready[t][1])

            do_scan_t = do_scan and do_mix
            u_b = vec_sb[:, AV_U:AV_U + NH].unsqueeze(2)
            sc_banks = [ps_m, banks[2], banks[3], ps_kv]

            for c in range(NCH if do_scan_t else 0):
                cl = slice(c * C, (c + 1) * C)
                w4 = w_sb[:, :, cl]
                r4 = r_sb[:, :, cl]
                k4 = k_sb[:, :, cl]
                if c == 0:
                    VE.wait_m(projdone[(t, "w")])
                    VE.wait_m(projdone[(t, "r")])
                    VE.wait_m(projdone[(t, "k")])
                    VE.wait_m(projdone[(t, "v")])
                    VE.wait_m(proj_mm_done[(t, 3)])
                    VE.wait_m(M_CONST_V)
                    VE.wait_m(M_CONST_GP)
                VE.wait_m(m_scan_sread[0])
                for h in range(NH):
                    VE.ins("tensor_tensor_scan", ics4[:, h, :], ones_f[:],
                           w_sb[:, h, cl], 0.0, ALU.mult, ALU.subtract)
                VE.ins("tensor_tensor", cs4[:], ics4[:], w4, ALU.add, inc=S_V)
                if c == NCH - 1:
                    w_last_read[t] = (S_V, S_V.n)
                VE.wait_m(arg_sread[0])
                VE.ins("tensor_tensor", arg4[0][:],
                       ics4[:, :, C - 1:C].broadcast_to((128, NH, C)), ics4[:],
                       ALU.subtract, inc=S_V)
                v1_mark = (S_V, S_V.n)
                SE.wait_m(v1_mark)
                if c == 0:
                    SE.wait_m(proj_mm_done[(t, 3)])
                SE.wait_m(exp_vread[0])
                SE.ins("activation", exp4[0][:], cs4[:], AF.Exp)
                SE.wait_m(exp_vread[1])
                SE.ins("activation", exp4[1][:], arg4[0][:], AF.Exp)
                arg_sread[0] = (S_S, S_S.n)
                SE.wait_m(m_supd_all[0])
                SE.ins("activation", dcol4[:].unsqueeze(2),
                       ics4[:, :, C - 1:C], AF.Exp, inc=S_S)
                s1_mark = (S_S, S_S.n)
                m_scan_sread[0] = s1_mark
                VE.wait_m(s1_mark)
                VE.wait_m(m_ogrp[0])
                VE.ins("tensor_tensor", rg4[:], r4, exp4[0][:], ALU.mult)
                exp_vread[0] = None
                VE.wait_m(m_tp_done[0])
                VE.wait_m(m_sc_read[0])
                VE.ins("tensor_tensor", khf4[:], k4, exp4[1][:], ALU.mult)
                exp_vread[1] = None
                VE.wait_m(m_dmm[0])
                VE.ins("tensor_tensor", td4[:], k4, u_b.broadcast_to((128, NH, C)),
                       ALU.mult)
                VE.ins("tensor_tensor", td4[:], td4[:], r4, ALU.mult, inc=S_V)
                v2_mark = (S_V, S_V.n)
                exp_vread[0] = (S_V, S_V.n)
                exp_vread[1] = (S_V, S_V.n)
                TE.wait_m(v2_mark)
                tp_banks = [ps_t7, banks[3]]
                for h in range(NH):
                    b = tp_banks[h % len(tp_banks)]
                    TE.wait_m(b.rmark)
                    TE.ins("transpose", tp_view(b), khf4[:, h, :], id_b[:],
                           inc=S_T)
                    b.wmark = (S_T, S_T.n)
                    SE.wait_m(b.wmark)
                    if h == 0:
                        SE.wait_m(m_kv_all[0])
                    SE.ins("activation", kht4[:, h, :], tp_view(b), AF.Copy,
                           inc=S_S)
                    b.rmark = (S_S, S_S.n)
                vt_mark = projdone[(t, "v")]
                m_tp_done[0] = (S_T, S_T.n)
                bc = ics4[:, :, SC - 1::SC]
                ics4d = ics4[:].rearrange("p h (b j) -> p h b j", j=SC)
                cs4d3 = cs4[:, :, SC:C].rearrange("p h (b j) -> p h b j", j=SC)
                kEv = arg4[1][:, :, 0:64].bitcast(BF16)
                pa = arg4[1][:, :, 64:74]
                pcx = arg4[1][:, :, 74:84]
                VE.wait_m(arg_sread[0])
                VE.ins("tensor_tensor",
                       arg4[0][:].rearrange("p h (b j) -> p h b j", j=SC),
                       bc.unsqueeze(3).broadcast_to((128, NH, NSUB, SC)),
                       ics4d, ALU.subtract)
                pidx = {}
                off = 0
                for si in range(NSUB):
                    rng = slice(off, off + si + 1)
                    for jb in range(si + 1):
                        pidx[(si, jb)] = off + jb
                    if si == 0:
                        VE.ins("tensor_scalar", pa[:, :, 0:1], bc[:, :, 0:1],
                               -1.0, None, ALU.mult)
                    else:
                        VE.ins("tensor_tensor", pa[:, :, rng],
                               bc[:, :, si - 1:si].broadcast_to((128, NH, si + 1)),
                               bc[:, :, 0:si + 1], ALU.subtract)
                    off += si + 1
                VE.ins("tensor_tensor",
                       cs4[:, :, SC:C].rearrange("p h (b j) -> p h b j", j=SC),
                       cs4d3,
                       bc[:, :, 0:3].unsqueeze(3).broadcast_to((128, NH, 3, SC)),
                       ALU.subtract, inc=S_V)
                varg_mark = (S_V, S_V.n)
                SE.wait_m(varg_mark)
                SE.wait_m(exp_vread[0])
                SE.ins("activation", exp4[0][:], arg4[0][:], AF.Exp)
                SE.ins("activation", pcx[:], pa[:], AF.Exp)
                SE.wait_m(exp_vread[1])
                SE.ins("activation", exp4[1][:, :, 0:96], cs4[:, :, SC:C],
                       AF.Exp, inc=S_S)
                arg_sread[0] = (S_S, S_S.n)
                exps2_mark = (S_S, S_S.n)
                m_scan_sread[0] = exps2_mark
                VE.wait_m(exps2_mark)
                VE.wait_m(m_sc_read[0])
                VE.ins("tensor_tensor", kEv[:], k4, exp4[0][:], ALU.mult)
                exp_vread[0] = None
                VE.ins("tensor_copy", rl4[:, :, 0:SC], rg4[:, :, 0:SC])
                VE.ins("tensor_tensor", rl4[:, :, SC:C], r4[:, :, SC:C],
                       exp4[1][:, :, 0:96], ALU.mult, inc=S_V)
                exp_vread[0] = (S_V, S_V.n)
                exp_vread[1] = (S_V, S_V.n)
                kE_mark = (S_V, S_V.n)
                m_scan_vread[0] = (S_V, S_V.n)
                sc_mm_mark = None
                kf_free = [m_sc_read[0], m_tp_done[0]]
                kf_bufs = [kf4, khf4]
                for si in range(NSUB):
                    kb = kf_bufs[si % 2]
                    hi = (si + 1) * SC
                    VE.wait_m(kf_free[si % 2])
                    for jb in range(si + 1):
                        VE.ins("tensor_tensor", kb[:, :, jb * SC:(jb + 1) * SC],
                               kEv[:, :, jb * SC:(jb + 1) * SC],
                               pcx[:, :, pidx[(si, jb)]:pidx[(si, jb)] + 1]
                               .broadcast_to((128, NH, SC)), ALU.mult)
                    if hi < C:
                        VE.ins("memset", kb[:, :, hi:C], 0.0, inc=S_V)
                    else:
                        VE.ins("tensor_copy", od1[0:1, 0:1], ones_f[0:1, 0:1],
                               inc=S_V)
                    mul_mark = (S_V, S_V.n)
                    TE.wait_m(mul_mark)
                    for h in range(NH):
                        b = sc_banks[h]
                        if si == 0:
                            TE.wait_m(b.rmark)
                        TE.ins("matmul", b.ps[:, 0:128][:, si * SC:hi],
                               kb[:, h, :], rl4[:, h, si * SC:hi],
                               start=True, stop=True, inc=S_T)
                        if si == NSUB - 1:
                            b.wmark = (S_T, S_T.n)
                    sc_mm_mark = (S_T, S_T.n)
                    kf_free[si % 2] = (S_T, S_T.n)
                m_sc_read[0] = (S_T, S_T.n)
                if c == NCH - 1:
                    m_scrB_free[0] = (S_T, S_T.n)
                for h in range(NH):
                    b = sc_banks[h]
                    VE.wait_m(b.wmark)
                    if h == 0:
                        VE.wait_m(m_ogrp[0])
                    VE.ins("tensor_tensor", mm4[:, h, :], b.ps[:, 0:128],
                           mask_su[:], ALU.mult, inc=S_V)
                    b.rmark = (S_V, S_V.n)
                mask_mark = (S_V, S_V.n)
                td_flat = td4[:].rearrange("p a b -> p (a b)")
                b2 = banks[2]
                TE.wait_m(v2_mark)
                TE.wait_m(b2.rmark)
                TE.ins("matmul", b2.ps[0:1, :], ones1_b[:], td_flat,
                       start=True, stop=True, inc=S_T)
                b2.wmark = (S_T, S_T.n)
                m_dmm[0] = (S_T, S_T.n)
                SE.wait_m(b2.wmark)
                SE.ins("activation", drow4[:], b2.ps[0:1, :], AF.Copy, inc=S_S)
                b2.rmark = (S_S, S_S.n)
                TE.wait_m(b2.rmark)
                for h in range(NH):
                    TE.ins("matmul", b2.ps[:, h:h + 1],
                           drow4[0:1, h * C:(h + 1) * C], one1_b[:],
                           start=True, stop=True,
                           inc=(S_T if h == NH - 1 else None))
                b2.wmark = (S_T, S_T.n)
                dcolp_mark = b2.wmark
                for h in range(NH):
                    ob = ps_o if h % 2 == 0 else ps_m
                    TE.wait_m(ob.rmark)
                    TE.wait_m(m_sbf_all[0])
                    TE.ins("matmul", ob.ps[:, 0:256], rg4[:, h, :],
                           Sb_sb[:, h, :], start=True, stop=False)
                    TE.wait_m(mask_mark)
                    TE.wait_m(vt_mark)
                    TE.ins("matmul", ob.ps[:, 0:256], mm4[:, h, :],
                           vtt[:, c, h, :], start=False, stop=True, inc=S_T)
                    ob.wmark = (S_T, S_T.n)
                    VE.wait_m(dcolp_mark)
                    VE.ins("tensor_tensor", od1[:], vtt[:, c, h, :],
                           b2.ps[:, h:h + 1].broadcast_to((128, 2 * C)),
                           ALU.mult)
                    VE.wait_m(ob.wmark)
                    VE.ins("tensor_tensor", osb4[:, h, :], ob.ps[:, 0:256],
                           od1[:], ALU.add, inc=S_V)
                    ob.rmark = (S_V, S_V.n)
                    if h == NH - 1:
                        b2.rmark = (S_V, S_V.n)
                        m_od_all[0] = (S_V, S_V.n)
                m_ogrp[0] = m_od_all[0]
                for h in range(NH):
                    kb = ps_kv if h % 2 == 0 else banks[3]
                    TE.wait_m(kb.rmark)
                    TE.ins("matmul", kb.ps[:, 0:256], kht4[:, h, :],
                           vtt[:, c, h, :], start=True, stop=True, inc=S_T)
                    if c == NCH - 1 and h == NH - 1:
                        v_last_read[t] = (S_T, S_T.n)
                    kb.wmark = (S_T, S_T.n)
                    VE.wait_m(kb.wmark)
                    VE.ins("scalar_tensor_tensor", S_sb[:, h, :], S_sb[:, h, :],
                           dcol4[:, h:h + 1], kb.ps[:, 0:256], ALU.mult,
                           ALU.add, inc=S_V)
                    kb.rmark = (S_V, S_V.n)
                m_supd_all[0] = (S_V, S_V.n)
                m_kv_all[0] = (S_T, S_T.n)
                SE.wait_m(m_supd_all[0])
                SE.ins("activation", Sb_sb[:], S_sb[:], AF.Copy, inc=S_S)
                m_sbf_all[0] = (S_S, S_S.n)
                XE, XS = aux_of(t)
                VE.ins("tensor_reduce", sum4[:].unsqueeze(2), osb4[:], AX.X,
                       ALU.add)
                VE.ins("tensor_scalar", mean4[:], sum4[:], 1.0 / DV, None,
                       ALU.mult, inc=S_V)
                XE.wait(S_V, S_V.n)
                XE.ins("tensor_tensor", cenb[:, c * NH:(c + 1) * NH, :],
                       osb4[:], mean4[:].unsqueeze(2).broadcast_to((128, NH, DV)),
                       ALU.subtract, inc=XS)
                cen_mark = (XS, XS.n)
                for h in range(NH):
                    SE.wait_m(cen_mark)
                    SE.ins("activation", od1[:].bitcast(BF16)[:, 0:DV],
                           cenb[:, c * NH + h, :], AF.Square,
                           accum_out=sumv16[:, c * NH + h:c * NH + h + 1],
                           inc=(S_S if h == NH - 1 else None))
            scan_done = (S_S, S_S.n)

            if t + 1 < nt:
                emit_ln_stats(t + 1)
                sg_write_ok[t] = xsq_read[t + 1]
            else:
                sg_write_ok[t] = None

            if do_scan_t:
                VE.wait_m(scan_done)
                VE.ins("tensor_scalar", varg[:], sumv16[:], 1.0 / DV, EPS,
                       ALU.mult, ALU.add, inc=S_V)
                SE.wait(S_V, S_V.n)
                SE.ins("activation", stdg[:], varg[:], AF.Sqrt, inc=S_S)
                VE.wait(S_S, S_S.n)
                VE.ins("reciprocal", stdg[:], stdg[:], inc=S_V)
            rstd_mark = (S_V, S_V.n)
            if t + 1 < nt:
                emit_ln_sqrt(t + 1)

            do_epi_t = do_epi and do_scan_t
            if do_epi_t:
                SE.wait_m(projdone[(t, "g")])
                SE.wait_m(sg_write_ok[t])
                SE.wait_m(x2_mark[t - 2] if t >= 2 else None)
                for fc in range(FC):
                    SE.ins("activation", sgxsq[:, fc, :], g_sb[:, fc, :],
                           AF.Silu, inc=(S_S if fc == FC - 1 else None))
            sg_mark = (S_S, S_S.n)

            tp_banks = [ps_t7, ps_o]
            for c in range(NCH if (do_epi_t and epi_mode >= 2) else 0):
                cl = slice(c * C, (c + 1) * C)
                c4 = slice(c * NH, (c + 1) * NH)
                XE, XS = aux_of(t)
                XE.wait_m(rstd_mark)
                XE.wait_m(m_ep_tp[0])
                if t >= 1 and c == 0:
                    XE.wait_m(o2_read[t - 1])
                XE.ins("tensor_tensor", onb4[:], cenb[:, c4, :],
                       stdg[:, c4].unsqueeze(2).broadcast_to((128, NH, DV)),
                       ALU.mult, inc=XS)
                on_mark = (XS, XS.n)
                for h in range(NH):
                    for vc in range(2):
                        gi = 2 * h + vc
                        b = tp_banks[gi % 2]
                        TE.wait_m(b.rmark)
                        TE.wait_m(on_mark)
                        TE.ins("transpose", tp_view(b),
                               onb4[:, h, vc * C:(vc + 1) * C], id_b[:],
                               inc=S_T)
                        b.wmark = (S_T, S_T.n)
                        SE.wait_m(b.wmark)
                        SE.wait_m(sg_mark)
                        if gi == 0:
                            SE.wait_m(m_of_read[0])
                        SE.ins("activation", ofb4[:, gi, :], tp_view(b),
                               AF.Identity,
                               bias=vec_sb[:, AV_GNB + gi:AV_GNB + gi + 1],
                               scale=vec_sb[:, AV_GNW + gi:AV_GNW + gi + 1],
                               inc=S_S)
                        b.rmark = (S_S, S_S.n)
                m_ep_tp[0] = (S_T, S_T.n)
                XE.wait(S_S, S_S.n)
                XE.ins("tensor_tensor", o2_sb[:, :, cl], ofb4[:],
                       sgxsq[:, :, cl], ALU.mult, inc=XS)
                m_of_read[0] = (XS, XS.n)
            o2_mark = (aux_of(t)[1], aux_of(t)[1].n)
            o2v_prev[0] = o2_mark

            if do_mix and t + 1 < nt:
                nslot = load_slab(wpa[0])
                preloaded[0] = {0: nslot}
                emit_mix_front(t + 1, nslot[0], nslot[1])

            for oc in range(FC if (do_epi_t and epi_mode >= 3) else 0):
                seg = 13 + oc // 2
                slot, tgt = tile_slabs[seg]
                TE.wait_m(tgt)
                b = banks[2] if oc % 2 == 0 else banks[3]
                TE.wait_m(b.rmark)
                TE.wait_m(o2_mark)
                for vc in range(FC):
                    TE.ins("matmul", b.ps[:], seg_128(slot, (oc % 2) * FC + vc),
                           o2_sb[:, vc, :], start=(vc == 0), stop=(vc == FC - 1),
                           inc=(S_T if vc == FC - 1 else None))
                b.wmark = (S_T, S_T.n)
                if oc % 2 == 1 or oc == FC - 1:
                    slot_free[slot] = (S_T, S_T.n)
                VE.wait_m(b.wmark)
                VE.ins("tensor_tensor", xbuf[:, tp, oc, :], b.ps[:],
                       xbuf[:, tp, oc, :], ALU.add, inc=S_V)
                b.rmark = (S_V, S_V.n)
            o2_read[t] = (S_T, S_T.n)
            x2_mark[t] = (S_V, S_V.n) if (do_epi_t and epi_mode >= 3) else ln_ready[t]

            if epi_mode >= 4 or not do_epi_t:
                SY.wait_m(x2_mark[t])
                x2_store_t[t] = dma("sy", sem_x2w[tp], x2s[t], xbuf[:, tp])

        TE.wait_m(ps_m.rmark)
        TE.ins("matmul", ps_m.ps[0:1, 0:1], one1_b[:], one1_b[:],
               start=True, stop=True, inc=S_PH)
        VE.ins("memset", dummy_sb[0:1, 0:1], 0.0, inc=S_PH)
        SE.ins("activation", dummy_sb[0:1, 1:2], eps_col[0:1, 0:1], AF.Copy,
               inc=S_PH)
        GE.ins("memset", dummy_sb[0:1, 2:3], 0.0, inc=S_PH)
        PH = S_PH.n
        for ep in (TE, VE, SE, GE):
            ep.wait(S_PH, PH)

        esA.close()

        x2b = E(nc.sbuf_tensor("x2b", [128, FC, FT], F32))
        xbb = E(nc.sbuf_tensor("xbb", [128, FC, FT], BF16))
        h0b = E(nc.sbuf_tensor("h0b", [128, 32, FT], BF16))
        h1b = E(nc.sbuf_tensor("h1b", [128, 32, FT], BF16))
        sub = E(nc.sbuf_tensor("sub", [128, 2, 512], BF16))
        mean_B = h1b[:, 0:4, :].bitcast(F32).rearrange("p a b -> p (a b)")
        var_B = h1b[:, 4:8, :].bitcast(F32).rearrange("p a b -> p (a b)")
        cen_B = h1b[:, 8:12, :].bitcast(F32).rearrange("p a b -> p (a b)")
        xsq_B = h0b[:, 0:8, :]

        for sm, v in t_pd_ffn:
            SY.wait(sm, v)
        SE.wait_m(t_fsc)
        sem_x2r = new_dsem("d_x2r")
        sem_x2r2 = new_dsem("d_x2rb")
        sem_yoc = [new_dsem(f"d_y{o}") for o in range(FC)]

        h2_done_prev = None
        w3_done_prev = None
        su_read = [None, None]

        y_store_t = [None] * NFT
        for ft in range(nft):
            if ft == 0:
                for sm, v in t_pd_ffn:
                    GE.wait(sm, v)
                GE.wait_m(t_awx2)
                GE.wait_m(t_aww2)
            if x2_store_t[2 * ft] is not None:
                GE.wait_m(x2_store_t[2 * ft])
            if x2_store_t[2 * ft + 1] is not None:
                GE.wait_m(x2_store_t[2 * ft + 1])
            if ft >= 1 and y_store_t[ft - 1] is not None:
                for mk in y_store_t[ft - 1]:
                    GE.wait_m(mk)
            tgt_x2a = dma("gp", sem_x2r, x2b[:, :, 0:TILE], x2s[2 * ft])
            tgt_x2b = dma("gp", sem_x2r2, x2b[:, :, TILE:FT], x2s[2 * ft + 1])

            half_marks = []
            for hf in range(2):
                hs = slice(hf * 512, (hf + 1) * 512)
                SE.wait_m(tgt_x2a if hf == 0 else tgt_x2b)
                if hf == 0:
                    SE.wait_m(w3_done_prev)
                for fc in range(FC):
                    SE.ins("activation", xsq_B[:, fc, hs], x2b[:, fc, hs],
                           AF.Square, inc=(S_S if fc == FC - 1 else None))
                sqm = S_S.n
                b0, b1 = banks[0], banks[1]
                TE.wait_m(b0.rmark)
                TE.wait_m(tgt_x2a if hf == 0 else tgt_x2b)
                for fc in range(FC):
                    TE.ins("matmul", b0.ps[:], ones_f[:], x2b[:, fc, hs],
                           start=(fc == 0), stop=(fc == FC - 1),
                           inc=(S_T if fc == FC - 1 else None))
                b0.wmark = (S_T, S_T.n)
                TE.wait_m(b1.rmark)
                TE.wait(S_S, sqm)
                for fc in range(FC):
                    TE.ins("matmul", b1.ps[:], ones_b[:], xsq_B[:, fc, hs],
                           start=(fc == 0), stop=(fc == FC - 1),
                           inc=(S_T if fc == FC - 1 else None))
                b1.wmark = (S_T, S_T.n)
                mh, vh, ch = mean_B[:, hs], var_B[:, hs], cen_B[:, hs]
                VE.wait_m(b0.wmark)
                if hf == 0:
                    VE.wait_m(h2_done_prev)
                VE.ins("tensor_scalar", mh, b0.ps[:], 1.0 / HID, None, ALU.mult,
                       inc=S_V)
                b0.rmark = (S_V, S_V.n)
                VE.wait_m(b1.wmark)
                VE.ins("tensor_scalar", vh, b1.ps[:], 1.0 / HID, None, ALU.mult,
                       inc=S_V)
                b1.rmark = (S_V, S_V.n)
                VE.ins("tensor_tensor", ch, mh, mh, ALU.mult)
                VE.ins("tensor_tensor", vh, vh, ch, ALU.subtract, inc=S_V)
                SE.wait(S_V, S_V.n)
                SE.ins("activation", vh, vh, AF.Sqrt, bias=eps_col[:], inc=S_S)
                VE.wait(S_S, S_S.n)
                VE.ins("reciprocal", vh, vh)
                for fc in range(FC):
                    VE.ins("tensor_tensor", ch, x2b[:, fc, hs], mh, ALU.subtract)
                    VE.ins("tensor_tensor", xbb[:, fc, hs], ch, vh, ALU.mult,
                           inc=(S_V if fc == FC - 1 else None))
                half_marks.append((S_V, S_V.n))
            xb_mark = (S_V, S_V.n)

            grp = [0]

            def next_bank():
                b = banks[grp[0] % 4]
                grp[0] += 1
                return b

            for o in range(32):
                slot, tgt = load_slab(wpf[o])
                TE.wait_m(tgt)
                for hf in range(2):
                    hs = slice(hf * 512, (hf + 1) * 512)
                    bu = next_bank()
                    if o == 0:
                        TE.wait_m(half_marks[hf])
                    TE.wait_m(bu.rmark)
                    for k in range(FC):
                        TE.ins("matmul", bu.ps[:],
                               ring[:, slot, k * 128:(k + 1) * 128],
                               xbb[:, k, hs], start=(k == 0), stop=(k == FC - 1),
                               inc=(S_T if k == FC - 1 else None))
                    bu.wmark = (S_T, S_T.n)
                    bv = next_bank()
                    TE.wait_m(bv.rmark)
                    for k in range(FC):
                        TE.ins("matmul", bv.ps[:],
                               ring[:, slot, 1024 + k * 128:1024 + (k + 1) * 128],
                               xbb[:, k, hs], start=(k == 0), stop=(k == FC - 1),
                               inc=(S_T if k == FC - 1 else None))
                    bv.wmark = (S_T, S_T.n)
                    SE.wait_m(bu.wmark)
                    SE.wait_m(su_read[hf])
                    SE.ins("activation", sub[:, hf, :], bu.ps[:], AF.Silu,
                           scale=fsc_sb[:, FS_W1 + o:FS_W1 + o + 1], inc=S_S)
                    bu.rmark = (S_S, S_S.n)
                    VE.wait_m(bv.wmark)
                    VE.wait(S_S, S_S.n)
                    VE.ins("scalar_tensor_tensor", h0b[:, o, hs], bv.ps[:],
                           fsc_sb[:, FS_W2 + o:FS_W2 + o + 1], sub[:, hf, :],
                           ALU.mult, ALU.mult, inc=S_V)
                    bv.rmark = (S_V, S_V.n)
                    su_read[hf] = (S_V, S_V.n)
                slot_free[slot] = (S_T, S_T.n)
            h0_mark = (S_V, S_V.n)

            def hidden_stage(base_slab, n_o, src, dst, scale_base, func,
                             first_wait, evac_v=False, per_oc_out=None):
                last = None
                for o in range(n_o):
                    slot1, tgt1 = load_slab(wpf[base_slab + 2 * o])
                    slot2, tgt2 = load_slab(wpf[base_slab + 2 * o + 1])
                    TE.wait_m(tgt1)
                    TE.wait_m(tgt2)
                    if o == 0 and first_wait is not None:
                        TE.wait_m(first_wait)
                    for hf in range(2):
                        hs = slice(hf * 512, (hf + 1) * 512)
                        b = next_bank()
                        TE.wait_m(b.rmark)
                        for k in range(32):
                            slot = slot1 if k < 16 else slot2
                            kk = k % 16
                            TE.ins("matmul", b.ps[:],
                                   ring[:, slot, kk * 128:(kk + 1) * 128],
                                   src[:, k, hs], start=(k == 0), stop=(k == 31),
                                   inc=(S_T if k == 31 else None))
                        b.wmark = (S_T, S_T.n)
                        if evac_v:
                            VE.wait_m(b.wmark)
                            VE.ins("scalar_tensor_tensor", dst[:, o, hs],
                                   b.ps[:],
                                   fsc_sb[:, scale_base + o:scale_base + o + 1],
                                   dst[:, o, hs], ALU.mult, ALU.add, inc=S_V)
                            b.rmark = (S_V, S_V.n)
                            last = (S_V, S_V.n)
                            if hf == 1 and per_oc_out is not None:
                                per_oc_out[o] = (S_V, S_V.n)
                        else:
                            SE.wait_m(b.wmark)
                            SE.ins("activation", dst[:, o, hs], b.ps[:], func,
                                   scale=fsc_sb[:, scale_base + o:scale_base + o + 1],
                                   inc=S_S)
                            b.rmark = (S_S, S_S.n)
                            last = (S_S, S_S.n)
                    slot_free[slot1] = (S_T, S_T.n)
                    slot_free[slot2] = (S_T, S_T.n)
                return last, (S_T, S_T.n)

            h1_done, _ = hidden_stage(32, 32, h0b, h1b, FS_H1, AF.Gelu, h0_mark)
            h2_done, _ = hidden_stage(96, 32, h1b, h0b, FS_H2, AF.Gelu, h1_done)
            h2_done_prev = h2_done
            per_oc = [None] * FC
            y_done, w3_mm = hidden_stage(160, 8, h0b, x2b, FS_W3, None, h2_done,
                                         evac_v=True, per_oc_out=per_oc)
            w3_done_prev = w3_mm

            ysl = slice(ft * FT, (ft + 1) * FT)
            yoc_marks = []
            for o in range(FC):
                GE.wait_m(per_oc[o])
                yoc_marks.append(dma("gp", sem_yoc[o], yTr[:, o, ysl],
                                     x2b[:, o, :]))
            y_store_t[ft] = yoc_marks

        for sm in all_dma_sems:
            if sm.n > 0:
                GE.wait(sm, sm.n)

        block = E(nc.Block())
        block.sync(lambda e: SY.run(e))
        block.tensor(lambda e: TE.run(e))
        block.vector(lambda e: VE.run(e))
        block.scalar(lambda e: SE.run(e))
        block.gpsimd(lambda e: GE.run(e))

    return nc



def make_in_maps(inputs):
    p_attn = {k[len("L0_attn_"):]: np.asarray(v, np.float32)
              for k, v in inputs.items()
              if k.startswith("L0_attn_") and k != "L0_attn_hidden_states"}
    p_ffn = {k[len("L0_ffn_"):]: np.asarray(v, np.float32)
             for k, v in inputs.items() if k.startswith("L0_ffn_")}
    x = np.asarray(inputs["L0_attn_hidden_states"], np.float32)

    aslabs = np.ascontiguousarray(pack_attn_slabs(p_attn))
    fslabs = np.ascontiguousarray(pack_ffn_slabs(p_ffn))
    common = {
        "aslab": aslabs,
        "fslab": fslabs,
        "awx2": np.ascontiguousarray(pack_awx2(p_attn)),
        "aww2": np.ascontiguousarray(pack_aww2(p_attn)),
        "avecs": np.ascontiguousarray(pack_avecs(p_attn)),
        "fscales": np.ascontiguousarray(pack_fscales(p_ffn)),
    }
    in_maps = []
    for b in range(NCORES):
        m = dict(common)
        m["xT"] = np.ascontiguousarray(x[b].T)
        in_maps.append(m)
    return in_maps


def kernel(**inputs):
    in_maps = make_in_maps(inputs)
    nc = build_nc()
    res = run_bass_kernel_spmd(nc, in_maps, core_ids=list(range(NCORES)))
    out = np.stack([np.ascontiguousarray(r["yT"].T) for r in res.results])
    return out.astype(np.float32)


# version 21
# speedup vs baseline: 1.0764x; 1.0764x over previous
import contextlib
import numpy as np
import concourse.bass as bass
import concourse.mybir as mybir
from concourse.bass_utils import run_bass_kernel_spmd

F32 = mybir.dt.float32
BF16 = mybir.dt.bfloat16
AF = mybir.ActivationFunctionType
ALU = mybir.AluOpType
AX = mybir.AxisListType

NCORES = 8
T = 4096
HID = 1024
FC = 8
TILE = 512
NT = T // TILE
C = 128
NCH = TILE // C
SC = 32
NSUB = C // SC
NH = 4
DK = 128
DV = 256
DH = 4096
FT = 1024
NFT = T // FT
SLABW = 2048
N_ASLAB = 17
N_FSLAB = 176
RING = 6
EPS = 1e-5



def _lhsT_chunks(w, kparts, oparts):
    O, K = w.shape
    noc, nkc = O // oparts, K // kparts
    t = w.reshape(noc, oparts, nkc, kparts)
    return np.ascontiguousarray(t.transpose(0, 2, 3, 1))


def pack_attn_slabs(p):
    slabs = np.zeros((N_ASLAB, 128, SLABW), np.float32)
    s0 = []
    wx0 = p["W_x0"]
    for n in range(5):
        for k in range(8):
            s0.append(wx0[n * 32:(n + 1) * 32, k * 128:(k + 1) * 128].T)
    ww1 = p["W_w1"]
    for k in range(8):
        s0.append(ww1[:, k * 128:(k + 1) * 128].T)
    col = 0
    for ch in s0:
        w = ch.shape[1]
        slabs[0, :, col:col + w] = ch
        col += w
    assert col == 1792

    def fill(weight, base_slab):
        chs = _lhsT_chunks(weight, 128, 128)
        noc, nkc = chs.shape[0], chs.shape[1]
        si, col = base_slab, 0
        for oc in range(noc):
            for kc in range(nkc):
                slabs[si, :, col:col + 128] = chs[oc, kc]
                col += 128
                if col == SLABW:
                    si += 1
                    col = 0

    fill(p["W_r"], 1)
    fill(p["W_k"], 3)
    wv = p["W_v"]
    si, col = 5, 0
    for half in range(2):
        for k in range(8):
            blk = wv[half * 512:(half + 1) * 512, k * 128:(k + 1) * 128].T
            slabs[si, :, col:col + 512] = blk
            col += 512
            if col == SLABW:
                si += 1
                col = 0
    fill(p["W_g"], 9)
    fill(p["W_o"], 13)
    return slabs


def pack_ffn_slabs(p):
    slabs = np.zeros((N_FSLAB, 128, SLABW), np.float32)
    w1 = _lhsT_chunks(p["w1_w"], 128, 128)
    w2 = _lhsT_chunks(p["w2_w"], 128, 128)
    for o in range(32):
        slabs[o, :, 0:1024] = w1[o].transpose(1, 0, 2).reshape(128, 1024)
        slabs[o, :, 1024:2048] = w2[o].transpose(1, 0, 2).reshape(128, 1024)
    h1 = _lhsT_chunks(p["h1_w"], 128, 128)
    h2 = _lhsT_chunks(p["h2_w"], 128, 128)
    for o in range(32):
        slabs[32 + 2 * o] = h1[o, 0:16].transpose(1, 0, 2).reshape(128, 2048)
        slabs[32 + 2 * o + 1] = h1[o, 16:32].transpose(1, 0, 2).reshape(128, 2048)
        slabs[96 + 2 * o] = h2[o, 0:16].transpose(1, 0, 2).reshape(128, 2048)
        slabs[96 + 2 * o + 1] = h2[o, 16:32].transpose(1, 0, 2).reshape(128, 2048)
    w3 = _lhsT_chunks(p["w3_w"], 128, 128)
    for o in range(8):
        slabs[160 + 2 * o] = w3[o, 0:16].transpose(1, 0, 2).reshape(128, 2048)
        slabs[160 + 2 * o + 1] = w3[o, 16:32].transpose(1, 0, 2).reshape(128, 2048)
    return slabs


def pack_awx2(p):
    out = np.zeros((32, 5 * 8 * 128), np.float32)
    wx2 = p["W_x2"]
    i = 0
    for n in range(5):
        for fc in range(8):
            out[:, i * 128:(i + 1) * 128] = \
                wx2[fc * 128:(fc + 1) * 128, n * 32:(n + 1) * 32].T
            i += 1
    return out


def pack_aww2(p):
    out = np.zeros((64, 4 * 128), np.float32)
    ww2 = p["W_w2"]
    for oc in range(4):
        out[:, oc * 128:(oc + 1) * 128] = ww2[oc * 128:(oc + 1) * 128, :].T
    return out


AV_MU0, AV_XB, AV_BW2, AV_U, AV_GNW, AV_GNB, AV_N = 0, 8, 48, 52, 56, 64, 72


def pack_avecs(p):
    out = np.zeros((128, AV_N), np.float32)
    out[:, AV_MU0:AV_MU0 + 8] = p["mu0"].reshape(8, 128).T
    for n in range(5):
        out[:, AV_XB + n * 8:AV_XB + (n + 1) * 8] = p["x_bias"][n].reshape(8, 128).T
    out[:, AV_BW2:AV_BW2 + 4] = p["b_w2"].reshape(4, 128).T
    out[:, AV_U:AV_U + 4] = p["bonus"].T
    out[:, AV_GNW:AV_GNW + 8] = p["gn_weight"].reshape(8, 128).T
    out[:, AV_GNB:AV_GNB + 8] = p["gn_bias"].reshape(8, 128).T
    return out


FS_W1, FS_W2, FS_H1, FS_H2, FS_W3, FS_N = 0, 32, 64, 96, 128, 136


def pack_fscales(p):
    out = np.zeros((128, FS_N), np.float32)
    out[:, FS_W1:FS_W1 + 32] = p["w1_s"].reshape(32, 128).T
    out[:, FS_W2:FS_W2 + 32] = p["w2_s"].reshape(32, 128).T
    out[:, FS_H1:FS_H1 + 32] = p["h1_s"].reshape(32, 128).T
    out[:, FS_H2:FS_H2 + 32] = p["h2_s"].reshape(32, 128).T
    out[:, FS_W3:FS_W3 + 8] = p["w3_s"].reshape(8, 128).T
    return out



class Sem:
    def __init__(self, handle):
        self.h = handle
        self.n = 0


class EP:
    def __init__(self, name):
        self.name = name
        self.cmds = []
        self.selfsem = None

    def ins(self, meth, *args, inc=None, inck=1, **kw):
        chain = self.selfsem is not None and meth != "dma_start"
        if chain and self.selfsem.n > 0:
            self.cmds.append(("wait_ge", (self.selfsem.h, self.selfsem.n), {},
                              None, 0))
        if inc is None and chain:
            inc = self.selfsem
            inck = 1
        if inc is not None:
            inc.n += inck
            self.cmds.append((meth, args, kw, inc.h, inck))
        else:
            self.cmds.append((meth, args, kw, None, 0))

    def wait(self, sem, val):
        if val > 0:
            self.cmds.append(("wait_ge", (sem.h, val), {}, None, 0))

    def wait_m(self, mark):
        if mark is not None:
            self.wait(mark[0], mark[1])

    def run(self, e):
        for idx, (meth, args, kw, inch, inck) in enumerate(self.cmds):
            try:
                r = getattr(e, meth)(*args, **kw)
            except Exception:
                print(f"FAILED {self.name} cmd#{idx} {meth} "
                      f"args={[getattr(a, 'shape', a) for a in args]}")
                raise
            if inch is not None:
                r.then_inc(inch, inck)


class Bank:
    def __init__(self, ps):
        self.ps = ps
        self.wmark = None
        self.rmark = None


def build_nc(debug_dump=(), nt=NT, nft=NFT, do_mix=True, do_scan=True, do_epi=True, epi_mode=4):
    nc = bass.Bass("TRN2", target_bir_lowering=False, debug=False)
    dd = set(debug_dump)

    xT = nc.dram_tensor("xT", [HID, T], F32, kind="ExternalInput").ap()
    aslab = nc.dram_tensor("aslab", [N_ASLAB, 128, SLABW], F32, kind="ExternalInput").ap()
    fslab = nc.dram_tensor("fslab", [N_FSLAB, 128, SLABW], F32, kind="ExternalInput").ap()
    awx2 = nc.dram_tensor("awx2", [32, 5 * 8 * 128], F32, kind="ExternalInput").ap()
    aww2 = nc.dram_tensor("aww2", [64, 4 * 128], F32, kind="ExternalInput").ap()
    avecs = nc.dram_tensor("avecs", [128, AV_N], F32, kind="ExternalInput").ap()
    fscales = nc.dram_tensor("fscales", [128, FS_N], F32, kind="ExternalInput").ap()
    yT = nc.dram_tensor("yT", [HID, T], F32, kind="ExternalOutput").ap()
    wpa = nc.dram_tensor("wpa", [N_ASLAB, 128, SLABW], BF16, kind="Internal").ap()
    wpf = nc.dram_tensor("wpf", [N_FSLAB, 128, SLABW], BF16, kind="Internal").ap()
    x2kind = "ExternalOutput" if "x2s" in dd else "Internal"
    x2s = nc.dram_tensor("x2s", [NT, 128, FC, TILE], F32, kind=x2kind).ap()

    xTr = xT.rearrange("(fc p) t -> p fc t", p=128)
    yTr = yT.rearrange("(fc p) t -> p fc t", p=128)

    TE, VE, SE, GE, SY = EP("tensor"), EP("vector"), EP("scalar"), EP("gpsimd"), EP("sync")
    globals()["_last_eps"] = {"TE": TE, "VE": VE, "SE": SE, "GE": GE, "SY": SY}

    es = contextlib.ExitStack()
    with es:
        E = es.enter_context

        S_T = Sem(E(nc.semaphore("s_t")))
        S_V = Sem(E(nc.semaphore("s_v")))
        S_S = Sem(E(nc.semaphore("s_s")))
        S_P = Sem(E(nc.semaphore("s_p")))
        S_PH = Sem(E(nc.semaphore("s_ph")))
        VE.selfsem = S_V
        SE.selfsem = S_S
        GE.selfsem = S_P

        all_dma_sems = []

        def new_dsem(name):
            sm = Sem(E(nc.semaphore(name)))
            all_dma_sems.append(sm)
            return sm

        def dma(q, sem, out, in_, **kw):
            ep = SY if q == "sy" else GE
            ep.ins("dma_start", out=out, in_=in_, inc=sem, inck=16, **kw)
            return (sem, sem.n)

        ring = E(nc.sbuf_tensor("ring", [128, RING, SLABW], BF16))
        ones_f = E(nc.sbuf_tensor("ones_f", [128, 128], F32))
        ones_b = E(nc.sbuf_tensor("ones_b", [128, 128], BF16))
        id_b = E(nc.sbuf_tensor("id_b", [128, 128], BF16))
        mask_su = E(nc.sbuf_tensor("mask_su", [128, 128], F32))
        ones1_b = E(nc.sbuf_tensor("ones1_b", [128, 1], BF16))
        one1_b = E(nc.sbuf_tensor("one1_b", [1, 1], BF16))
        eps_col = E(nc.sbuf_tensor("eps_col", [128, 1], F32))
        vec_sb = E(nc.sbuf_tensor("vec_sb", [128, AV_N], F32))
        fsc_sb = E(nc.sbuf_tensor("fsc_sb", [128, FS_N], F32))
        dummy_sb = E(nc.sbuf_tensor("dummy_sb", [128, 4], F32))

        banks = [Bank(E(nc.psum_tensor(f"pb{i}", [128, 512], F32))) for i in range(4)]
        ps_m = Bank(E(nc.psum_tensor("ps_m", [128, 256], F32)))
        ps_o = Bank(E(nc.psum_tensor("ps_o", [128, 256], F32)))
        ps_kv = Bank(E(nc.psum_tensor("ps_kv", [128, 256], F32)))
        ps_t7 = Bank(E(nc.psum_tensor("ps_t7", [128, 128], BF16)))

        def tp_view(bank):
            if bank is ps_t7:
                return ps_t7.ps[:]
            return bank.ps[:, 0:64].bitcast(BF16)

        hoist_done = {}

        VE.ins("memset", ones_b[:], 1.0)
        VE.ins("memset", ones1_b[:], 1.0)
        VE.ins("memset", one1_b[:], 1.0)
        VE.ins("memset", mask_su[:], 1.0, inc=S_V)
        M_CONST_VB = (S_V, S_V.n)
        GE.wait_m(M_CONST_VB)
        GE.ins("affine_select", id_b[:], ones_b[:], [[1, 128]], ALU.is_equal, 0.0,
               base=0, channel_multiplier=-1, inc=S_P)
        GE.ins("affine_select", mask_su[:], mask_su[:], [[1, 128]], ALU.is_gt, 0.0,
               base=0, channel_multiplier=-1, inc=S_P)
        M_CONST_GP = (S_P, S_P.n)

        sem_misc = new_dsem("d_misc")
        t_vec = dma("sy", sem_misc, vec_sb[:], avecs[:])
        t_fsc = dma("sy", sem_misc, fsc_sb[:], fscales[:])
        t_vec = t_fsc

        slot_free = [None] * RING
        slab_seq = [0]
        sem_ring = [new_dsem(f"d_ring{j}") for j in range(RING)]

        def load_slab(src_ap):
            i = slab_seq[0]
            slot = i % RING
            SY.wait_m(slot_free[slot])
            tgt = dma("sy", sem_ring[slot], ring[:, slot, :], src_ap)
            slab_seq[0] += 1
            return slot, tgt

        esA = contextlib.ExitStack()
        A = esA.enter_context
        pdb = A(nc.sbuf_tensor("pdb", [128, 2, SLABW], BF16))
        pd_srcs = [(aslab[i], wpa[i]) for i in range(N_ASLAB)]
        pd_srcs += [(fslab[i], wpf[i]) for i in range(N_FSLAB)]
        NPD = len(pd_srcs)
        sem_pdl = [new_dsem(f"d_pdl{j}") for j in range(2)]
        sem_pds = [new_dsem(f"d_pds{j}") for j in range(2)]
        pd_load = [None] * NPD
        pd_store = [None] * NPD
        t_awx2 = t_aww2 = None
        awx2_sb = A(nc.sbuf_tensor("awx2_sb", [32, 5 * 8 * 128], BF16))
        aww2_sb = A(nc.sbuf_tensor("aww2_sb", [64, 4 * 128], BF16))
        sem_aw = new_dsem("d_aw")
        sem_aw2 = new_dsem("d_aw2")
        t_awx2 = dma("gp", sem_aw, awx2_sb[:], awx2[:])
        t_aww2 = dma("gp", sem_aw2, aww2_sb[:], aww2[:])
        t0_slabs = {}
        sem_t0 = [new_dsem(f"d_t0s{j}") for j in range(3)]
        for seg in range(3):
            slot = slab_seq[0] % RING
            tgt = dma("gp", sem_t0[seg], ring[:, slot, :], aslab[seg])
            t0_slabs[seg] = (slot, tgt)
            slab_seq[0] += 1
        for i in range(NPD):
            j = i % 2
            if i >= 2:
                GE.wait_m(pd_store[i - 2])
            pd_load[i] = dma("gp", sem_pdl[j], pdb[:, j, :], pd_srcs[i][0])
            if i >= 1:
                GE.wait_m(pd_load[i - 1])
                pd_store[i - 1] = dma("gp", sem_pds[(i - 1) % 2],
                                      pd_srcs[i - 1][1], pdb[:, (i - 1) % 2, :])
        GE.wait_m(pd_load[NPD - 1])
        pd_store[NPD - 1] = dma("gp", sem_pds[(NPD - 1) % 2],
                                pd_srcs[NPD - 1][1], pdb[:, (NPD - 1) % 2, :])
        def pd_gate(upto):
            marks = {}
            for i in range(upto):
                sm, v = pd_store[i]
                marks[sm] = max(marks.get(sm, 0), v)
            return list(marks.items())
        t_pd_attn = pd_gate(N_ASLAB)
        t_pd_ffn = pd_gate(NPD)

        xbuf = A(nc.sbuf_tensor("xbuf", [128, 2, FC, TILE], F32))
        lnb = A(nc.sbuf_tensor("lnb", [128, 2, FC, TILE + 8], BF16))
        sgxsq = A(nc.sbuf_tensor("sgxsq", [128, FC, TILE], BF16))
        scrA = A(nc.sbuf_tensor("scrA", [128, 2048], F32))
        delta = scrA[:].bitcast(BF16).rearrange("p (a b) -> p a b", a=FC)
        ics4 = scrA[:, 0:512].rearrange("p (a b) -> p a b", a=NH)
        cs4 = scrA[:, 512:1024].rearrange("p (a b) -> p a b", a=NH)
        arg4 = [scrA[:, 1024 + i * 512:1536 + i * 512]
                .rearrange("p (a b) -> p a b", a=NH) for i in range(2)]
        scrB = A(nc.sbuf_tensor("scrB", [128, 2048], F32))
        xi0v = scrB[:].bitcast(BF16).rearrange("p (a b) -> p a b", a=FC)
        xi1 = A(nc.sbuf_tensor("xi1", [128, FC, TILE], BF16))

        def xi_v(pipe):
            return xi0v if pipe == 0 else xi1[:]

        exp4 = [scrB[:, i * 512:(i + 1) * 512]
                .rearrange("p (a b) -> p a b", a=NH) for i in range(2)]
        _scrBb = scrB[:, 1024:2048].bitcast(BF16)
        rg4 = _scrBb[:, 0:512].rearrange("p (a b) -> p a b", a=NH)
        rl4 = _scrBb[:, 512:1024].rearrange("p (a b) -> p a b", a=NH)
        khf4 = _scrBb[:, 1024:1536].rearrange("p (a b) -> p a b", a=NH)
        kf4 = _scrBb[:, 1536:2048].rearrange("p (a b) -> p a b", a=NH)
        tmpm = A(nc.sbuf_tensor("tmpm", [128, FC, TILE], BF16))
        cenb = tmpm[:].rearrange("p a (b c) -> p (a b) c", c=DV)
        xpN = A(nc.sbuf_tensor("xpN", [32, 5, TILE], BF16))
        th_sb = A(nc.sbuf_tensor("th_sb", [64, TILE], BF16))
        r_sb = A(nc.sbuf_tensor("r_sb", [128, NH, TILE], BF16))
        k_sb = A(nc.sbuf_tensor("k_sb", [128, NH, TILE], BF16))
        g_sb = A(nc.sbuf_tensor("g_sb", [128, FC, TILE], BF16))
        w_sb = A(nc.sbuf_tensor("w_sb", [128, NH, TILE], F32))
        o2_sb = A(nc.sbuf_tensor("o2_sb", [128, FC, TILE], BF16))
        mean_t = A(nc.sbuf_tensor("mean_t", [128, TILE], F32))
        var_t = A(nc.sbuf_tensor("var_t", [128, TILE], F32))
        cen_s = A(nc.sbuf_tensor("cen_s", [128, TILE], F32))
        td4 = A(nc.sbuf_tensor("td4", [128, NH, C], BF16))
        kht4 = A(nc.sbuf_tensor("kht4", [128, NH, C], BF16))
        vtt = A(nc.sbuf_tensor("vtt", [128, NCH, NH, DV], BF16))
        mm4 = A(nc.sbuf_tensor("mm4", [128, NH, C], BF16))
        drow4 = A(nc.sbuf_tensor("drow4", [1, NH * C], BF16))
        od1 = A(nc.sbuf_tensor("od1", [128, 2 * C], F32))
        osb4 = A(nc.sbuf_tensor("osb4", [128, NH, 2 * C], F32))
        dcol4 = A(nc.sbuf_tensor("dcol4", [128, NH], F32))
        sum4 = A(nc.sbuf_tensor("sum4", [128, NH], F32))
        mean4 = A(nc.sbuf_tensor("mean4", [128, NH], F32))
        sumv16 = A(nc.sbuf_tensor("sumv16", [128, NCH * NH], F32))
        S_sb = A(nc.sbuf_tensor("S_sb", [128, NH, DV], F32))
        Sb_sb = A(nc.sbuf_tensor("Sb_sb", [128, NH, DV], BF16))
        varg = A(nc.sbuf_tensor("varg", [128, NCH * NH], F32))
        stdg = A(nc.sbuf_tensor("stdg", [128, NCH * NH], F32))
        onb4 = A(nc.sbuf_tensor("onb4", [128, NH, 2 * C], BF16))
        ofb4 = A(nc.sbuf_tensor("ofb4", [128, FC, C], BF16))

        VE.ins("memset", ones_f[:], 1.0)
        VE.ins("memset", eps_col[:], EPS)
        VE.ins("memset", S_sb[:], 0.0)
        VE.ins("memset", Sb_sb[:], 0.0)
        VE.ins("memset", lnb[:, 0, :, 7:8], 0.0, inc=S_V)
        M_CONST_V = (S_V, S_V.n)

        x_load_t = [None] * (NT + 1)
        stats_pre = [None] * (NT + 1)
        std_ready = [None] * (NT + 1)
        ln_ready = [None] * (NT + 1)
        xsq_read = [None] * (NT + 1)
        o2_read = [None] * NT
        sg_write_ok = [None] * NT
        w_last_read = [None] * NT
        v_last_read = [None] * NT
        x2_mark = [None] * NT
        x2_store_t = [None] * NT
        projdone = {}
        proj_mm_done = {}
        th_read = [None] * NT
        xp_read = [None] * NT
        o2v_prev = [None]
        preloaded = [None]
        m_scan_sread = [None]
        m_ogrp = [None]
        m_tp_done = [None]
        m_kv_all = [None]
        m_od_all = [None]
        m_sbf_all = [M_CONST_V]
        m_supd_all = [None]
        m_scrB_free = [None]
        m_dmm = [None]
        m_sc_read = [None]
        m_ep_tp = [None]
        m_of_read = [None]
        arg_sread = [None, None]
        exp_vread = [None, None]

        def emit_ln_stats(t, is_first=False):
            xb = xbuf[:, t % 2]
            SE.wait_m(x_load_t[t])
            if t >= 2:
                SE.wait_m(o2_read[t - 2])
                SE.wait_m(x2_mark[t - 2])
            for fc in range(FC):
                SE.ins("activation", sgxsq[:, fc, :], xb[:, fc, :], AF.Square,
                       inc=(S_S if fc == FC - 1 else None))
            sq_mark = S_S.n
            b_sum, b_sumsq = banks[0], banks[1]
            TE.wait_m(b_sum.rmark)
            TE.wait_m(x_load_t[t])
            if is_first:
                TE.wait_m(M_CONST_V)
                TE.wait_m(M_CONST_GP)
            for fc in range(FC):
                TE.ins("matmul", b_sum.ps[:], ones_f[:], xb[:, fc, :],
                       start=(fc == 0), stop=(fc == FC - 1),
                       inc=(S_T if fc == FC - 1 else None))
            b_sum.wmark = (S_T, S_T.n)
            TE.wait_m(b_sumsq.rmark)
            TE.wait(S_S, sq_mark)
            for fc in range(FC):
                TE.ins("matmul", b_sumsq.ps[:], ones_b[:], sgxsq[:, fc, :],
                       start=(fc == 0), stop=(fc == FC - 1),
                       inc=(S_T if fc == FC - 1 else None))
            b_sumsq.wmark = (S_T, S_T.n)
            xsq_read[t] = (S_T, S_T.n)
            VE.wait_m(b_sum.wmark)
            VE.ins("tensor_scalar", mean_t[:], b_sum.ps[:], 1.0 / HID, None,
                   ALU.mult, inc=S_V)
            b_sum.rmark = (S_V, S_V.n)
            VE.wait_m(b_sumsq.wmark)
            VE.ins("tensor_scalar", var_t[:], b_sumsq.ps[:], 1.0 / HID, None,
                   ALU.mult, inc=S_V)
            b_sumsq.rmark = (S_V, S_V.n)
            VE.ins("tensor_tensor", cen_s[:], mean_t[:], mean_t[:], ALU.mult)
            VE.ins("tensor_tensor", var_t[:], var_t[:], cen_s[:], ALU.subtract,
                   inc=S_V)
            stats_pre[t] = (S_V, S_V.n)

        def emit_ln_sqrt(t):
            SE.wait_m(stats_pre[t])
            SE.ins("activation", var_t[:], var_t[:], AF.Sqrt, bias=eps_col[:],
                   inc=S_S)
            VE.wait(S_S, S_S.n)
            VE.ins("reciprocal", var_t[:], var_t[:], inc=S_V)
            std_ready[t] = (S_V, S_V.n)

        def emit_ln_normalize(t):
            xb = xbuf[:, t % 2]
            lb = lnb[:, t % 2]
            VE.wait_m(std_ready[t])
            VE.wait_m(m_scan_sread[0])
            if t >= 2:
                VE.wait_m(projdone.get((t - 2, "lnb_free")))
            for fc in range(FC):
                VE.ins("tensor_tensor", cen_s[:], xb[:, fc, :], mean_t[:],
                       ALU.subtract)
                VE.ins("tensor_tensor", lb[:, fc, 8:TILE + 8], cen_s[:], var_t[:],
                       ALU.mult)
            if t > 0:
                pb = lnb[:, (t - 1) % 2]
                for fc in range(FC):
                    VE.ins("tensor_copy", lb[:, fc, 7:8], pb[:, fc, TILE + 7:TILE + 8])
            for fc in range(FC):
                VE.ins("tensor_tensor", delta[:, fc, :], lb[:, fc, 7:TILE + 7],
                       lb[:, fc, 8:TILE + 8], ALU.subtract,
                       inc=(S_V if fc == FC - 1 else None))
            ln_ready[t] = (S_V, S_V.n)

        sem_x = [new_dsem("d_x0"), new_dsem("d_x1")]
        sem_x2w = [new_dsem("d_x2w0"), new_dsem("d_x2w1")]
        x_load_t[0] = dma("sy", sem_x[0], xbuf[:, 0], xTr[:, :, 0:TILE])
        emit_ln_stats(0, is_first=True)
        emit_ln_sqrt(0)
        for sm, v in t_pd_attn:
            SY.wait(sm, v)

        def seg0_xp_lhsT(slot, n, k):
            return ring[:, slot, (n * 8 + k) * 32:(n * 8 + k + 1) * 32]

        def seg0_ww1_lhsT(slot, k):
            return ring[:, slot, 1280 + k * 64: 1280 + (k + 1) * 64]

        def seg_128(slot, idx):
            return ring[:, slot, idx * 128:(idx + 1) * 128]

        proj_plan = [("w", 1), ("r", 0), ("k", 2), ("v", 3), ("g", 4)]
        proj_out = {"r": (4, r_sb, "V"), "k": (4, k_sb, "V"),
                    "g": (8, g_sb, "S")}
        proj_segs = {"r": (1, 2), "k": (3, 2), "v": (5, 4), "g": (9, 4)}

        def xp_slice(n):
            return xpN[:, n, :]

        def emit_mix_front(t, slot0, tgt0):
            emit_ln_normalize(t)
            lb = lnb[:, t % 2]
            VE.wait_m(ln_ready[t])
            if t == 0:
                VE.wait_m(t_vec)
            VE.wait_m(hoist_done.get((t - 1, "xp_mm")))
            VE.wait_m(m_scrB_free[0])
            for fc in range(FC):
                VE.ins("scalar_tensor_tensor", xi_v(0)[:, fc, :],
                       delta[:, fc, :],
                       vec_sb[:, AV_MU0 + fc:AV_MU0 + fc + 1],
                       lb[:, fc, 8:TILE + 8], ALU.mult, ALU.add,
                       inc=(S_V if fc == FC - 1 else None))
            xi0_mark = S_V.n

            TE.wait_m(tgt0)
            TE.wait(S_V, xi0_mark)
            if t >= 1:
                SE.wait_m(xp_read[t - 1])
            for n in range(5):
                b = banks[2] if n % 2 == 0 else banks[3]
                TE.wait_m(b.rmark)
                for k in range(FC):
                    TE.ins("matmul", b.ps[0:32, :], seg0_xp_lhsT(slot0, n, k),
                           xi_v(0)[:, k, :], start=(k == 0), stop=(k == FC - 1),
                           inc=(S_T if k == FC - 1 else None))
                b.wmark = (S_T, S_T.n)
                if n == 4:
                    hoist_done[(t, "xp_mm")] = (S_T, S_T.n)
                SE.wait_m(b.wmark)
                SE.ins("activation", xpN[:, n, :], b.ps[0:32, :], AF.Tanh,
                       inc=S_S)
                b.rmark = (S_S, S_S.n)
            hoist_done[(t, "xp")] = (S_S, S_S.n)

            if t == 0:
                TE.wait_m(t_awx2)
                TE.wait_m(t_aww2)
            TE.wait(S_S, S_S.n)
            mu_idx = 1
            for fc in range(FC):
                b = banks[fc % 2]
                TE.wait_m(b.rmark)
                TE.ins("matmul", b.ps[:],
                       awx2_sb[:, (mu_idx * 8 + fc) * 128:(mu_idx * 8 + fc + 1) * 128],
                       xp_slice(mu_idx), start=True, stop=True, inc=S_T)
                b.wmark = (S_T, S_T.n)
                VE.wait_m(b.wmark)
                if fc == 0:
                    VE.wait_m(proj_mm_done.get((t - 1, 3)))
                VE.ins("scalar_tensor_tensor", xi1[:, fc, :], b.ps[:],
                       vec_sb[:, AV_XB + mu_idx * 8 + fc:AV_XB + mu_idx * 8 + fc + 1],
                       delta[:, fc, :], ALU.add, ALU.mult, inc=S_V)
                b.rmark = (S_V, S_V.n)
            VE.ins("tensor_tensor", xi1[:], xi1[:], lb[:, :, 8:TILE + 8],
                   ALU.add, inc=S_V)
            xiw_mark = S_V.n

            TE.wait(S_V, xiw_mark)
            b = banks[2]
            TE.wait_m(b.rmark)
            for k in range(FC):
                TE.ins("matmul", b.ps[0:64, :], seg0_ww1_lhsT(slot0, k),
                       xi1[:, k, :], start=(k == 0), stop=(k == FC - 1),
                       inc=(S_T if k == FC - 1 else None))
            b.wmark = (S_T, S_T.n)
            proj_mm_done[(t, 0)] = (S_T, S_T.n)
            slot_free[slot0] = (S_T, S_T.n)
            SE.wait_m(b.wmark)
            if t >= 1:
                SE.wait_m(th_read[t - 1])
            SE.ins("activation", th_sb[:], b.ps[0:64, :], AF.Tanh, inc=S_S)
            b.rmark = (S_S, S_S.n)
            th_mark = S_S.n
            TE.wait(S_S, th_mark)
            for oc in range(4):
                b = banks[3] if oc % 2 else banks[2]
                TE.wait_m(b.rmark)
                TE.ins("matmul", b.ps[:],
                       aww2_sb[:, oc * 128:(oc + 1) * 128], th_sb[:],
                       start=True, stop=True, inc=S_T)
                b.wmark = (S_T, S_T.n)
                if oc == 3:
                    th_read[t] = (S_T, S_T.n)
                SE.wait_m(b.wmark)
                if oc == 0 and t >= 1:
                    SE.wait_m(w_last_read[t - 1])
                SE.ins("activation", w_sb[:, oc, :], b.ps[:], AF.Exp,
                       bias=vec_sb[:, AV_BW2 + oc:AV_BW2 + oc + 1], inc=S_S)
                b.rmark = (S_S, S_S.n)
            projdone[(t, "w")] = (S_S, S_S.n)

        for t in range(nt):
            tp = t % 2
            if t + 1 < nt:
                if t >= 1 and x2_store_t[t - 1] is not None:
                    SY.wait_m(x2_store_t[t - 1])
                x_load_t[t + 1] = dma("sy", sem_x[(t + 1) % 2],
                                      xbuf[:, (t + 1) % 2],
                                      xTr[:, :, (t + 1) * TILE:(t + 2) * TILE])

            if preloaded[0] is not None:
                tile_slabs = preloaded[0]
                preloaded[0] = None
            elif t == 0:
                tile_slabs = dict(t0_slabs)
            else:
                tile_slabs = {}

            def load_segs(a, b):
                for seg in range(a, b):
                    tile_slabs[seg] = load_slab(wpa[seg])

            if 0 not in tile_slabs:
                load_segs(0, 1)
            if 1 not in tile_slabs:
                load_segs(1, 3)

            if t == 0 and do_mix:
                emit_mix_front(0, tile_slabs[0][0], tile_slabs[0][1])
            lb = lnb[:, tp]

            xi_mark = {}
            tmp_free = [None]

            def emit_mus(ni):
                name, mu_idx = proj_plan[ni]
                pipe = (ni + 1) % 2
                TE.wait_m(hoist_done[(t, "xp")])
                xi_free_wait = {1: hoist_done.get((t, "xp_mm")),
                                2: proj_mm_done.get((t, 0)),
                                3: proj_mm_done.get((t, 1)),
                                4: proj_mm_done.get((t, 2))}[ni]
                first_v_wait = tmp_free[0]
                if ni == 1 and t >= 1:
                    first_v_wait2 = o2v_prev[0]
                else:
                    first_v_wait2 = None
                for fc in range(FC):
                    b = banks[fc % 2]
                    TE.wait_m(b.rmark)
                    TE.ins("matmul", b.ps[:],
                           awx2_sb[:, (mu_idx * 8 + fc) * 128:(mu_idx * 8 + fc + 1) * 128],
                           xp_slice(mu_idx), start=True, stop=True, inc=S_T)
                    b.wmark = (S_T, S_T.n)
                    VE.wait_m(b.wmark)
                    if fc == 0 and first_v_wait is not None:
                        VE.wait_m(first_v_wait)
                    if fc == 0 and first_v_wait2 is not None:
                        VE.wait_m(first_v_wait2)
                    VE.ins("scalar_tensor_tensor", tmpm[:, fc, :], b.ps[:],
                           vec_sb[:, AV_XB + mu_idx * 8 + fc:AV_XB + mu_idx * 8 + fc + 1],
                           delta[:, fc, :], ALU.add, ALU.mult, inc=S_V)
                    b.rmark = (S_V, S_V.n)
                    if fc == FC - 1 and ni == 4:
                        xp_read[t] = (S_T, S_T.n)
                VE.wait_m(xi_free_wait)
                VE.ins("tensor_tensor", xi_v(pipe)[:],
                       tmpm[:].rearrange("p a b -> p a b"),
                       lb[:, :, 8:TILE + 8], ALU.add, inc=S_V)
                xi_mark[ni] = S_V.n
                tmp_free[0] = (S_V, S_V.n)

            def emit_proj(ni):
                name, mu_idx = proj_plan[ni]
                pipe = (ni + 1) % 2
                TE.wait(S_V, xi_mark[ni])
                if name == "w":
                    b = banks[2]
                    TE.wait_m(b.rmark)
                    for k in range(FC):
                        TE.ins("matmul", b.ps[0:64, :], seg0_ww1_lhsT(slot0, k),
                               xi_v(pipe)[:, k, :], start=(k == 0),
                               stop=(k == FC - 1),
                               inc=(S_T if k == FC - 1 else None))
                    b.wmark = (S_T, S_T.n)
                    proj_mm_done[(t, ni)] = (S_T, S_T.n)
                    SE.wait_m(b.wmark)
                    if t >= 1:
                        SE.wait_m(th_read[t - 1])
                    SE.ins("activation", th_sb[:], b.ps[0:64, :], AF.Tanh,
                           inc=S_S)
                    b.rmark = (S_S, S_S.n)
                    th_mark = S_S.n
                    if t == 0:
                        TE.wait_m(t_aww2)
                    TE.wait(S_S, th_mark)
                    for oc in range(4):
                        b = banks[3] if oc % 2 else banks[2]
                        TE.wait_m(b.rmark)
                        TE.ins("matmul", b.ps[:],
                               aww2_sb[:, oc * 128:(oc + 1) * 128], th_sb[:],
                               start=True, stop=True, inc=S_T)
                        b.wmark = (S_T, S_T.n)
                        if oc == 3:
                            th_read[t] = (S_T, S_T.n)
                        SE.wait_m(b.wmark)
                        if oc == 0 and t >= 1:
                            SE.wait_m(w_last_read[t - 1])
                        SE.ins("activation", w_sb[:, oc, :], b.ps[:],
                               AF.Exp,
                               bias=vec_sb[:, AV_BW2 + oc:AV_BW2 + oc + 1],
                               inc=S_S)
                        b.rmark = (S_S, S_S.n)
                    projdone[(t, "w")] = (S_S, S_S.n)
                    slot_free[tile_slabs[0][0]] = (S_T, th_read[t][1])
                elif name == "v":
                    for tc in range(NCH):
                        for half in range(2):
                            b = banks[3] if (2 * tc + half) % 2 else banks[2]
                            TE.wait_m(b.rmark)
                            for k in range(FC):
                                seg = 5 + half * 2 + k // 4
                                slot, tgt = tile_slabs[seg]
                                TE.wait_m(tgt)
                                TE.ins("matmul", b.ps[:],
                                       xi_v(pipe)[:, k, tc * C:(tc + 1) * C],
                                       ring[:, slot, (k % 4) * 512:(k % 4 + 1) * 512],
                                       start=(k == 0), stop=(k == FC - 1),
                                       inc=(S_T if k == FC - 1 else None))
                            b.wmark = (S_T, S_T.n)
                            if tc == NCH - 1 and half == 1:
                                proj_mm_done[(t, ni)] = (S_T, S_T.n)
                                for seg in range(5, 9):
                                    slot_free[tile_slabs[seg][0]] = (S_T, S_T.n)
                            SE.wait_m(b.wmark)
                            if tc == 0 and half == 0 and t >= 1:
                                SE.wait_m(v_last_read[t - 1])
                            for hh in range(2):
                                SE.ins("activation",
                                       vtt[:, tc, 2 * half + hh, :],
                                       b.ps[:, hh * DV:(hh + 1) * DV], AF.Copy,
                                       inc=S_S)
                            b.rmark = (S_S, S_S.n)
                    projdone[(t, "v")] = (S_S, S_S.n)
                else:
                    n_oc, out_sb, evac = proj_out[name]
                    base_seg, nseg = proj_segs[name]
                    oc_per_seg = n_oc // nseg
                    for oc in range(n_oc):
                        seg = base_seg + oc // oc_per_seg
                        slot, tgt = tile_slabs[seg]
                        oin = oc % oc_per_seg
                        TE.wait_m(tgt)
                        b = banks[3] if oc % 2 else banks[2]
                        TE.wait_m(b.rmark)
                        for k in range(FC):
                            TE.ins("matmul", b.ps[:], seg_128(slot, oin * FC + k),
                                   xi_v(pipe)[:, k, :], start=(k == 0),
                                   stop=(k == FC - 1),
                                   inc=(S_T if k == FC - 1 else None))
                        b.wmark = (S_T, S_T.n)
                        if oin == oc_per_seg - 1:
                            slot_free[slot] = (S_T, S_T.n)
                        if oc == n_oc - 1:
                            proj_mm_done[(t, ni)] = (S_T, S_T.n)
                        ee, se = (VE, S_V) if evac == "V" else (SE, S_S)
                        ee.wait_m(b.wmark)
                        if oc == 0 and evac == "S" and name == "v" and t >= 1:
                            ee.wait_m(v_last_read[t - 1])
                        if ee is VE:
                            ee.ins("tensor_copy", out_sb[:, oc, :], b.ps[:],
                                   inc=se)
                        else:
                            ee.ins("activation", out_sb[:, oc, :], b.ps[:],
                                   AF.Copy, inc=se)
                        b.rmark = (se, se.n)
                    projdone[(t, name)] = (se, se.n)

            if do_mix:
                emit_mus(1)
                emit_proj(1)
                load_segs(3, 5)
                emit_mus(2)
                emit_proj(2)
                load_segs(5, 9)
                emit_mus(3)
                emit_proj(3)
                load_segs(9, 13)
                emit_mus(4)
                emit_proj(4)
                load_segs(13, 17)
                projdone[(t, "lnb_free")] = (S_V, xi_mark[4])
            else:
                projdone[(t, "lnb_free")] = (S_V, ln_ready[t][1])

            do_scan_t = do_scan and do_mix
            u_b = vec_sb[:, AV_U:AV_U + NH].unsqueeze(2)
            sc_banks = [ps_m, banks[2], banks[3], ps_kv]

            for c in range(NCH if do_scan_t else 0):
                cl = slice(c * C, (c + 1) * C)
                w4 = w_sb[:, :, cl]
                r4 = r_sb[:, :, cl]
                k4 = k_sb[:, :, cl]
                if c == 0:
                    VE.wait_m(projdone[(t, "w")])
                    VE.wait_m(projdone[(t, "r")])
                    VE.wait_m(projdone[(t, "k")])
                    VE.wait_m(projdone[(t, "v")])
                    VE.wait_m(proj_mm_done[(t, 3)])
                    VE.wait_m(M_CONST_V)
                    VE.wait_m(M_CONST_GP)
                VE.wait_m(m_scan_sread[0])
                for h in range(NH):
                    VE.ins("tensor_tensor_scan", ics4[:, h, :], ones_f[:],
                           w_sb[:, h, cl], 0.0, ALU.mult, ALU.subtract)
                VE.ins("tensor_tensor", cs4[:], ics4[:], w4, ALU.add, inc=S_V)
                if c == NCH - 1:
                    w_last_read[t] = (S_V, S_V.n)
                VE.wait_m(arg_sread[0])
                VE.ins("tensor_tensor", arg4[0][:],
                       ics4[:, :, C - 1:C].broadcast_to((128, NH, C)), ics4[:],
                       ALU.subtract, inc=S_V)
                v1_mark = (S_V, S_V.n)
                SE.wait_m(v1_mark)
                if c == 0:
                    SE.wait_m(proj_mm_done[(t, 3)])
                SE.wait_m(exp_vread[0])
                SE.ins("activation", exp4[0][:], cs4[:], AF.Exp)
                SE.wait_m(exp_vread[1])
                SE.ins("activation", exp4[1][:], arg4[0][:], AF.Exp)
                arg_sread[0] = (S_S, S_S.n)
                SE.wait_m(m_supd_all[0])
                SE.ins("activation", dcol4[:].unsqueeze(2),
                       ics4[:, :, C - 1:C], AF.Exp, inc=S_S)
                s1_mark = (S_S, S_S.n)
                m_scan_sread[0] = s1_mark
                VE.wait_m(s1_mark)
                VE.wait_m(m_ogrp[0])
                VE.ins("tensor_tensor", rg4[:], r4, exp4[0][:], ALU.mult)
                exp_vread[0] = None
                VE.wait_m(m_tp_done[0])
                VE.wait_m(m_sc_read[0])
                VE.ins("tensor_tensor", khf4[:], k4, exp4[1][:], ALU.mult)
                exp_vread[1] = None
                VE.wait_m(m_dmm[0])
                VE.ins("tensor_tensor", td4[:], k4, u_b.broadcast_to((128, NH, C)),
                       ALU.mult)
                VE.ins("tensor_tensor", td4[:], td4[:], r4, ALU.mult, inc=S_V)
                v2_mark = (S_V, S_V.n)
                exp_vread[0] = (S_V, S_V.n)
                exp_vread[1] = (S_V, S_V.n)
                TE.wait_m(v2_mark)
                tp_banks = [ps_t7, banks[3]]
                for h in range(NH):
                    b = tp_banks[h % len(tp_banks)]
                    TE.wait_m(b.rmark)
                    TE.ins("transpose", tp_view(b), khf4[:, h, :], id_b[:],
                           inc=S_T)
                    b.wmark = (S_T, S_T.n)
                    SE.wait_m(b.wmark)
                    if h == 0:
                        SE.wait_m(m_kv_all[0])
                    SE.ins("activation", kht4[:, h, :], tp_view(b), AF.Copy,
                           inc=S_S)
                    b.rmark = (S_S, S_S.n)
                vt_mark = projdone[(t, "v")]
                m_tp_done[0] = (S_T, S_T.n)
                bc = ics4[:, :, SC - 1::SC]
                ics4d = ics4[:].rearrange("p h (b j) -> p h b j", j=SC)
                cs4d3 = cs4[:, :, SC:C].rearrange("p h (b j) -> p h b j", j=SC)
                kEv = arg4[1][:, :, 0:64].bitcast(BF16)
                pa = arg4[1][:, :, 64:74]
                pcx = arg4[1][:, :, 74:84]
                VE.wait_m(arg_sread[0])
                VE.ins("tensor_tensor",
                       arg4[0][:].rearrange("p h (b j) -> p h b j", j=SC),
                       bc.unsqueeze(3).broadcast_to((128, NH, NSUB, SC)),
                       ics4d, ALU.subtract)
                pidx = {}
                off = 0
                for si in range(NSUB):
                    rng = slice(off, off + si + 1)
                    for jb in range(si + 1):
                        pidx[(si, jb)] = off + jb
                    if si == 0:
                        VE.ins("tensor_scalar", pa[:, :, 0:1], bc[:, :, 0:1],
                               -1.0, None, ALU.mult)
                    else:
                        VE.ins("tensor_tensor", pa[:, :, rng],
                               bc[:, :, si - 1:si].broadcast_to((128, NH, si + 1)),
                               bc[:, :, 0:si + 1], ALU.subtract)
                    off += si + 1
                VE.ins("tensor_tensor",
                       cs4[:, :, SC:C].rearrange("p h (b j) -> p h b j", j=SC),
                       cs4d3,
                       bc[:, :, 0:3].unsqueeze(3).broadcast_to((128, NH, 3, SC)),
                       ALU.subtract, inc=S_V)
                varg_mark = (S_V, S_V.n)
                SE.wait_m(varg_mark)
                SE.wait_m(exp_vread[0])
                SE.ins("activation", exp4[0][:], arg4[0][:], AF.Exp)
                SE.ins("activation", pcx[:], pa[:], AF.Exp)
                SE.wait_m(exp_vread[1])
                SE.ins("activation", exp4[1][:, :, 0:96], cs4[:, :, SC:C],
                       AF.Exp, inc=S_S)
                arg_sread[0] = (S_S, S_S.n)
                exps2_mark = (S_S, S_S.n)
                m_scan_sread[0] = exps2_mark
                VE.wait_m(exps2_mark)
                VE.wait_m(m_sc_read[0])
                VE.ins("tensor_tensor", kEv[:], k4, exp4[0][:], ALU.mult)
                exp_vread[0] = None
                VE.ins("tensor_copy", rl4[:, :, 0:SC], rg4[:, :, 0:SC])
                VE.ins("tensor_tensor", rl4[:, :, SC:C], r4[:, :, SC:C],
                       exp4[1][:, :, 0:96], ALU.mult, inc=S_V)
                exp_vread[0] = (S_V, S_V.n)
                exp_vread[1] = (S_V, S_V.n)
                kE_mark = (S_V, S_V.n)
                sc_mm_mark = None
                kf_free = [m_sc_read[0], m_tp_done[0]]
                kf_bufs = [kf4, khf4]
                for si in range(NSUB):
                    kb = kf_bufs[si % 2]
                    hi = (si + 1) * SC
                    VE.wait_m(kf_free[si % 2])
                    for jb in range(si + 1):
                        VE.ins("tensor_tensor", kb[:, :, jb * SC:(jb + 1) * SC],
                               kEv[:, :, jb * SC:(jb + 1) * SC],
                               pcx[:, :, pidx[(si, jb)]:pidx[(si, jb)] + 1]
                               .broadcast_to((128, NH, SC)), ALU.mult)
                    if hi < C:
                        VE.ins("memset", kb[:, :, hi:C], 0.0, inc=S_V)
                    else:
                        VE.ins("tensor_copy", od1[0:1, 0:1], ones_f[0:1, 0:1],
                               inc=S_V)
                    mul_mark = (S_V, S_V.n)
                    TE.wait_m(mul_mark)
                    for h in range(NH):
                        b = sc_banks[h]
                        if si == 0:
                            TE.wait_m(b.rmark)
                        TE.ins("matmul", b.ps[:, 0:128][:, si * SC:hi],
                               kb[:, h, :], rl4[:, h, si * SC:hi],
                               start=True, stop=True, inc=S_T)
                        if si == NSUB - 1:
                            b.wmark = (S_T, S_T.n)
                    sc_mm_mark = (S_T, S_T.n)
                    kf_free[si % 2] = (S_T, S_T.n)
                m_sc_read[0] = (S_T, S_T.n)
                if c == NCH - 1:
                    m_scrB_free[0] = (S_T, S_T.n)
                for h in range(NH):
                    b = sc_banks[h]
                    VE.wait_m(b.wmark)
                    if h == 0:
                        VE.wait_m(m_ogrp[0])
                    VE.ins("tensor_tensor", mm4[:, h, :], b.ps[:, 0:128],
                           mask_su[:], ALU.mult, inc=S_V)
                    b.rmark = (S_V, S_V.n)
                mask_mark = (S_V, S_V.n)
                td_flat = td4[:].rearrange("p a b -> p (a b)")
                b2 = banks[2]
                TE.wait_m(v2_mark)
                TE.wait_m(b2.rmark)
                TE.ins("matmul", b2.ps[0:1, :], ones1_b[:], td_flat,
                       start=True, stop=True, inc=S_T)
                b2.wmark = (S_T, S_T.n)
                m_dmm[0] = (S_T, S_T.n)
                SE.wait_m(b2.wmark)
                SE.ins("activation", drow4[:], b2.ps[0:1, :], AF.Copy, inc=S_S)
                b2.rmark = (S_S, S_S.n)
                TE.wait_m(b2.rmark)
                for h in range(NH):
                    TE.ins("matmul", b2.ps[:, h:h + 1],
                           drow4[0:1, h * C:(h + 1) * C], one1_b[:],
                           start=True, stop=True,
                           inc=(S_T if h == NH - 1 else None))
                b2.wmark = (S_T, S_T.n)
                dcolp_mark = b2.wmark
                for h in range(NH):
                    ob = ps_o if h % 2 == 0 else ps_m
                    TE.wait_m(ob.rmark)
                    TE.wait_m(m_sbf_all[0])
                    TE.ins("matmul", ob.ps[:, 0:256], rg4[:, h, :],
                           Sb_sb[:, h, :], start=True, stop=False)
                    TE.wait_m(mask_mark)
                    TE.wait_m(vt_mark)
                    TE.ins("matmul", ob.ps[:, 0:256], mm4[:, h, :],
                           vtt[:, c, h, :], start=False, stop=True, inc=S_T)
                    ob.wmark = (S_T, S_T.n)
                    VE.wait_m(dcolp_mark)
                    VE.ins("tensor_tensor", od1[:], vtt[:, c, h, :],
                           b2.ps[:, h:h + 1].broadcast_to((128, 2 * C)),
                           ALU.mult)
                    VE.wait_m(ob.wmark)
                    VE.ins("tensor_tensor", osb4[:, h, :], ob.ps[:, 0:256],
                           od1[:], ALU.add, inc=S_V)
                    ob.rmark = (S_V, S_V.n)
                    if h == NH - 1:
                        b2.rmark = (S_V, S_V.n)
                        m_od_all[0] = (S_V, S_V.n)
                m_ogrp[0] = m_od_all[0]
                for h in range(NH):
                    kb = ps_kv if h % 2 == 0 else banks[3]
                    TE.wait_m(kb.rmark)
                    TE.ins("matmul", kb.ps[:, 0:256], kht4[:, h, :],
                           vtt[:, c, h, :], start=True, stop=True, inc=S_T)
                    if c == NCH - 1 and h == NH - 1:
                        v_last_read[t] = (S_T, S_T.n)
                    kb.wmark = (S_T, S_T.n)
                    VE.wait_m(kb.wmark)
                    VE.ins("scalar_tensor_tensor", S_sb[:, h, :], S_sb[:, h, :],
                           dcol4[:, h:h + 1], kb.ps[:, 0:256], ALU.mult,
                           ALU.add, inc=S_V)
                    kb.rmark = (S_V, S_V.n)
                m_supd_all[0] = (S_V, S_V.n)
                m_kv_all[0] = (S_T, S_T.n)
                SE.wait_m(m_supd_all[0])
                SE.ins("activation", Sb_sb[:], S_sb[:], AF.Copy, inc=S_S)
                m_sbf_all[0] = (S_S, S_S.n)
                VE.ins("tensor_reduce", sum4[:].unsqueeze(2), osb4[:], AX.X,
                       ALU.add)
                VE.ins("tensor_scalar", mean4[:], sum4[:], 1.0 / DV, None,
                       ALU.mult)
                VE.ins("tensor_tensor", cenb[:, c * NH:(c + 1) * NH, :],
                       osb4[:], mean4[:].unsqueeze(2).broadcast_to((128, NH, DV)),
                       ALU.subtract, inc=S_V)
                cen_mark = (S_V, S_V.n)
                for h in range(NH):
                    SE.wait_m(cen_mark)
                    SE.ins("activation", od1[:].bitcast(BF16)[:, 0:DV],
                           cenb[:, c * NH + h, :], AF.Square,
                           accum_out=sumv16[:, c * NH + h:c * NH + h + 1],
                           inc=(S_S if h == NH - 1 else None))
            scan_done = (S_S, S_S.n)

            if t + 1 < nt:
                emit_ln_stats(t + 1)
                sg_write_ok[t] = xsq_read[t + 1]
            else:
                sg_write_ok[t] = None

            if do_scan_t:
                VE.wait_m(scan_done)
                VE.ins("tensor_scalar", varg[:], sumv16[:], 1.0 / DV, EPS,
                       ALU.mult, ALU.add, inc=S_V)
                SE.wait(S_V, S_V.n)
                SE.ins("activation", stdg[:], varg[:], AF.Sqrt, inc=S_S)
                VE.wait(S_S, S_S.n)
                VE.ins("reciprocal", stdg[:], stdg[:], inc=S_V)
            rstd_mark = (S_V, S_V.n)
            if t + 1 < nt:
                emit_ln_sqrt(t + 1)

            do_epi_t = do_epi and do_scan_t
            if do_epi_t:
                SE.wait_m(projdone[(t, "g")])
                SE.wait_m(sg_write_ok[t])
                SE.wait_m(x2_mark[t - 2] if t >= 2 else None)
                for fc in range(FC):
                    SE.ins("activation", sgxsq[:, fc, :], g_sb[:, fc, :],
                           AF.Silu, inc=(S_S if fc == FC - 1 else None))
            sg_mark = (S_S, S_S.n)

            tp_banks = [ps_t7, ps_o]
            for c in range(NCH if (do_epi_t and epi_mode >= 2) else 0):
                cl = slice(c * C, (c + 1) * C)
                c4 = slice(c * NH, (c + 1) * NH)
                VE.wait_m(rstd_mark)
                VE.wait_m(m_ep_tp[0])
                if t >= 1 and c == 0:
                    VE.wait_m(o2_read[t - 1])
                VE.ins("tensor_tensor", onb4[:], cenb[:, c4, :],
                       stdg[:, c4].unsqueeze(2).broadcast_to((128, NH, DV)),
                       ALU.mult, inc=S_V)
                on_mark = (S_V, S_V.n)
                for h in range(NH):
                    for vc in range(2):
                        gi = 2 * h + vc
                        b = tp_banks[gi % 2]
                        TE.wait_m(b.rmark)
                        TE.wait_m(on_mark)
                        TE.ins("transpose", tp_view(b),
                               onb4[:, h, vc * C:(vc + 1) * C], id_b[:],
                               inc=S_T)
                        b.wmark = (S_T, S_T.n)
                        SE.wait_m(b.wmark)
                        SE.wait_m(sg_mark)
                        if gi == 0:
                            SE.wait_m(m_of_read[0])
                        SE.ins("activation", ofb4[:, gi, :], tp_view(b),
                               AF.Identity,
                               bias=vec_sb[:, AV_GNB + gi:AV_GNB + gi + 1],
                               scale=vec_sb[:, AV_GNW + gi:AV_GNW + gi + 1],
                               inc=S_S)
                        b.rmark = (S_S, S_S.n)
                m_ep_tp[0] = (S_T, S_T.n)
                VE.wait(S_S, S_S.n)
                VE.ins("tensor_tensor", o2_sb[:, :, cl], ofb4[:],
                       sgxsq[:, :, cl], ALU.mult, inc=S_V)
                m_of_read[0] = (S_V, S_V.n)
            o2_mark = (S_V, S_V.n)
            o2v_prev[0] = o2_mark

            if do_mix and t + 1 < nt:
                nslot = load_slab(wpa[0])
                preloaded[0] = {0: nslot}
                emit_mix_front(t + 1, nslot[0], nslot[1])

            for oc in range(FC if (do_epi_t and epi_mode >= 3) else 0):
                seg = 13 + oc // 2
                slot, tgt = tile_slabs[seg]
                TE.wait_m(tgt)
                b = banks[2] if oc % 2 == 0 else banks[3]
                TE.wait_m(b.rmark)
                TE.wait_m(o2_mark)
                for vc in range(FC):
                    TE.ins("matmul", b.ps[:], seg_128(slot, (oc % 2) * FC + vc),
                           o2_sb[:, vc, :], start=(vc == 0), stop=(vc == FC - 1),
                           inc=(S_T if vc == FC - 1 else None))
                b.wmark = (S_T, S_T.n)
                if oc % 2 == 1 or oc == FC - 1:
                    slot_free[slot] = (S_T, S_T.n)
                VE.wait_m(b.wmark)
                VE.ins("tensor_tensor", xbuf[:, tp, oc, :], b.ps[:],
                       xbuf[:, tp, oc, :], ALU.add, inc=S_V)
                b.rmark = (S_V, S_V.n)
            o2_read[t] = (S_T, S_T.n)
            x2_mark[t] = (S_V, S_V.n) if (do_epi_t and epi_mode >= 3) else ln_ready[t]

            if epi_mode >= 4 or not do_epi_t:
                SY.wait_m(x2_mark[t])
                x2_store_t[t] = dma("sy", sem_x2w[tp], x2s[t], xbuf[:, tp])

        TE.wait_m(ps_m.rmark)
        TE.ins("matmul", ps_m.ps[0:1, 0:1], one1_b[:], one1_b[:],
               start=True, stop=True, inc=S_PH)
        VE.ins("memset", dummy_sb[0:1, 0:1], 0.0, inc=S_PH)
        SE.ins("activation", dummy_sb[0:1, 1:2], eps_col[0:1, 0:1], AF.Copy,
               inc=S_PH)
        for mk in x2_store_t:
            if mk is not None:
                GE.wait_m(mk)
        GE.ins("memset", dummy_sb[0:1, 2:3], 0.0, inc=S_PH)
        PH = S_PH.n
        for ep in (TE, VE, SE, GE):
            ep.wait(S_PH, PH)

        esA.close()

        x2b = E(nc.sbuf_tensor("x2b", [128, FC, FT], F32))
        xbb = E(nc.sbuf_tensor("xbb", [128, FC, FT], BF16))
        h0b = E(nc.sbuf_tensor("h0b", [128, 32, FT], BF16))
        h1b = E(nc.sbuf_tensor("h1b", [128, 32, FT], BF16))
        sub = E(nc.sbuf_tensor("sub", [128, 2, 512], BF16))
        mean_B = h1b[:, 0:4, :].bitcast(F32).rearrange("p a b -> p (a b)")
        var_B = h1b[:, 4:8, :].bitcast(F32).rearrange("p a b -> p (a b)")
        cen_B = h1b[:, 8:12, :].bitcast(F32).rearrange("p a b -> p (a b)")
        xsq_B = h0b[:, 0:8, :]

        for sm, v in t_pd_ffn:
            SY.wait(sm, v)
        SE.wait_m(t_fsc)
        sem_x2r = new_dsem("d_x2r")
        sem_x2r2 = new_dsem("d_x2rb")
        sem_yoc = [new_dsem(f"d_y{o}") for o in range(FC)]

        su_read = [None, None]

        ybF = h1b[:, 0:16, :].bitcast(F32)
        x2nF = h1b[:, 16:32, :].bitcast(F32)
        x2n_dst = x2nF.rearrange("p (a b) t -> p b a t", b=2)
        sem_x2n = new_dsem("d_x2n")
        sem_x2n2 = new_dsem("d_x2nb")

        grp = [0]

        def next_bank():
            b = banks[grp[0] % 4]
            grp[0] += 1
            return b

        def emit_ln_pipelined(nx, w1w2_mm, h2_mm):
            GE.wait_m(h2_mm)
            if x2_store_t[2 * nx] is not None:
                GE.wait_m(x2_store_t[2 * nx])
            if x2_store_t[2 * nx + 1] is not None:
                GE.wait_m(x2_store_t[2 * nx + 1])
            tga = dma("gp", sem_x2n, x2n_dst[:, 0], x2s[2 * nx])
            tgb = dma("gp", sem_x2n2, x2n_dst[:, 1], x2s[2 * nx + 1])
            hms = []
            for hf in range(2):
                hs = slice(hf * 512, (hf + 1) * 512)
                SE.wait_m(tga if hf == 0 else tgb)
                if hf == 0:
                    SE.wait_m(w1w2_mm)
                for fc in range(FC):
                    SE.ins("activation", xbb[:, fc, hs],
                           x2nF[:, 2 * fc + hf, :], AF.Square,
                           inc=(S_S if fc == FC - 1 else None))
                sqm = S_S.n
                b0, b1 = next_bank(), next_bank()
                TE.wait_m(b0.rmark)
                TE.wait_m(tga if hf == 0 else tgb)
                for fc in range(FC):
                    TE.ins("matmul", b0.ps[:], ones_f[:],
                           x2nF[:, 2 * fc + hf, :],
                           start=(fc == 0), stop=(fc == FC - 1),
                           inc=(S_T if fc == FC - 1 else None))
                b0.wmark = (S_T, S_T.n)
                TE.wait_m(b1.rmark)
                TE.wait(S_S, sqm)
                for fc in range(FC):
                    TE.ins("matmul", b1.ps[:], ones_b[:], xbb[:, fc, hs],
                           start=(fc == 0), stop=(fc == FC - 1),
                           inc=(S_T if fc == FC - 1 else None))
                b1.wmark = (S_T, S_T.n)
                mh = x2b[:, 0, hs]
                vh = x2b[:, 1, hs]
                ch = x2b[:, 2, hs]
                VE.wait_m(b0.wmark)
                VE.ins("tensor_scalar", mh, b0.ps[:], 1.0 / HID, None,
                       ALU.mult, inc=S_V)
                b0.rmark = (S_V, S_V.n)
                VE.wait_m(b1.wmark)
                VE.ins("tensor_scalar", vh, b1.ps[:], 1.0 / HID, None,
                       ALU.mult, inc=S_V)
                b1.rmark = (S_V, S_V.n)
                VE.ins("tensor_tensor", ch, mh, mh, ALU.mult)
                VE.ins("tensor_tensor", vh, vh, ch, ALU.subtract, inc=S_V)
                SE.wait(S_V, S_V.n)
                SE.ins("activation", vh, vh, AF.Sqrt, bias=eps_col[:], inc=S_S)
                VE.wait(S_S, S_S.n)
                VE.ins("reciprocal", vh, vh)
                for fc in range(FC):
                    VE.ins("tensor_tensor", ch, x2nF[:, 2 * fc + hf, :], mh,
                           ALU.subtract)
                    VE.ins("tensor_tensor", xbb[:, fc, hs], ch, vh, ALU.mult,
                           inc=(S_V if fc == FC - 1 else None))
                hms.append((S_V, S_V.n))
            return hms

        y_store_t = [None] * NFT
        pending_half_marks = None
        for ft in range(nft):
            if ft == 0:
                for sm, v in t_pd_ffn:
                    GE.wait(sm, v)
                GE.wait_m(t_awx2)
                GE.wait_m(t_aww2)
                if x2_store_t[0] is not None:
                    GE.wait_m(x2_store_t[0])
                if x2_store_t[1] is not None:
                    GE.wait_m(x2_store_t[1])
            else:
                GE.wait_m(pending_half_marks[1])
            tgt_x2a = dma("gp", sem_x2r, x2b[:, :, 0:TILE], x2s[2 * ft])
            tgt_x2b = dma("gp", sem_x2r2, x2b[:, :, TILE:FT], x2s[2 * ft + 1])

            if ft == 0:
                half_marks = []
                for hf in range(2):
                    hs = slice(hf * 512, (hf + 1) * 512)
                    SE.wait_m(tgt_x2a if hf == 0 else tgt_x2b)
                    for fc in range(FC):
                        SE.ins("activation", xsq_B[:, fc, hs], x2b[:, fc, hs],
                               AF.Square, inc=(S_S if fc == FC - 1 else None))
                    sqm = S_S.n
                    b0, b1 = next_bank(), next_bank()
                    TE.wait_m(b0.rmark)
                    TE.wait_m(tgt_x2a if hf == 0 else tgt_x2b)
                    for fc in range(FC):
                        TE.ins("matmul", b0.ps[:], ones_f[:], x2b[:, fc, hs],
                               start=(fc == 0), stop=(fc == FC - 1),
                               inc=(S_T if fc == FC - 1 else None))
                    b0.wmark = (S_T, S_T.n)
                    TE.wait_m(b1.rmark)
                    TE.wait(S_S, sqm)
                    for fc in range(FC):
                        TE.ins("matmul", b1.ps[:], ones_b[:], xsq_B[:, fc, hs],
                               start=(fc == 0), stop=(fc == FC - 1),
                               inc=(S_T if fc == FC - 1 else None))
                    b1.wmark = (S_T, S_T.n)
                    mh, vh, ch = mean_B[:, hs], var_B[:, hs], cen_B[:, hs]
                    VE.wait_m(b0.wmark)
                    VE.ins("tensor_scalar", mh, b0.ps[:], 1.0 / HID, None,
                           ALU.mult, inc=S_V)
                    b0.rmark = (S_V, S_V.n)
                    VE.wait_m(b1.wmark)
                    VE.ins("tensor_scalar", vh, b1.ps[:], 1.0 / HID, None,
                           ALU.mult, inc=S_V)
                    b1.rmark = (S_V, S_V.n)
                    VE.ins("tensor_tensor", ch, mh, mh, ALU.mult)
                    VE.ins("tensor_tensor", vh, vh, ch, ALU.subtract, inc=S_V)
                    SE.wait(S_V, S_V.n)
                    SE.ins("activation", vh, vh, AF.Sqrt, bias=eps_col[:],
                           inc=S_S)
                    VE.wait(S_S, S_S.n)
                    VE.ins("reciprocal", vh, vh)
                    for fc in range(FC):
                        VE.ins("tensor_tensor", ch, x2b[:, fc, hs], mh,
                               ALU.subtract)
                        VE.ins("tensor_tensor", xbb[:, fc, hs], ch, vh,
                               ALU.mult,
                               inc=(S_V if fc == FC - 1 else None))
                    half_marks.append((S_V, S_V.n))
            else:
                half_marks = pending_half_marks

            for o in range(32):
                slot, tgt = load_slab(wpf[o])
                TE.wait_m(tgt)
                for hf in range(2):
                    hs = slice(hf * 512, (hf + 1) * 512)
                    bu = next_bank()
                    if o == 0:
                        TE.wait_m(half_marks[hf])
                    TE.wait_m(bu.rmark)
                    for k in range(FC):
                        TE.ins("matmul", bu.ps[:],
                               ring[:, slot, k * 128:(k + 1) * 128],
                               xbb[:, k, hs], start=(k == 0), stop=(k == FC - 1),
                               inc=(S_T if k == FC - 1 else None))
                    bu.wmark = (S_T, S_T.n)
                    bv = next_bank()
                    TE.wait_m(bv.rmark)
                    for k in range(FC):
                        TE.ins("matmul", bv.ps[:],
                               ring[:, slot, 1024 + k * 128:1024 + (k + 1) * 128],
                               xbb[:, k, hs], start=(k == 0), stop=(k == FC - 1),
                               inc=(S_T if k == FC - 1 else None))
                    bv.wmark = (S_T, S_T.n)
                    SE.wait_m(bu.wmark)
                    SE.wait_m(su_read[hf])
                    SE.ins("activation", sub[:, hf, :], bu.ps[:], AF.Silu,
                           scale=fsc_sb[:, FS_W1 + o:FS_W1 + o + 1], inc=S_S)
                    bu.rmark = (S_S, S_S.n)
                    VE.wait_m(bv.wmark)
                    VE.wait(S_S, S_S.n)
                    VE.ins("scalar_tensor_tensor", h0b[:, o, hs], bv.ps[:],
                           fsc_sb[:, FS_W2 + o:FS_W2 + o + 1], sub[:, hf, :],
                           ALU.mult, ALU.mult, inc=S_V)
                    bv.rmark = (S_V, S_V.n)
                    su_read[hf] = (S_V, S_V.n)
                slot_free[slot] = (S_T, S_T.n)
            h0_mark = (S_V, S_V.n)
            w1w2_mm = (S_T, S_T.n)

            def hidden_stage(base_slab, n_o, src, dst, scale_base, func,
                             first_waits, evac_v=False, per_oc_out=None,
                             ecb=None, first_v_waits=()):
                last = None
                for o in range(n_o):
                    slot1, tgt1 = load_slab(wpf[base_slab + 2 * o])
                    slot2, tgt2 = load_slab(wpf[base_slab + 2 * o + 1])
                    TE.wait_m(tgt1)
                    TE.wait_m(tgt2)
                    if o == 0:
                        for fw in first_waits:
                            TE.wait_m(fw)
                    for hf in range(2):
                        hs = slice(hf * 512, (hf + 1) * 512)
                        b = next_bank()
                        TE.wait_m(b.rmark)
                        for k in range(32):
                            slot = slot1 if k < 16 else slot2
                            kk = k % 16
                            TE.ins("matmul", b.ps[:],
                                   ring[:, slot, kk * 128:(kk + 1) * 128],
                                   src[:, k, hs], start=(k == 0), stop=(k == 31),
                                   inc=(S_T if k == 31 else None))
                        b.wmark = (S_T, S_T.n)
                        if evac_v:
                            VE.wait_m(b.wmark)
                            if o == 0 and hf == 0:
                                for fw in first_v_waits:
                                    VE.wait_m(fw)
                            ecb(o, hf, hs, b, scale_base)
                            b.rmark = (S_V, S_V.n)
                            last = (S_V, S_V.n)
                            if hf == 1 and per_oc_out is not None:
                                per_oc_out[o] = (S_V, S_V.n)
                        else:
                            SE.wait_m(b.wmark)
                            SE.ins("activation", dst[:, o, hs], b.ps[:], func,
                                   scale=fsc_sb[:, scale_base + o:scale_base + o + 1],
                                   inc=S_S)
                            b.rmark = (S_S, S_S.n)
                            last = (S_S, S_S.n)
                    slot_free[slot1] = (S_T, S_T.n)
                    slot_free[slot2] = (S_T, S_T.n)
                return last, (S_T, S_T.n)

            h1_waits = [h0_mark]
            if ft >= 1:
                h1_waits += y_store_t[ft - 1]
            h1_done, _ = hidden_stage(32, 32, h0b, h1b, FS_H1, AF.Gelu,
                                      h1_waits)
            h2_done, h2_mm = hidden_stage(96, 32, h1b, h0b, FS_H2, AF.Gelu,
                                          [h1_done])
            per_oc = [None] * FC

            def w3_evac(o, hf, hs, b, scale_base):
                VE.ins("scalar_tensor_tensor", ybF[:, 2 * o + hf, :],
                       b.ps[:],
                       fsc_sb[:, scale_base + o:scale_base + o + 1],
                       x2b[:, o, hs], ALU.mult, ALU.add, inc=S_V)

            y_done, w3_mm = hidden_stage(160, 8, h0b, None, FS_W3, None,
                                         [h2_done], evac_v=True,
                                         per_oc_out=per_oc, ecb=w3_evac,
                                         first_v_waits=(tgt_x2a, tgt_x2b))

            if ft + 1 < nft:
                pending_half_marks = emit_ln_pipelined(ft + 1, w1w2_mm, h2_mm)

            ysl = slice(ft * FT, (ft + 1) * FT)
            yoc_marks = []
            for o in range(FC):
                GE.wait_m(per_oc[o])
                src = ybF[:, 2 * o:2 * o + 2, :].rearrange("p a t -> p (a t)")
                yoc_marks.append(dma("gp", sem_yoc[o], yTr[:, o, ysl], src))
            y_store_t[ft] = yoc_marks

        for sm in all_dma_sems:
            if sm.n > 0:
                GE.wait(sm, sm.n)

        block = E(nc.Block())
        block.sync(lambda e: SY.run(e))
        block.tensor(lambda e: TE.run(e))
        block.vector(lambda e: VE.run(e))
        block.scalar(lambda e: SE.run(e))
        block.gpsimd(lambda e: GE.run(e))

    return nc



def make_in_maps(inputs):
    p_attn = {k[len("L0_attn_"):]: np.asarray(v, np.float32)
              for k, v in inputs.items()
              if k.startswith("L0_attn_") and k != "L0_attn_hidden_states"}
    p_ffn = {k[len("L0_ffn_"):]: np.asarray(v, np.float32)
             for k, v in inputs.items() if k.startswith("L0_ffn_")}
    x = np.asarray(inputs["L0_attn_hidden_states"], np.float32)

    aslabs = np.ascontiguousarray(pack_attn_slabs(p_attn))
    fslabs = np.ascontiguousarray(pack_ffn_slabs(p_ffn))
    common = {
        "aslab": aslabs,
        "fslab": fslabs,
        "awx2": np.ascontiguousarray(pack_awx2(p_attn)),
        "aww2": np.ascontiguousarray(pack_aww2(p_attn)),
        "avecs": np.ascontiguousarray(pack_avecs(p_attn)),
        "fscales": np.ascontiguousarray(pack_fscales(p_ffn)),
    }
    in_maps = []
    for b in range(NCORES):
        m = dict(common)
        m["xT"] = np.ascontiguousarray(x[b].T)
        in_maps.append(m)
    return in_maps


def kernel(**inputs):
    in_maps = make_in_maps(inputs)
    nc = build_nc()
    res = run_bass_kernel_spmd(nc, in_maps, core_ids=list(range(NCORES)))
    out = np.stack([np.ascontiguousarray(r["yT"].T) for r in res.results])
    return out.astype(np.float32)


# version 22 (final)
# speedup vs baseline: 1.1751x; 1.0917x over previous
import contextlib
import numpy as np
import concourse.bass as bass
import concourse.mybir as mybir
from concourse.bass_utils import run_bass_kernel_spmd

F32 = mybir.dt.float32
BF16 = mybir.dt.bfloat16
AF = mybir.ActivationFunctionType
ALU = mybir.AluOpType
AX = mybir.AxisListType

NCORES = 8
T = 4096
HID = 1024
FC = 8
TILE = 512
NT = T // TILE
C = 128
NCH = TILE // C
SC = 32
NSUB = C // SC
NH = 4
DK = 128
DV = 256
DH = 4096
FT = 1024
NFT = T // FT
SLABW = 2048
N_ASLAB = 17
N_FSLAB = 176
RING = 6
EPS = 1e-5



def _lhsT_chunks(w, kparts, oparts):
    O, K = w.shape
    noc, nkc = O // oparts, K // kparts
    t = w.reshape(noc, oparts, nkc, kparts)
    return np.ascontiguousarray(t.transpose(0, 2, 3, 1))


def pack_attn_slabs(p):
    slabs = np.zeros((N_ASLAB, 128, SLABW), np.float32)
    s0 = []
    wx0 = p["W_x0"]
    for n in range(5):
        for k in range(8):
            s0.append(wx0[n * 32:(n + 1) * 32, k * 128:(k + 1) * 128].T)
    ww1 = p["W_w1"]
    for k in range(8):
        s0.append(ww1[:, k * 128:(k + 1) * 128].T)
    col = 0
    for ch in s0:
        w = ch.shape[1]
        slabs[0, :, col:col + w] = ch
        col += w
    assert col == 1792

    def fill(weight, base_slab):
        chs = _lhsT_chunks(weight, 128, 128)
        noc, nkc = chs.shape[0], chs.shape[1]
        si, col = base_slab, 0
        for oc in range(noc):
            for kc in range(nkc):
                slabs[si, :, col:col + 128] = chs[oc, kc]
                col += 128
                if col == SLABW:
                    si += 1
                    col = 0

    fill(p["W_r"], 1)
    fill(p["W_k"], 3)
    wv = p["W_v"]
    si, col = 5, 0
    for half in range(2):
        for k in range(8):
            blk = wv[half * 512:(half + 1) * 512, k * 128:(k + 1) * 128].T
            slabs[si, :, col:col + 512] = blk
            col += 512
            if col == SLABW:
                si += 1
                col = 0
    fill(p["W_g"], 9)
    fill(p["W_o"], 13)
    return slabs


def pack_ffn_slabs(p):
    slabs = np.zeros((N_FSLAB, 128, SLABW), np.float32)
    w1 = _lhsT_chunks(p["w1_w"], 128, 128)
    w2 = _lhsT_chunks(p["w2_w"], 128, 128)
    for o in range(32):
        slabs[o, :, 0:1024] = w1[o].transpose(1, 0, 2).reshape(128, 1024)
        slabs[o, :, 1024:2048] = w2[o].transpose(1, 0, 2).reshape(128, 1024)
    h1 = _lhsT_chunks(p["h1_w"], 128, 128)
    h2 = _lhsT_chunks(p["h2_w"], 128, 128)
    for o in range(32):
        slabs[32 + 2 * o] = h1[o, 0:16].transpose(1, 0, 2).reshape(128, 2048)
        slabs[32 + 2 * o + 1] = h1[o, 16:32].transpose(1, 0, 2).reshape(128, 2048)
        slabs[96 + 2 * o] = h2[o, 0:16].transpose(1, 0, 2).reshape(128, 2048)
        slabs[96 + 2 * o + 1] = h2[o, 16:32].transpose(1, 0, 2).reshape(128, 2048)
    w3 = _lhsT_chunks(p["w3_w"], 128, 128)
    for o in range(8):
        slabs[160 + 2 * o] = w3[o, 0:16].transpose(1, 0, 2).reshape(128, 2048)
        slabs[160 + 2 * o + 1] = w3[o, 16:32].transpose(1, 0, 2).reshape(128, 2048)
    return slabs


def pack_awx2(p):
    out = np.zeros((32, 5 * 8 * 128), np.float32)
    wx2 = p["W_x2"]
    i = 0
    for n in range(5):
        for fc in range(8):
            out[:, i * 128:(i + 1) * 128] = \
                wx2[fc * 128:(fc + 1) * 128, n * 32:(n + 1) * 32].T
            i += 1
    return out


def pack_aww2(p):
    out = np.zeros((64, 4 * 128), np.float32)
    ww2 = p["W_w2"]
    for oc in range(4):
        out[:, oc * 128:(oc + 1) * 128] = ww2[oc * 128:(oc + 1) * 128, :].T
    return out


AV_MU0, AV_XB, AV_BW2, AV_U, AV_GNW, AV_GNB, AV_N = 0, 8, 48, 52, 56, 64, 72


def pack_avecs(p):
    out = np.zeros((128, AV_N), np.float32)
    out[:, AV_MU0:AV_MU0 + 8] = p["mu0"].reshape(8, 128).T
    for n in range(5):
        out[:, AV_XB + n * 8:AV_XB + (n + 1) * 8] = p["x_bias"][n].reshape(8, 128).T
    out[:, AV_BW2:AV_BW2 + 4] = p["b_w2"].reshape(4, 128).T
    out[:, AV_U:AV_U + 4] = p["bonus"].T
    out[:, AV_GNW:AV_GNW + 8] = p["gn_weight"].reshape(8, 128).T
    out[:, AV_GNB:AV_GNB + 8] = p["gn_bias"].reshape(8, 128).T
    return out


FS_W1, FS_W2, FS_H1, FS_H2, FS_W3, FS_N = 0, 32, 64, 96, 128, 136


def pack_fscales(p):
    out = np.zeros((128, FS_N), np.float32)
    out[:, FS_W1:FS_W1 + 32] = p["w1_s"].reshape(32, 128).T
    out[:, FS_W2:FS_W2 + 32] = p["w2_s"].reshape(32, 128).T
    out[:, FS_H1:FS_H1 + 32] = p["h1_s"].reshape(32, 128).T
    out[:, FS_H2:FS_H2 + 32] = p["h2_s"].reshape(32, 128).T
    out[:, FS_W3:FS_W3 + 8] = p["w3_s"].reshape(8, 128).T
    return out



class Sem:
    def __init__(self, handle):
        self.h = handle
        self.n = 0


class EP:
    def __init__(self, name):
        self.name = name
        self.cmds = []
        self.selfsem = None

    def ins(self, meth, *args, inc=None, inck=1, **kw):
        chain = self.selfsem is not None and meth != "dma_start"
        if chain and self.selfsem.n > 0:
            self.cmds.append(("wait_ge", (self.selfsem.h, self.selfsem.n), {},
                              None, 0))
        if inc is None and chain:
            inc = self.selfsem
            inck = 1
        if inc is not None:
            inc.n += inck
            self.cmds.append((meth, args, kw, inc.h, inck))
        else:
            self.cmds.append((meth, args, kw, None, 0))

    def wait(self, sem, val):
        if val > 0:
            self.cmds.append(("wait_ge", (sem.h, val), {}, None, 0))

    def wait_m(self, mark):
        if mark is not None:
            self.wait(mark[0], mark[1])

    def run(self, e):
        for idx, (meth, args, kw, inch, inck) in enumerate(self.cmds):
            try:
                r = getattr(e, meth)(*args, **kw)
            except Exception:
                print(f"FAILED {self.name} cmd#{idx} {meth} "
                      f"args={[getattr(a, 'shape', a) for a in args]}")
                raise
            if inch is not None:
                r.then_inc(inch, inck)


class Bank:
    def __init__(self, ps):
        self.ps = ps
        self.wmark = None
        self.rmark = None


def build_nc(debug_dump=(), nt=NT, nft=NFT, do_mix=True, do_scan=True, do_epi=True, epi_mode=4):
    nc = bass.Bass("TRN2", target_bir_lowering=False, debug=False)
    dd = set(debug_dump)

    xT = nc.dram_tensor("xT", [HID, T], F32, kind="ExternalInput").ap()
    aslab = nc.dram_tensor("aslab", [N_ASLAB, 128, SLABW], F32, kind="ExternalInput").ap()
    fslab = nc.dram_tensor("fslab", [N_FSLAB, 128, SLABW], F32, kind="ExternalInput").ap()
    awx2 = nc.dram_tensor("awx2", [32, 5 * 8 * 128], F32, kind="ExternalInput").ap()
    aww2 = nc.dram_tensor("aww2", [64, 4 * 128], F32, kind="ExternalInput").ap()
    avecs = nc.dram_tensor("avecs", [128, AV_N], F32, kind="ExternalInput").ap()
    fscales = nc.dram_tensor("fscales", [128, FS_N], F32, kind="ExternalInput").ap()
    yT = nc.dram_tensor("yT", [HID, T], F32, kind="ExternalOutput").ap()
    wpa = nc.dram_tensor("wpa", [N_ASLAB, 128, SLABW], BF16, kind="Internal").ap()
    wpf = nc.dram_tensor("wpf", [N_FSLAB, 128, SLABW], BF16, kind="Internal").ap()
    x2kind = "ExternalOutput" if "x2s" in dd else "Internal"
    x2s = nc.dram_tensor("x2s", [NT, 128, FC, TILE], F32, kind=x2kind).ap()

    xTr = xT.rearrange("(fc p) t -> p fc t", p=128)
    yTr = yT.rearrange("(fc p) t -> p fc t", p=128)

    TE, VE, SE, GE, SY = EP("tensor"), EP("vector"), EP("scalar"), EP("gpsimd"), EP("sync")
    globals()["_last_eps"] = {"TE": TE, "VE": VE, "SE": SE, "GE": GE, "SY": SY}

    es = contextlib.ExitStack()
    with es:
        E = es.enter_context

        S_T = Sem(E(nc.semaphore("s_t")))
        S_V = Sem(E(nc.semaphore("s_v")))
        S_S = Sem(E(nc.semaphore("s_s")))
        S_P = Sem(E(nc.semaphore("s_p")))
        S_PH = Sem(E(nc.semaphore("s_ph")))
        VE.selfsem = S_V
        SE.selfsem = S_S
        GE.selfsem = S_P

        all_dma_sems = []

        def new_dsem(name):
            sm = Sem(E(nc.semaphore(name)))
            all_dma_sems.append(sm)
            return sm

        def dma(q, sem, out, in_, **kw):
            ep = SY if q == "sy" else GE
            ep.ins("dma_start", out=out, in_=in_, inc=sem, inck=16, **kw)
            return (sem, sem.n)

        ring = E(nc.sbuf_tensor("ring", [128, RING, SLABW], BF16))
        ones_f = E(nc.sbuf_tensor("ones_f", [128, 128], F32))
        ones_b = E(nc.sbuf_tensor("ones_b", [128, 128], BF16))
        id_b = E(nc.sbuf_tensor("id_b", [128, 128], BF16))
        mask_su = E(nc.sbuf_tensor("mask_su", [128, 128], F32))
        ones1_b = E(nc.sbuf_tensor("ones1_b", [128, 1], BF16))
        one1_b = E(nc.sbuf_tensor("one1_b", [1, 1], BF16))
        eps_col = E(nc.sbuf_tensor("eps_col", [128, 1], F32))
        vec_sb = E(nc.sbuf_tensor("vec_sb", [128, AV_N], F32))
        fsc_sb = E(nc.sbuf_tensor("fsc_sb", [128, FS_N], F32))
        dummy_sb = E(nc.sbuf_tensor("dummy_sb", [128, 4], F32))

        banks = [Bank(E(nc.psum_tensor(f"pb{i}", [128, 512], F32))) for i in range(4)]
        ps_m = Bank(E(nc.psum_tensor("ps_m", [128, 256], F32)))
        ps_o = Bank(E(nc.psum_tensor("ps_o", [128, 256], F32)))
        ps_kv = Bank(E(nc.psum_tensor("ps_kv", [128, 256], F32)))
        ps_t7 = Bank(E(nc.psum_tensor("ps_t7", [128, 128], BF16)))

        def tp_view(bank):
            if bank is ps_t7:
                return ps_t7.ps[:]
            return bank.ps[:, 0:64].bitcast(BF16)

        hoist_done = {}

        VE.ins("memset", ones_b[:], 1.0)
        VE.ins("memset", ones1_b[:], 1.0)
        VE.ins("memset", one1_b[:], 1.0)
        VE.ins("memset", mask_su[:], 1.0, inc=S_V)
        M_CONST_VB = (S_V, S_V.n)
        GE.wait_m(M_CONST_VB)
        GE.ins("affine_select", id_b[:], ones_b[:], [[1, 128]], ALU.is_equal, 0.0,
               base=0, channel_multiplier=-1, inc=S_P)
        GE.ins("affine_select", mask_su[:], mask_su[:], [[1, 128]], ALU.is_gt, 0.0,
               base=0, channel_multiplier=-1, inc=S_P)
        M_CONST_GP = (S_P, S_P.n)

        sem_misc = new_dsem("d_misc")
        t_vec = dma("sy", sem_misc, vec_sb[:], avecs[:])
        t_fsc = dma("sy", sem_misc, fsc_sb[:], fscales[:])
        t_vec = t_fsc

        slot_free = [None] * RING
        slab_seq = [0]
        sem_ring = [new_dsem(f"d_ring{j}") for j in range(RING)]

        def load_slab(src_ap):
            i = slab_seq[0]
            slot = i % RING
            SY.wait_m(slot_free[slot])
            tgt = dma("sy", sem_ring[slot], ring[:, slot, :], src_ap)
            slab_seq[0] += 1
            return slot, tgt

        esA = contextlib.ExitStack()
        A = esA.enter_context
        pdb = A(nc.sbuf_tensor("pdb", [128, 2, SLABW], BF16))
        pd_srcs = [(aslab[i], wpa[i]) for i in range(N_ASLAB)]
        pd_srcs += [(fslab[i], wpf[i]) for i in range(N_FSLAB)]
        NPD = len(pd_srcs)
        sem_pdl = [new_dsem(f"d_pdl{j}") for j in range(2)]
        sem_pds = [new_dsem(f"d_pds{j}") for j in range(2)]
        pd_load = [None] * NPD
        pd_store = [None] * NPD
        t_awx2 = t_aww2 = None
        awx2_sb = A(nc.sbuf_tensor("awx2_sb", [32, 5 * 8 * 128], BF16))
        aww2_sb = A(nc.sbuf_tensor("aww2_sb", [64, 4 * 128], BF16))
        sem_aw = new_dsem("d_aw")
        sem_aw2 = new_dsem("d_aw2")
        t_awx2 = dma("gp", sem_aw, awx2_sb[:], awx2[:])
        t_aww2 = dma("gp", sem_aw2, aww2_sb[:], aww2[:])
        t0_slabs = {}
        sem_t0 = [new_dsem(f"d_t0s{j}") for j in range(3)]
        for seg in range(3):
            slot = slab_seq[0] % RING
            tgt = dma("gp", sem_t0[seg], ring[:, slot, :], aslab[seg])
            t0_slabs[seg] = (slot, tgt)
            slab_seq[0] += 1
        for i in range(NPD):
            j = i % 2
            if i >= 2:
                GE.wait_m(pd_store[i - 2])
            pd_load[i] = dma("gp", sem_pdl[j], pdb[:, j, :], pd_srcs[i][0])
            if i >= 1:
                GE.wait_m(pd_load[i - 1])
                pd_store[i - 1] = dma("gp", sem_pds[(i - 1) % 2],
                                      pd_srcs[i - 1][1], pdb[:, (i - 1) % 2, :])
        GE.wait_m(pd_load[NPD - 1])
        pd_store[NPD - 1] = dma("gp", sem_pds[(NPD - 1) % 2],
                                pd_srcs[NPD - 1][1], pdb[:, (NPD - 1) % 2, :])
        def pd_gate(upto):
            marks = {}
            for i in range(upto):
                sm, v = pd_store[i]
                marks[sm] = max(marks.get(sm, 0), v)
            return list(marks.items())
        t_pd_attn = pd_gate(N_ASLAB)
        t_pd_ffn = pd_gate(NPD)

        xbuf = A(nc.sbuf_tensor("xbuf", [128, 2, FC, TILE], F32))
        lnb = A(nc.sbuf_tensor("lnb", [128, 2, FC, TILE + 8], BF16))
        sgxsq = A(nc.sbuf_tensor("sgxsq", [128, FC, TILE], BF16))
        scrA = A(nc.sbuf_tensor("scrA", [128, 2048], F32))
        delta = scrA[:].bitcast(BF16).rearrange("p (a b) -> p a b", a=FC)
        ics4 = scrA[:, 0:512].rearrange("p (a b) -> p a b", a=NH)
        cs4 = scrA[:, 512:1024].rearrange("p (a b) -> p a b", a=NH)
        arg4 = [scrA[:, 1024 + i * 512:1536 + i * 512]
                .rearrange("p (a b) -> p a b", a=NH) for i in range(2)]
        scrB = A(nc.sbuf_tensor("scrB", [128, 2048], F32))
        xi0v = scrB[:].bitcast(BF16).rearrange("p (a b) -> p a b", a=FC)
        xi1 = A(nc.sbuf_tensor("xi1", [128, FC, TILE], BF16))

        def xi_v(pipe):
            return xi0v if pipe == 0 else xi1[:]

        exp4 = [scrB[:, i * 512:(i + 1) * 512]
                .rearrange("p (a b) -> p a b", a=NH) for i in range(2)]
        _scrBb = scrB[:, 1024:2048].bitcast(BF16)
        rg4 = _scrBb[:, 0:512].rearrange("p (a b) -> p a b", a=NH)
        rl4 = _scrBb[:, 512:1024].rearrange("p (a b) -> p a b", a=NH)
        khf4 = _scrBb[:, 1024:1536].rearrange("p (a b) -> p a b", a=NH)
        kf4 = _scrBb[:, 1536:2048].rearrange("p (a b) -> p a b", a=NH)
        tmpm = A(nc.sbuf_tensor("tmpm", [128, FC, TILE], BF16))
        cenb = tmpm[:].rearrange("p a (b c) -> p (a b) c", c=DV)
        xpN = A(nc.sbuf_tensor("xpN", [32, 5, TILE], BF16))
        th_sb = A(nc.sbuf_tensor("th_sb", [64, TILE], BF16))
        r_sb = A(nc.sbuf_tensor("r_sb", [128, NH, TILE], BF16))
        k_sb = A(nc.sbuf_tensor("k_sb", [128, NH, TILE], BF16))
        g_sb = A(nc.sbuf_tensor("g_sb", [128, FC, TILE], BF16))
        w_sb = A(nc.sbuf_tensor("w_sb", [128, NH, TILE], F32))
        o2_sb = A(nc.sbuf_tensor("o2_sb", [128, FC, TILE], BF16))
        mean_t = A(nc.sbuf_tensor("mean_t", [128, TILE], F32))
        var_t = A(nc.sbuf_tensor("var_t", [128, TILE], F32))
        cen_s = A(nc.sbuf_tensor("cen_s", [128, TILE], F32))
        td4 = A(nc.sbuf_tensor("td4", [128, NH, C], BF16))
        kht4 = A(nc.sbuf_tensor("kht4", [128, NH, C], BF16))
        vtt = A(nc.sbuf_tensor("vtt", [128, NCH, NH, DV], BF16))
        mm4 = A(nc.sbuf_tensor("mm4", [128, NH, C], BF16))
        drow4 = A(nc.sbuf_tensor("drow4", [1, NH * C], BF16))
        od1 = A(nc.sbuf_tensor("od1", [128, 2 * C], F32))
        osb4 = A(nc.sbuf_tensor("osb4", [128, NH, 2 * C], F32))
        dcol4 = A(nc.sbuf_tensor("dcol4", [128, NH], F32))
        sum4 = A(nc.sbuf_tensor("sum4", [128, NH], F32))
        mean4 = A(nc.sbuf_tensor("mean4", [128, NH], F32))
        sumv16 = A(nc.sbuf_tensor("sumv16", [128, NCH * NH], F32))
        S_sb = A(nc.sbuf_tensor("S_sb", [128, NH, DV], F32))
        Sb_sb = A(nc.sbuf_tensor("Sb_sb", [128, NH, DV], BF16))
        varg = A(nc.sbuf_tensor("varg", [128, NCH * NH], F32))
        stdg = A(nc.sbuf_tensor("stdg", [128, NCH * NH], F32))
        onb4 = A(nc.sbuf_tensor("onb4", [128, NH, 2 * C], BF16))
        ofb4 = A(nc.sbuf_tensor("ofb4", [128, FC, C], BF16))

        VE.ins("memset", ones_f[:], 1.0)
        VE.ins("memset", eps_col[:], EPS)
        VE.ins("memset", S_sb[:], 0.0)
        VE.ins("memset", Sb_sb[:], 0.0)
        VE.ins("memset", lnb[:, 0, :, 7:8], 0.0, inc=S_V)
        M_CONST_V = (S_V, S_V.n)

        x_load_t = [None] * (NT + 1)
        stats_pre = [None] * (NT + 1)
        std_ready = [None] * (NT + 1)
        ln_ready = [None] * (NT + 1)
        xsq_read = [None] * (NT + 1)
        o2_read = [None] * NT
        sg_write_ok = [None] * NT
        w_last_read = [None] * NT
        v_last_read = [None] * NT
        x2_mark = [None] * NT
        x2_store_t = [None] * NT
        projdone = {}
        proj_mm_done = {}
        th_read = [None] * NT
        xp_read = [None] * NT
        o2v_prev = [None]
        preloaded = [None]
        m_scan_sread = [None]
        m_ogrp = [None]
        m_tp_done = [None]
        m_kv_all = [None]
        m_od_all = [None]
        m_sbf_all = [M_CONST_V]
        m_supd_all = [None]
        m_scrB_free = [None]
        m_dmm = [None]
        m_sc_read = [None]
        m_ep_tp = [None]
        m_of_read = [None]
        arg_sread = [None, None]
        exp_vread = [None, None]

        def emit_ln_stats(t, is_first=False):
            xb = xbuf[:, t % 2]
            SE.wait_m(x_load_t[t])
            if t >= 2:
                SE.wait_m(o2_read[t - 2])
                SE.wait_m(x2_mark[t - 2])
            for fc in range(FC):
                SE.ins("activation", sgxsq[:, fc, :], xb[:, fc, :], AF.Square,
                       inc=(S_S if fc == FC - 1 else None))
            sq_mark = S_S.n
            b_sum, b_sumsq = banks[0], banks[1]
            TE.wait_m(b_sum.rmark)
            TE.wait_m(x_load_t[t])
            if is_first:
                TE.wait_m(M_CONST_V)
                TE.wait_m(M_CONST_GP)
            for fc in range(FC):
                TE.ins("matmul", b_sum.ps[:], ones_f[:], xb[:, fc, :],
                       start=(fc == 0), stop=(fc == FC - 1),
                       inc=(S_T if fc == FC - 1 else None))
            b_sum.wmark = (S_T, S_T.n)
            TE.wait_m(b_sumsq.rmark)
            TE.wait(S_S, sq_mark)
            for fc in range(FC):
                TE.ins("matmul", b_sumsq.ps[:], ones_b[:], sgxsq[:, fc, :],
                       start=(fc == 0), stop=(fc == FC - 1),
                       inc=(S_T if fc == FC - 1 else None))
            b_sumsq.wmark = (S_T, S_T.n)
            xsq_read[t] = (S_T, S_T.n)
            VE.wait_m(b_sum.wmark)
            VE.ins("tensor_scalar", mean_t[:], b_sum.ps[:], 1.0 / HID, None,
                   ALU.mult, inc=S_V)
            b_sum.rmark = (S_V, S_V.n)
            VE.wait_m(b_sumsq.wmark)
            VE.ins("tensor_scalar", var_t[:], b_sumsq.ps[:], 1.0 / HID, None,
                   ALU.mult, inc=S_V)
            b_sumsq.rmark = (S_V, S_V.n)
            VE.ins("tensor_tensor", cen_s[:], mean_t[:], mean_t[:], ALU.mult)
            VE.ins("tensor_tensor", var_t[:], var_t[:], cen_s[:], ALU.subtract,
                   inc=S_V)
            stats_pre[t] = (S_V, S_V.n)

        def emit_ln_sqrt(t):
            SE.wait_m(stats_pre[t])
            SE.ins("activation", var_t[:], var_t[:], AF.Sqrt, bias=eps_col[:],
                   inc=S_S)
            VE.wait(S_S, S_S.n)
            VE.ins("reciprocal", var_t[:], var_t[:], inc=S_V)
            std_ready[t] = (S_V, S_V.n)

        def emit_ln_normalize(t):
            xb = xbuf[:, t % 2]
            lb = lnb[:, t % 2]
            VE.wait_m(std_ready[t])
            VE.wait_m(m_scan_sread[0])
            if t >= 2:
                VE.wait_m(projdone.get((t - 2, "lnb_free")))
            for fc in range(FC):
                VE.ins("tensor_tensor", cen_s[:], xb[:, fc, :], mean_t[:],
                       ALU.subtract)
                VE.ins("tensor_tensor", lb[:, fc, 8:TILE + 8], cen_s[:], var_t[:],
                       ALU.mult)
            if t > 0:
                pb = lnb[:, (t - 1) % 2]
                for fc in range(FC):
                    VE.ins("tensor_copy", lb[:, fc, 7:8], pb[:, fc, TILE + 7:TILE + 8])
            for fc in range(FC):
                VE.ins("tensor_tensor", delta[:, fc, :], lb[:, fc, 7:TILE + 7],
                       lb[:, fc, 8:TILE + 8], ALU.subtract,
                       inc=(S_V if fc == FC - 1 else None))
            ln_ready[t] = (S_V, S_V.n)

        sem_x = [new_dsem("d_x0"), new_dsem("d_x1")]
        sem_x2w = [new_dsem("d_x2w0"), new_dsem("d_x2w1")]
        x_load_t[0] = dma("sy", sem_x[0], xbuf[:, 0], xTr[:, :, 0:TILE])
        emit_ln_stats(0, is_first=True)
        emit_ln_sqrt(0)
        for sm, v in t_pd_attn:
            SY.wait(sm, v)

        def seg0_xp_lhsT(slot, n, k):
            return ring[:, slot, (n * 8 + k) * 32:(n * 8 + k + 1) * 32]

        def seg0_ww1_lhsT(slot, k):
            return ring[:, slot, 1280 + k * 64: 1280 + (k + 1) * 64]

        def seg_128(slot, idx):
            return ring[:, slot, idx * 128:(idx + 1) * 128]

        proj_plan = [("w", 1), ("r", 0), ("k", 2), ("v", 3), ("g", 4)]
        proj_out = {"r": (4, r_sb, "V"), "k": (4, k_sb, "V"),
                    "g": (8, g_sb, "S")}
        proj_segs = {"r": (1, 2), "k": (3, 2), "v": (5, 4), "g": (9, 4)}

        def xp_slice(n):
            return xpN[:, n, :]

        def emit_mix_front(t, slot0, tgt0):
            emit_ln_normalize(t)
            lb = lnb[:, t % 2]
            VE.wait_m(ln_ready[t])
            if t == 0:
                VE.wait_m(t_vec)
            VE.wait_m(hoist_done.get((t - 1, "xp_mm")))
            VE.wait_m(m_scrB_free[0])
            for fc in range(FC):
                VE.ins("scalar_tensor_tensor", xi_v(0)[:, fc, :],
                       delta[:, fc, :],
                       vec_sb[:, AV_MU0 + fc:AV_MU0 + fc + 1],
                       lb[:, fc, 8:TILE + 8], ALU.mult, ALU.add,
                       inc=(S_V if fc == FC - 1 else None))
            xi0_mark = S_V.n

            TE.wait_m(tgt0)
            TE.wait(S_V, xi0_mark)
            if t >= 1:
                SE.wait_m(xp_read[t - 1])
            for n in range(5):
                b = banks[2] if n % 2 == 0 else banks[3]
                TE.wait_m(b.rmark)
                for k in range(FC):
                    TE.ins("matmul", b.ps[0:32, :], seg0_xp_lhsT(slot0, n, k),
                           xi_v(0)[:, k, :], start=(k == 0), stop=(k == FC - 1),
                           inc=(S_T if k == FC - 1 else None))
                b.wmark = (S_T, S_T.n)
                if n == 4:
                    hoist_done[(t, "xp_mm")] = (S_T, S_T.n)
                SE.wait_m(b.wmark)
                SE.ins("activation", xpN[:, n, :], b.ps[0:32, :], AF.Tanh,
                       inc=S_S)
                b.rmark = (S_S, S_S.n)
            hoist_done[(t, "xp")] = (S_S, S_S.n)

            if t == 0:
                TE.wait_m(t_awx2)
                TE.wait_m(t_aww2)
            TE.wait(S_S, S_S.n)
            mu_idx = 1
            for fc in range(FC):
                b = banks[fc % 2]
                TE.wait_m(b.rmark)
                TE.ins("matmul", b.ps[:],
                       awx2_sb[:, (mu_idx * 8 + fc) * 128:(mu_idx * 8 + fc + 1) * 128],
                       xp_slice(mu_idx), start=True, stop=True, inc=S_T)
                b.wmark = (S_T, S_T.n)
                VE.wait_m(b.wmark)
                if fc == 0:
                    VE.wait_m(proj_mm_done.get((t - 1, 3)))
                VE.ins("scalar_tensor_tensor", xi1[:, fc, :], b.ps[:],
                       vec_sb[:, AV_XB + mu_idx * 8 + fc:AV_XB + mu_idx * 8 + fc + 1],
                       delta[:, fc, :], ALU.add, ALU.mult, inc=S_V)
                b.rmark = (S_V, S_V.n)
            VE.ins("tensor_tensor", xi1[:], xi1[:], lb[:, :, 8:TILE + 8],
                   ALU.add, inc=S_V)
            xiw_mark = S_V.n

            TE.wait(S_V, xiw_mark)
            b = banks[2]
            TE.wait_m(b.rmark)
            for k in range(FC):
                TE.ins("matmul", b.ps[0:64, :], seg0_ww1_lhsT(slot0, k),
                       xi1[:, k, :], start=(k == 0), stop=(k == FC - 1),
                       inc=(S_T if k == FC - 1 else None))
            b.wmark = (S_T, S_T.n)
            proj_mm_done[(t, 0)] = (S_T, S_T.n)
            slot_free[slot0] = (S_T, S_T.n)
            SE.wait_m(b.wmark)
            if t >= 1:
                SE.wait_m(th_read[t - 1])
            SE.ins("activation", th_sb[:], b.ps[0:64, :], AF.Tanh, inc=S_S)
            b.rmark = (S_S, S_S.n)
            th_mark = S_S.n
            TE.wait(S_S, th_mark)
            for oc in range(4):
                b = banks[3] if oc % 2 else banks[2]
                TE.wait_m(b.rmark)
                TE.ins("matmul", b.ps[:],
                       aww2_sb[:, oc * 128:(oc + 1) * 128], th_sb[:],
                       start=True, stop=True, inc=S_T)
                b.wmark = (S_T, S_T.n)
                if oc == 3:
                    th_read[t] = (S_T, S_T.n)
                SE.wait_m(b.wmark)
                if oc == 0 and t >= 1:
                    SE.wait_m(w_last_read[t - 1])
                SE.ins("activation", w_sb[:, oc, :], b.ps[:], AF.Exp,
                       bias=vec_sb[:, AV_BW2 + oc:AV_BW2 + oc + 1], inc=S_S)
                b.rmark = (S_S, S_S.n)
            projdone[(t, "w")] = (S_S, S_S.n)

        for t in range(nt):
            tp = t % 2
            if t + 1 < nt:
                if t >= 1 and x2_store_t[t - 1] is not None:
                    SY.wait_m(x2_store_t[t - 1])
                x_load_t[t + 1] = dma("sy", sem_x[(t + 1) % 2],
                                      xbuf[:, (t + 1) % 2],
                                      xTr[:, :, (t + 1) * TILE:(t + 2) * TILE])

            if preloaded[0] is not None:
                tile_slabs = preloaded[0]
                preloaded[0] = None
            elif t == 0:
                tile_slabs = dict(t0_slabs)
            else:
                tile_slabs = {}

            def load_segs(a, b):
                for seg in range(a, b):
                    tile_slabs[seg] = load_slab(wpa[seg])

            if 0 not in tile_slabs:
                load_segs(0, 1)
            if 1 not in tile_slabs:
                load_segs(1, 3)

            if t == 0 and do_mix:
                emit_mix_front(0, tile_slabs[0][0], tile_slabs[0][1])
            lb = lnb[:, tp]

            xi_mark = {}
            tmp_free = [None]

            def emit_mus(ni):
                name, mu_idx = proj_plan[ni]
                pipe = (ni + 1) % 2
                TE.wait_m(hoist_done[(t, "xp")])
                xi_free_wait = {1: hoist_done.get((t, "xp_mm")),
                                2: proj_mm_done.get((t, 0)),
                                3: proj_mm_done.get((t, 1)),
                                4: proj_mm_done.get((t, 2))}[ni]
                first_v_wait = tmp_free[0]
                if ni == 1 and t >= 1:
                    first_v_wait2 = o2v_prev[0]
                else:
                    first_v_wait2 = None
                for fc in range(FC):
                    b = banks[fc % 2]
                    TE.wait_m(b.rmark)
                    TE.ins("matmul", b.ps[:],
                           awx2_sb[:, (mu_idx * 8 + fc) * 128:(mu_idx * 8 + fc + 1) * 128],
                           xp_slice(mu_idx), start=True, stop=True, inc=S_T)
                    b.wmark = (S_T, S_T.n)
                    VE.wait_m(b.wmark)
                    if fc == 0 and first_v_wait is not None:
                        VE.wait_m(first_v_wait)
                    if fc == 0 and first_v_wait2 is not None:
                        VE.wait_m(first_v_wait2)
                    VE.ins("scalar_tensor_tensor", tmpm[:, fc, :], b.ps[:],
                           vec_sb[:, AV_XB + mu_idx * 8 + fc:AV_XB + mu_idx * 8 + fc + 1],
                           delta[:, fc, :], ALU.add, ALU.mult, inc=S_V)
                    b.rmark = (S_V, S_V.n)
                    if fc == FC - 1 and ni == 4:
                        xp_read[t] = (S_T, S_T.n)
                VE.wait_m(xi_free_wait)
                VE.ins("tensor_tensor", xi_v(pipe)[:],
                       tmpm[:].rearrange("p a b -> p a b"),
                       lb[:, :, 8:TILE + 8], ALU.add, inc=S_V)
                xi_mark[ni] = S_V.n
                tmp_free[0] = (S_V, S_V.n)

            def emit_proj(ni):
                name, mu_idx = proj_plan[ni]
                pipe = (ni + 1) % 2
                TE.wait(S_V, xi_mark[ni])
                if name == "w":
                    b = banks[2]
                    TE.wait_m(b.rmark)
                    for k in range(FC):
                        TE.ins("matmul", b.ps[0:64, :], seg0_ww1_lhsT(slot0, k),
                               xi_v(pipe)[:, k, :], start=(k == 0),
                               stop=(k == FC - 1),
                               inc=(S_T if k == FC - 1 else None))
                    b.wmark = (S_T, S_T.n)
                    proj_mm_done[(t, ni)] = (S_T, S_T.n)
                    SE.wait_m(b.wmark)
                    if t >= 1:
                        SE.wait_m(th_read[t - 1])
                    SE.ins("activation", th_sb[:], b.ps[0:64, :], AF.Tanh,
                           inc=S_S)
                    b.rmark = (S_S, S_S.n)
                    th_mark = S_S.n
                    if t == 0:
                        TE.wait_m(t_aww2)
                    TE.wait(S_S, th_mark)
                    for oc in range(4):
                        b = banks[3] if oc % 2 else banks[2]
                        TE.wait_m(b.rmark)
                        TE.ins("matmul", b.ps[:],
                               aww2_sb[:, oc * 128:(oc + 1) * 128], th_sb[:],
                               start=True, stop=True, inc=S_T)
                        b.wmark = (S_T, S_T.n)
                        if oc == 3:
                            th_read[t] = (S_T, S_T.n)
                        SE.wait_m(b.wmark)
                        if oc == 0 and t >= 1:
                            SE.wait_m(w_last_read[t - 1])
                        SE.ins("activation", w_sb[:, oc, :], b.ps[:],
                               AF.Exp,
                               bias=vec_sb[:, AV_BW2 + oc:AV_BW2 + oc + 1],
                               inc=S_S)
                        b.rmark = (S_S, S_S.n)
                    projdone[(t, "w")] = (S_S, S_S.n)
                    slot_free[tile_slabs[0][0]] = (S_T, th_read[t][1])
                elif name == "v":
                    for tc in range(NCH):
                        for half in range(2):
                            b = banks[3] if (2 * tc + half) % 2 else banks[2]
                            TE.wait_m(b.rmark)
                            for k in range(FC):
                                seg = 5 + half * 2 + k // 4
                                slot, tgt = tile_slabs[seg]
                                TE.wait_m(tgt)
                                TE.ins("matmul", b.ps[:],
                                       xi_v(pipe)[:, k, tc * C:(tc + 1) * C],
                                       ring[:, slot, (k % 4) * 512:(k % 4 + 1) * 512],
                                       start=(k == 0), stop=(k == FC - 1),
                                       inc=(S_T if k == FC - 1 else None))
                            b.wmark = (S_T, S_T.n)
                            if tc == NCH - 1 and half == 1:
                                proj_mm_done[(t, ni)] = (S_T, S_T.n)
                                for seg in range(5, 9):
                                    slot_free[tile_slabs[seg][0]] = (S_T, S_T.n)
                            SE.wait_m(b.wmark)
                            if tc == 0 and half == 0 and t >= 1:
                                SE.wait_m(v_last_read[t - 1])
                            for hh in range(2):
                                SE.ins("activation",
                                       vtt[:, tc, 2 * half + hh, :],
                                       b.ps[:, hh * DV:(hh + 1) * DV], AF.Copy,
                                       inc=S_S)
                            b.rmark = (S_S, S_S.n)
                    projdone[(t, "v")] = (S_S, S_S.n)
                else:
                    n_oc, out_sb, evac = proj_out[name]
                    base_seg, nseg = proj_segs[name]
                    oc_per_seg = n_oc // nseg
                    for oc in range(n_oc):
                        seg = base_seg + oc // oc_per_seg
                        slot, tgt = tile_slabs[seg]
                        oin = oc % oc_per_seg
                        TE.wait_m(tgt)
                        b = banks[3] if oc % 2 else banks[2]
                        TE.wait_m(b.rmark)
                        for k in range(FC):
                            TE.ins("matmul", b.ps[:], seg_128(slot, oin * FC + k),
                                   xi_v(pipe)[:, k, :], start=(k == 0),
                                   stop=(k == FC - 1),
                                   inc=(S_T if k == FC - 1 else None))
                        b.wmark = (S_T, S_T.n)
                        if oin == oc_per_seg - 1:
                            slot_free[slot] = (S_T, S_T.n)
                        if oc == n_oc - 1:
                            proj_mm_done[(t, ni)] = (S_T, S_T.n)
                        ee, se = (VE, S_V) if evac == "V" else (SE, S_S)
                        ee.wait_m(b.wmark)
                        if oc == 0 and evac == "S" and name == "v" and t >= 1:
                            ee.wait_m(v_last_read[t - 1])
                        if ee is VE:
                            ee.ins("tensor_copy", out_sb[:, oc, :], b.ps[:],
                                   inc=se)
                        else:
                            ee.ins("activation", out_sb[:, oc, :], b.ps[:],
                                   AF.Copy, inc=se)
                        b.rmark = (se, se.n)
                    projdone[(t, name)] = (se, se.n)

            if do_mix:
                emit_mus(1)
                emit_proj(1)
                load_segs(3, 5)
                emit_mus(2)
                emit_proj(2)
                load_segs(5, 9)
                emit_mus(3)
                emit_proj(3)
                load_segs(9, 13)
                emit_mus(4)
                emit_proj(4)
                load_segs(13, 17)
                projdone[(t, "lnb_free")] = (S_V, xi_mark[4])
            else:
                projdone[(t, "lnb_free")] = (S_V, ln_ready[t][1])

            do_scan_t = do_scan and do_mix
            u_b = vec_sb[:, AV_U:AV_U + NH].unsqueeze(2)
            sc_banks = [ps_m, banks[2], banks[3], ps_kv]

            for c in range(NCH if do_scan_t else 0):
                cl = slice(c * C, (c + 1) * C)
                w4 = w_sb[:, :, cl]
                r4 = r_sb[:, :, cl]
                k4 = k_sb[:, :, cl]
                if c == 0:
                    VE.wait_m(projdone[(t, "w")])
                    VE.wait_m(projdone[(t, "r")])
                    VE.wait_m(projdone[(t, "k")])
                    VE.wait_m(projdone[(t, "v")])
                    VE.wait_m(proj_mm_done[(t, 3)])
                    VE.wait_m(M_CONST_V)
                    VE.wait_m(M_CONST_GP)
                VE.wait_m(m_scan_sread[0])
                for h in range(NH):
                    VE.ins("tensor_tensor_scan", ics4[:, h, :], ones_f[:],
                           w_sb[:, h, cl], 0.0, ALU.mult, ALU.subtract)
                VE.ins("tensor_tensor", cs4[:], ics4[:], w4, ALU.add, inc=S_V)
                if c == NCH - 1:
                    w_last_read[t] = (S_V, S_V.n)
                VE.wait_m(arg_sread[0])
                VE.ins("tensor_tensor", arg4[0][:],
                       ics4[:, :, C - 1:C].broadcast_to((128, NH, C)), ics4[:],
                       ALU.subtract, inc=S_V)
                v1_mark = (S_V, S_V.n)
                SE.wait_m(v1_mark)
                if c == 0:
                    SE.wait_m(proj_mm_done[(t, 3)])
                SE.wait_m(exp_vread[0])
                SE.ins("activation", exp4[0][:], cs4[:], AF.Exp)
                SE.wait_m(exp_vread[1])
                SE.ins("activation", exp4[1][:], arg4[0][:], AF.Exp)
                arg_sread[0] = (S_S, S_S.n)
                SE.wait_m(m_supd_all[0])
                SE.ins("activation", dcol4[:].unsqueeze(2),
                       ics4[:, :, C - 1:C], AF.Exp, inc=S_S)
                s1_mark = (S_S, S_S.n)
                m_scan_sread[0] = s1_mark
                VE.wait_m(s1_mark)
                VE.wait_m(m_ogrp[0])
                VE.ins("tensor_tensor", rg4[:], r4, exp4[0][:], ALU.mult)
                exp_vread[0] = None
                VE.wait_m(m_tp_done[0])
                VE.wait_m(m_sc_read[0])
                VE.ins("tensor_tensor", khf4[:], k4, exp4[1][:], ALU.mult)
                exp_vread[1] = None
                VE.wait_m(m_dmm[0])
                VE.ins("tensor_tensor", td4[:], k4, u_b.broadcast_to((128, NH, C)),
                       ALU.mult)
                VE.ins("tensor_tensor", td4[:], td4[:], r4, ALU.mult, inc=S_V)
                v2_mark = (S_V, S_V.n)
                exp_vread[0] = (S_V, S_V.n)
                exp_vread[1] = (S_V, S_V.n)
                TE.wait_m(v2_mark)
                tp_banks = [ps_t7, banks[3]]
                for h in range(NH):
                    b = tp_banks[h % len(tp_banks)]
                    TE.wait_m(b.rmark)
                    TE.ins("transpose", tp_view(b), khf4[:, h, :], id_b[:],
                           inc=S_T)
                    b.wmark = (S_T, S_T.n)
                    SE.wait_m(b.wmark)
                    if h == 0:
                        SE.wait_m(m_kv_all[0])
                    SE.ins("activation", kht4[:, h, :], tp_view(b), AF.Copy,
                           inc=S_S)
                    b.rmark = (S_S, S_S.n)
                vt_mark = projdone[(t, "v")]
                m_tp_done[0] = (S_T, S_T.n)
                bc = ics4[:, :, SC - 1::SC]
                ics4d = ics4[:].rearrange("p h (b j) -> p h b j", j=SC)
                cs4d3 = cs4[:, :, SC:C].rearrange("p h (b j) -> p h b j", j=SC)
                kEv = arg4[1][:, :, 0:64].bitcast(BF16)
                pa = arg4[1][:, :, 64:74]
                pcx = arg4[1][:, :, 74:84]
                VE.wait_m(arg_sread[0])
                VE.ins("tensor_tensor",
                       arg4[0][:].rearrange("p h (b j) -> p h b j", j=SC),
                       bc.unsqueeze(3).broadcast_to((128, NH, NSUB, SC)),
                       ics4d, ALU.subtract)
                pidx = {}
                off = 0
                for si in range(NSUB):
                    rng = slice(off, off + si + 1)
                    for jb in range(si + 1):
                        pidx[(si, jb)] = off + jb
                    if si == 0:
                        VE.ins("tensor_scalar", pa[:, :, 0:1], bc[:, :, 0:1],
                               -1.0, None, ALU.mult)
                    else:
                        VE.ins("tensor_tensor", pa[:, :, rng],
                               bc[:, :, si - 1:si].broadcast_to((128, NH, si + 1)),
                               bc[:, :, 0:si + 1], ALU.subtract)
                    off += si + 1
                VE.ins("tensor_tensor",
                       cs4[:, :, SC:C].rearrange("p h (b j) -> p h b j", j=SC),
                       cs4d3,
                       bc[:, :, 0:3].unsqueeze(3).broadcast_to((128, NH, 3, SC)),
                       ALU.subtract, inc=S_V)
                varg_mark = (S_V, S_V.n)
                SE.wait_m(varg_mark)
                SE.wait_m(exp_vread[0])
                SE.ins("activation", exp4[0][:], arg4[0][:], AF.Exp)
                SE.ins("activation", pcx[:], pa[:], AF.Exp)
                SE.wait_m(exp_vread[1])
                SE.ins("activation", exp4[1][:, :, 0:96], cs4[:, :, SC:C],
                       AF.Exp, inc=S_S)
                arg_sread[0] = (S_S, S_S.n)
                exps2_mark = (S_S, S_S.n)
                m_scan_sread[0] = exps2_mark
                VE.wait_m(exps2_mark)
                VE.wait_m(m_sc_read[0])
                VE.ins("tensor_tensor", kEv[:], k4, exp4[0][:], ALU.mult)
                exp_vread[0] = None
                VE.ins("tensor_copy", rl4[:, :, 0:SC], rg4[:, :, 0:SC])
                VE.ins("tensor_tensor", rl4[:, :, SC:C], r4[:, :, SC:C],
                       exp4[1][:, :, 0:96], ALU.mult, inc=S_V)
                exp_vread[0] = (S_V, S_V.n)
                exp_vread[1] = (S_V, S_V.n)
                kE_mark = (S_V, S_V.n)
                sc_mm_mark = None
                kf_free = [m_sc_read[0], m_tp_done[0]]
                kf_bufs = [kf4, khf4]
                for si in range(NSUB):
                    kb = kf_bufs[si % 2]
                    hi = (si + 1) * SC
                    VE.wait_m(kf_free[si % 2])
                    for jb in range(si + 1):
                        VE.ins("tensor_tensor", kb[:, :, jb * SC:(jb + 1) * SC],
                               kEv[:, :, jb * SC:(jb + 1) * SC],
                               pcx[:, :, pidx[(si, jb)]:pidx[(si, jb)] + 1]
                               .broadcast_to((128, NH, SC)), ALU.mult)
                    if hi < C:
                        VE.ins("memset", kb[:, :, hi:C], 0.0, inc=S_V)
                    else:
                        VE.ins("tensor_copy", od1[0:1, 0:1], ones_f[0:1, 0:1],
                               inc=S_V)
                    mul_mark = (S_V, S_V.n)
                    TE.wait_m(mul_mark)
                    for h in range(NH):
                        b = sc_banks[h]
                        if si == 0:
                            TE.wait_m(b.rmark)
                        TE.ins("matmul", b.ps[:, 0:128][:, si * SC:hi],
                               kb[:, h, :], rl4[:, h, si * SC:hi],
                               start=True, stop=True, inc=S_T)
                        if si == NSUB - 1:
                            b.wmark = (S_T, S_T.n)
                    sc_mm_mark = (S_T, S_T.n)
                    kf_free[si % 2] = (S_T, S_T.n)
                m_sc_read[0] = (S_T, S_T.n)
                if c == NCH - 1:
                    m_scrB_free[0] = (S_T, S_T.n)
                for h in range(NH):
                    b = sc_banks[h]
                    VE.wait_m(b.wmark)
                    if h == 0:
                        VE.wait_m(m_ogrp[0])
                    VE.ins("tensor_tensor", mm4[:, h, :], b.ps[:, 0:128],
                           mask_su[:], ALU.mult, inc=S_V)
                    b.rmark = (S_V, S_V.n)
                mask_mark = (S_V, S_V.n)
                td_flat = td4[:].rearrange("p a b -> p (a b)")
                b2 = banks[2]
                TE.wait_m(v2_mark)
                TE.wait_m(b2.rmark)
                TE.ins("matmul", b2.ps[0:1, :], ones1_b[:], td_flat,
                       start=True, stop=True, inc=S_T)
                b2.wmark = (S_T, S_T.n)
                m_dmm[0] = (S_T, S_T.n)
                SE.wait_m(b2.wmark)
                SE.ins("activation", drow4[:], b2.ps[0:1, :], AF.Copy, inc=S_S)
                b2.rmark = (S_S, S_S.n)
                TE.wait_m(b2.rmark)
                for h in range(NH):
                    TE.ins("matmul", b2.ps[:, h:h + 1],
                           drow4[0:1, h * C:(h + 1) * C], one1_b[:],
                           start=True, stop=True,
                           inc=(S_T if h == NH - 1 else None))
                b2.wmark = (S_T, S_T.n)
                dcolp_mark = b2.wmark
                for h in range(NH):
                    ob = ps_o if h % 2 == 0 else ps_m
                    TE.wait_m(ob.rmark)
                    TE.wait_m(m_sbf_all[0])
                    TE.ins("matmul", ob.ps[:, 0:256], rg4[:, h, :],
                           Sb_sb[:, h, :], start=True, stop=False)
                    TE.wait_m(mask_mark)
                    TE.wait_m(vt_mark)
                    TE.ins("matmul", ob.ps[:, 0:256], mm4[:, h, :],
                           vtt[:, c, h, :], start=False, stop=True, inc=S_T)
                    ob.wmark = (S_T, S_T.n)
                    VE.wait_m(dcolp_mark)
                    VE.ins("tensor_tensor", od1[:], vtt[:, c, h, :],
                           b2.ps[:, h:h + 1].broadcast_to((128, 2 * C)),
                           ALU.mult)
                    VE.wait_m(ob.wmark)
                    VE.ins("tensor_tensor", osb4[:, h, :], ob.ps[:, 0:256],
                           od1[:], ALU.add, inc=S_V)
                    ob.rmark = (S_V, S_V.n)
                    if h == NH - 1:
                        b2.rmark = (S_V, S_V.n)
                        m_od_all[0] = (S_V, S_V.n)
                m_ogrp[0] = m_od_all[0]
                for h in range(NH):
                    kb = ps_kv if h % 2 == 0 else banks[3]
                    TE.wait_m(kb.rmark)
                    TE.ins("matmul", kb.ps[:, 0:256], kht4[:, h, :],
                           vtt[:, c, h, :], start=True, stop=True, inc=S_T)
                    if c == NCH - 1 and h == NH - 1:
                        v_last_read[t] = (S_T, S_T.n)
                    kb.wmark = (S_T, S_T.n)
                    VE.wait_m(kb.wmark)
                    VE.ins("scalar_tensor_tensor", S_sb[:, h, :], S_sb[:, h, :],
                           dcol4[:, h:h + 1], kb.ps[:, 0:256], ALU.mult,
                           ALU.add, inc=S_V)
                    kb.rmark = (S_V, S_V.n)
                m_supd_all[0] = (S_V, S_V.n)
                m_kv_all[0] = (S_T, S_T.n)
                SE.wait_m(m_supd_all[0])
                SE.ins("activation", Sb_sb[:], S_sb[:], AF.Copy, inc=S_S)
                m_sbf_all[0] = (S_S, S_S.n)
                VE.ins("tensor_reduce", sum4[:].unsqueeze(2), osb4[:], AX.X,
                       ALU.add)
                VE.ins("tensor_scalar", mean4[:], sum4[:], 1.0 / DV, None,
                       ALU.mult)
                VE.ins("tensor_tensor", cenb[:, c * NH:(c + 1) * NH, :],
                       osb4[:], mean4[:].unsqueeze(2).broadcast_to((128, NH, DV)),
                       ALU.subtract, inc=S_V)
                cen_mark = (S_V, S_V.n)
                for h in range(NH):
                    SE.wait_m(cen_mark)
                    SE.ins("activation", od1[:].bitcast(BF16)[:, 0:DV],
                           cenb[:, c * NH + h, :], AF.Square,
                           accum_out=sumv16[:, c * NH + h:c * NH + h + 1],
                           inc=(S_S if h == NH - 1 else None))
            scan_done = (S_S, S_S.n)

            if t + 1 < nt:
                emit_ln_stats(t + 1)
                sg_write_ok[t] = xsq_read[t + 1]
            else:
                sg_write_ok[t] = None

            if do_scan_t:
                VE.wait_m(scan_done)
                VE.ins("tensor_scalar", varg[:], sumv16[:], 1.0 / DV, EPS,
                       ALU.mult, ALU.add, inc=S_V)
                SE.wait(S_V, S_V.n)
                SE.ins("activation", stdg[:], varg[:], AF.Sqrt, inc=S_S)
                VE.wait(S_S, S_S.n)
                VE.ins("reciprocal", stdg[:], stdg[:], inc=S_V)
            rstd_mark = (S_V, S_V.n)
            if t + 1 < nt:
                emit_ln_sqrt(t + 1)

            do_epi_t = do_epi and do_scan_t
            if do_epi_t:
                SE.wait_m(projdone[(t, "g")])
                SE.wait_m(sg_write_ok[t])
                SE.wait_m(x2_mark[t - 2] if t >= 2 else None)
                for fc in range(FC):
                    SE.ins("activation", sgxsq[:, fc, :], g_sb[:, fc, :],
                           AF.Silu, inc=(S_S if fc == FC - 1 else None))
            sg_mark = (S_S, S_S.n)

            tp_banks = [ps_t7, ps_o]
            for c in range(NCH if (do_epi_t and epi_mode >= 2) else 0):
                cl = slice(c * C, (c + 1) * C)
                c4 = slice(c * NH, (c + 1) * NH)
                VE.wait_m(rstd_mark)
                VE.wait_m(m_ep_tp[0])
                if t >= 1 and c == 0:
                    VE.wait_m(o2_read[t - 1])
                VE.ins("tensor_tensor", onb4[:], cenb[:, c4, :],
                       stdg[:, c4].unsqueeze(2).broadcast_to((128, NH, DV)),
                       ALU.mult, inc=S_V)
                on_mark = (S_V, S_V.n)
                for h in range(NH):
                    for vc in range(2):
                        gi = 2 * h + vc
                        b = tp_banks[gi % 2]
                        TE.wait_m(b.rmark)
                        TE.wait_m(on_mark)
                        TE.ins("transpose", tp_view(b),
                               onb4[:, h, vc * C:(vc + 1) * C], id_b[:],
                               inc=S_T)
                        b.wmark = (S_T, S_T.n)
                        SE.wait_m(b.wmark)
                        SE.wait_m(sg_mark)
                        if gi == 0:
                            SE.wait_m(m_of_read[0])
                        SE.ins("activation", ofb4[:, gi, :], tp_view(b),
                               AF.Identity,
                               bias=vec_sb[:, AV_GNB + gi:AV_GNB + gi + 1],
                               scale=vec_sb[:, AV_GNW + gi:AV_GNW + gi + 1],
                               inc=S_S)
                        b.rmark = (S_S, S_S.n)
                m_ep_tp[0] = (S_T, S_T.n)
                VE.wait(S_S, S_S.n)
                VE.ins("tensor_tensor", o2_sb[:, :, cl], ofb4[:],
                       sgxsq[:, :, cl], ALU.mult, inc=S_V)
                m_of_read[0] = (S_V, S_V.n)
            o2_mark = (S_V, S_V.n)
            o2v_prev[0] = o2_mark

            if do_mix and t + 1 < nt:
                nslot = load_slab(wpa[0])
                preloaded[0] = {0: nslot}
                emit_mix_front(t + 1, nslot[0], nslot[1])

            for oc in range(FC if (do_epi_t and epi_mode >= 3) else 0):
                seg = 13 + oc // 2
                slot, tgt = tile_slabs[seg]
                TE.wait_m(tgt)
                b = banks[2] if oc % 2 == 0 else banks[3]
                TE.wait_m(b.rmark)
                TE.wait_m(o2_mark)
                for vc in range(FC):
                    TE.ins("matmul", b.ps[:], seg_128(slot, (oc % 2) * FC + vc),
                           o2_sb[:, vc, :], start=(vc == 0), stop=(vc == FC - 1),
                           inc=(S_T if vc == FC - 1 else None))
                b.wmark = (S_T, S_T.n)
                if oc % 2 == 1 or oc == FC - 1:
                    slot_free[slot] = (S_T, S_T.n)
                VE.wait_m(b.wmark)
                VE.ins("tensor_tensor", xbuf[:, tp, oc, :], b.ps[:],
                       xbuf[:, tp, oc, :], ALU.add, inc=S_V)
                b.rmark = (S_V, S_V.n)
            o2_read[t] = (S_T, S_T.n)
            x2_mark[t] = (S_V, S_V.n) if (do_epi_t and epi_mode >= 3) else ln_ready[t]

            if epi_mode >= 4 or not do_epi_t:
                SY.wait_m(x2_mark[t])
                x2_store_t[t] = dma("sy", sem_x2w[tp], x2s[t], xbuf[:, tp])

        TE.wait_m(ps_m.rmark)
        TE.ins("matmul", ps_m.ps[0:1, 0:1], one1_b[:], one1_b[:],
               start=True, stop=True, inc=S_PH)
        VE.ins("memset", dummy_sb[0:1, 0:1], 0.0, inc=S_PH)
        SE.ins("activation", dummy_sb[0:1, 1:2], eps_col[0:1, 0:1], AF.Copy,
               inc=S_PH)
        for mk in x2_store_t:
            if mk is not None:
                GE.wait_m(mk)
        GE.ins("memset", dummy_sb[0:1, 2:3], 0.0, inc=S_PH)
        PH = S_PH.n
        for ep in (TE, VE, SE, GE):
            ep.wait(S_PH, PH)

        esA.close()

        x2b = E(nc.sbuf_tensor("x2b", [128, FC, FT], F32))
        xbb = E(nc.sbuf_tensor("xbb", [128, FC, FT], BF16))
        h0b = E(nc.sbuf_tensor("h0b", [128, 32, FT], BF16))
        h1b = E(nc.sbuf_tensor("h1b", [128, 32, FT], BF16))
        sub = E(nc.sbuf_tensor("sub", [128, 2, 512], BF16))
        mean_B = h1b[:, 0:4, :].bitcast(F32).rearrange("p a b -> p (a b)")
        var_B = h1b[:, 4:8, :].bitcast(F32).rearrange("p a b -> p (a b)")
        cen_B = h1b[:, 8:12, :].bitcast(F32).rearrange("p a b -> p (a b)")
        xsq_B = h0b[:, 0:8, :]

        for sm, v in t_pd_ffn:
            SY.wait(sm, v)
        SE.wait_m(t_fsc)
        sem_x2r = new_dsem("d_x2r")
        sem_x2r2 = new_dsem("d_x2rb")
        sem_yoc = [new_dsem(f"d_y{o}") for o in range(FC)]

        su_read = [None, None]

        ybF = h1b[:, 0:16, :].bitcast(F32)
        x2nF = h1b[:, 16:32, :].bitcast(F32)
        x2n_dst = x2nF.rearrange("p (a b) t -> p b a t", b=2)
        sem_x2n = new_dsem("d_x2n")
        sem_x2n2 = new_dsem("d_x2nb")

        grp = [0]

        def next_bank():
            b = banks[grp[0] % 4]
            grp[0] += 1
            return b

        def emit_ln_pipelined(nx, w1w2_mm, h2_mm):
            GE.wait_m(h2_mm)
            if x2_store_t[2 * nx] is not None:
                GE.wait_m(x2_store_t[2 * nx])
            if x2_store_t[2 * nx + 1] is not None:
                GE.wait_m(x2_store_t[2 * nx + 1])
            tga = dma("gp", sem_x2n, x2n_dst[:, 0], x2s[2 * nx])
            tgb = dma("gp", sem_x2n2, x2n_dst[:, 1], x2s[2 * nx + 1])
            hms = []
            for hf in range(2):
                hs = slice(hf * 512, (hf + 1) * 512)
                SE.wait_m(tga if hf == 0 else tgb)
                if hf == 0:
                    SE.wait_m(w1w2_mm)
                for fc in range(FC):
                    SE.ins("activation", xbb[:, fc, hs],
                           x2nF[:, 2 * fc + hf, :], AF.Square,
                           inc=(S_S if fc == FC - 1 else None))
                sqm = S_S.n
                b0, b1 = next_bank(), next_bank()
                TE.wait_m(b0.rmark)
                TE.wait_m(tga if hf == 0 else tgb)
                for fc in range(FC):
                    TE.ins("matmul", b0.ps[:], ones_f[:],
                           x2nF[:, 2 * fc + hf, :],
                           start=(fc == 0), stop=(fc == FC - 1),
                           inc=(S_T if fc == FC - 1 else None))
                b0.wmark = (S_T, S_T.n)
                TE.wait_m(b1.rmark)
                TE.wait(S_S, sqm)
                for fc in range(FC):
                    TE.ins("matmul", b1.ps[:], ones_b[:], xbb[:, fc, hs],
                           start=(fc == 0), stop=(fc == FC - 1),
                           inc=(S_T if fc == FC - 1 else None))
                b1.wmark = (S_T, S_T.n)
                mh = x2b[:, 0, hs]
                vh = x2b[:, 1, hs]
                ch = x2b[:, 2, hs]
                VE.wait_m(b0.wmark)
                VE.ins("tensor_scalar", mh, b0.ps[:], 1.0 / HID, None,
                       ALU.mult, inc=S_V)
                b0.rmark = (S_V, S_V.n)
                VE.wait_m(b1.wmark)
                VE.ins("tensor_scalar", vh, b1.ps[:], 1.0 / HID, None,
                       ALU.mult, inc=S_V)
                b1.rmark = (S_V, S_V.n)
                VE.ins("tensor_tensor", ch, mh, mh, ALU.mult)
                VE.ins("tensor_tensor", vh, vh, ch, ALU.subtract, inc=S_V)
                SE.wait(S_V, S_V.n)
                SE.ins("activation", vh, vh, AF.Sqrt, bias=eps_col[:], inc=S_S)
                VE.wait(S_S, S_S.n)
                VE.ins("reciprocal", vh, vh)
                for fc in range(FC):
                    VE.ins("tensor_tensor", ch, x2nF[:, 2 * fc + hf, :], mh,
                           ALU.subtract)
                    VE.ins("tensor_tensor", xbb[:, fc, hs], ch, vh, ALU.mult,
                           inc=(S_V if fc == FC - 1 else None))
                hms.append((S_V, S_V.n))
            return hms

        y_store_t = [None] * NFT
        pending_half_marks = None
        for ft in range(nft):
            if ft == 0:
                for sm, v in t_pd_ffn:
                    GE.wait(sm, v)
                GE.wait_m(t_awx2)
                GE.wait_m(t_aww2)
                if x2_store_t[0] is not None:
                    GE.wait_m(x2_store_t[0])
                if x2_store_t[1] is not None:
                    GE.wait_m(x2_store_t[1])
            else:
                GE.wait_m(prev_per_oc[FC - 1])
            tgt_x2a = dma("gp", sem_x2r, x2b[:, :, 0:TILE], x2s[2 * ft])
            tgt_x2b = dma("gp", sem_x2r2, x2b[:, :, TILE:FT], x2s[2 * ft + 1])

            if ft == 0:
                half_marks = []
                for hf in range(2):
                    hs = slice(hf * 512, (hf + 1) * 512)
                    SE.wait_m(tgt_x2a if hf == 0 else tgt_x2b)
                    for fc in range(FC):
                        SE.ins("activation", xsq_B[:, fc, hs], x2b[:, fc, hs],
                               AF.Square, inc=(S_S if fc == FC - 1 else None))
                    sqm = S_S.n
                    b0, b1 = next_bank(), next_bank()
                    TE.wait_m(b0.rmark)
                    TE.wait_m(tgt_x2a if hf == 0 else tgt_x2b)
                    for fc in range(FC):
                        TE.ins("matmul", b0.ps[:], ones_f[:], x2b[:, fc, hs],
                               start=(fc == 0), stop=(fc == FC - 1),
                               inc=(S_T if fc == FC - 1 else None))
                    b0.wmark = (S_T, S_T.n)
                    TE.wait_m(b1.rmark)
                    TE.wait(S_S, sqm)
                    for fc in range(FC):
                        TE.ins("matmul", b1.ps[:], ones_b[:], xsq_B[:, fc, hs],
                               start=(fc == 0), stop=(fc == FC - 1),
                               inc=(S_T if fc == FC - 1 else None))
                    b1.wmark = (S_T, S_T.n)
                    mh, vh, ch = mean_B[:, hs], var_B[:, hs], cen_B[:, hs]
                    VE.wait_m(b0.wmark)
                    VE.ins("tensor_scalar", mh, b0.ps[:], 1.0 / HID, None,
                           ALU.mult, inc=S_V)
                    b0.rmark = (S_V, S_V.n)
                    VE.wait_m(b1.wmark)
                    VE.ins("tensor_scalar", vh, b1.ps[:], 1.0 / HID, None,
                           ALU.mult, inc=S_V)
                    b1.rmark = (S_V, S_V.n)
                    VE.ins("tensor_tensor", ch, mh, mh, ALU.mult)
                    VE.ins("tensor_tensor", vh, vh, ch, ALU.subtract, inc=S_V)
                    SE.wait(S_V, S_V.n)
                    SE.ins("activation", vh, vh, AF.Sqrt, bias=eps_col[:],
                           inc=S_S)
                    VE.wait(S_S, S_S.n)
                    VE.ins("reciprocal", vh, vh)
                    for fc in range(FC):
                        VE.ins("tensor_tensor", ch, x2b[:, fc, hs], mh,
                               ALU.subtract)
                        VE.ins("tensor_tensor", xbb[:, fc, hs], ch, vh,
                               ALU.mult,
                               inc=(S_V if fc == FC - 1 else None))
                    half_marks.append((S_V, S_V.n))
            else:
                half_marks = pending_half_marks

            for o in range(32):
                slot, tgt = load_slab(wpf[o])
                TE.wait_m(tgt)
                for hf in range(2):
                    hs = slice(hf * 512, (hf + 1) * 512)
                    bu = next_bank()
                    if o == 0:
                        TE.wait_m(half_marks[hf])
                    TE.wait_m(bu.rmark)
                    for k in range(FC):
                        TE.ins("matmul", bu.ps[:],
                               ring[:, slot, k * 128:(k + 1) * 128],
                               xbb[:, k, hs], start=(k == 0), stop=(k == FC - 1),
                               inc=(S_T if k == FC - 1 else None))
                    bu.wmark = (S_T, S_T.n)
                    bv = next_bank()
                    TE.wait_m(bv.rmark)
                    for k in range(FC):
                        TE.ins("matmul", bv.ps[:],
                               ring[:, slot, 1024 + k * 128:1024 + (k + 1) * 128],
                               xbb[:, k, hs], start=(k == 0), stop=(k == FC - 1),
                               inc=(S_T if k == FC - 1 else None))
                    bv.wmark = (S_T, S_T.n)
                    SE.wait_m(bu.wmark)
                    SE.wait_m(su_read[hf])
                    SE.ins("activation", sub[:, hf, :], bu.ps[:], AF.Silu,
                           scale=fsc_sb[:, FS_W1 + o:FS_W1 + o + 1], inc=S_S)
                    bu.rmark = (S_S, S_S.n)
                    VE.wait_m(bv.wmark)
                    VE.wait(S_S, S_S.n)
                    VE.ins("scalar_tensor_tensor", h0b[:, o, hs], bv.ps[:],
                           fsc_sb[:, FS_W2 + o:FS_W2 + o + 1], sub[:, hf, :],
                           ALU.mult, ALU.mult, inc=S_V)
                    bv.rmark = (S_V, S_V.n)
                    su_read[hf] = (S_V, S_V.n)
                slot_free[slot] = (S_T, S_T.n)
            h0_mark = (S_V, S_V.n)
            w1w2_mm = (S_T, S_T.n)

            def hidden_stage(base_slab, n_o, src, dst, scale_base, func,
                             first_waits, evac_v=False, per_oc_out=None,
                             ecb=None, first_v_waits=(), o0=0):
                last = None
                for ol in range(n_o):
                    o = o0 + ol
                    slot1, tgt1 = load_slab(wpf[base_slab + 2 * ol])
                    slot2, tgt2 = load_slab(wpf[base_slab + 2 * ol + 1])
                    TE.wait_m(tgt1)
                    TE.wait_m(tgt2)
                    if ol == 0:
                        for fw in first_waits:
                            TE.wait_m(fw)
                    for hf in range(2):
                        hs = slice(hf * 512, (hf + 1) * 512)
                        b = next_bank()
                        TE.wait_m(b.rmark)
                        for k in range(32):
                            slot = slot1 if k < 16 else slot2
                            kk = k % 16
                            TE.ins("matmul", b.ps[:],
                                   ring[:, slot, kk * 128:(kk + 1) * 128],
                                   src[:, k, hs], start=(k == 0), stop=(k == 31),
                                   inc=(S_T if k == 31 else None))
                        b.wmark = (S_T, S_T.n)
                        if evac_v:
                            VE.wait_m(b.wmark)
                            if ol == 0 and hf == 0:
                                for fw in first_v_waits:
                                    VE.wait_m(fw)
                            ecb(o, hf, hs, b, scale_base)
                            b.rmark = (S_V, S_V.n)
                            last = (S_V, S_V.n)
                            if hf == 1 and per_oc_out is not None:
                                per_oc_out[o] = (S_V, S_V.n)
                        else:
                            SE.wait_m(b.wmark)
                            SE.ins("activation", dst[:, o, hs], b.ps[:], func,
                                   scale=fsc_sb[:, scale_base + o:scale_base + o + 1],
                                   inc=S_S)
                            b.rmark = (S_S, S_S.n)
                            last = (S_S, S_S.n)
                    slot_free[slot1] = (S_T, S_T.n)
                    slot_free[slot2] = (S_T, S_T.n)
                return last, (S_T, S_T.n)

            h1_waits = [h0_mark]
            if ft >= 1:
                h1_waits += y_store_t[ft - 1]
            h1_done, _ = hidden_stage(32, 32, h0b, h1b, FS_H1, AF.Gelu,
                                      h1_waits)
            h2_done, h2_mm = hidden_stage(96, 32, h1b, h0b, FS_H2, AF.Gelu,
                                          [h1_done])
            per_oc = [None] * FC

            def w3_evac(o, hf, hs, b, scale_base):
                VE.ins("scalar_tensor_tensor", ybF[:, 2 * o + hf, :],
                       b.ps[:],
                       fsc_sb[:, scale_base + o:scale_base + o + 1],
                       x2b[:, o, hs], ALU.mult, ALU.add, inc=S_V)

            _, _ = hidden_stage(160, 4, h0b, None, FS_W3, None,
                                [h2_done], evac_v=True,
                                per_oc_out=per_oc, ecb=w3_evac,
                                first_v_waits=(tgt_x2a, tgt_x2b))
            if ft + 1 < nft:
                pending_half_marks = emit_ln_pipelined(ft + 1, w1w2_mm, h2_mm)
            y_done, w3_mm = hidden_stage(168, 4, h0b, None, FS_W3, None,
                                         [], evac_v=True,
                                         per_oc_out=per_oc, ecb=w3_evac,
                                         o0=4)

            ysl = slice(ft * FT, (ft + 1) * FT)
            yoc_marks = []
            for o in range(FC):
                GE.wait_m(per_oc[o])
                src = ybF[:, 2 * o:2 * o + 2, :].rearrange("p a t -> p (a t)")
                yoc_marks.append(dma("gp", sem_yoc[o], yTr[:, o, ysl], src))
            y_store_t[ft] = yoc_marks
            prev_per_oc = per_oc

        for sm in all_dma_sems:
            if sm.n > 0:
                GE.wait(sm, sm.n)

        block = E(nc.Block())
        block.sync(lambda e: SY.run(e))
        block.tensor(lambda e: TE.run(e))
        block.vector(lambda e: VE.run(e))
        block.scalar(lambda e: SE.run(e))
        block.gpsimd(lambda e: GE.run(e))

    return nc



def make_in_maps(inputs):
    p_attn = {k[len("L0_attn_"):]: np.asarray(v, np.float32)
              for k, v in inputs.items()
              if k.startswith("L0_attn_") and k != "L0_attn_hidden_states"}
    p_ffn = {k[len("L0_ffn_"):]: np.asarray(v, np.float32)
             for k, v in inputs.items() if k.startswith("L0_ffn_")}
    x = np.asarray(inputs["L0_attn_hidden_states"], np.float32)

    aslabs = np.ascontiguousarray(pack_attn_slabs(p_attn))
    fslabs = np.ascontiguousarray(pack_ffn_slabs(p_ffn))
    common = {
        "aslab": aslabs,
        "fslab": fslabs,
        "awx2": np.ascontiguousarray(pack_awx2(p_attn)),
        "aww2": np.ascontiguousarray(pack_aww2(p_attn)),
        "avecs": np.ascontiguousarray(pack_avecs(p_attn)),
        "fscales": np.ascontiguousarray(pack_fscales(p_ffn)),
    }
    in_maps = []
    for b in range(NCORES):
        m = dict(common)
        m["xT"] = np.ascontiguousarray(x[b].T)
        in_maps.append(m)
    return in_maps


def kernel(**inputs):
    in_maps = make_in_maps(inputs)
    nc = build_nc()
    res = run_bass_kernel_spmd(nc, in_maps, core_ids=list(range(NCORES)))
    out = np.stack([np.ascontiguousarray(r["yT"].T) for r in res.results])
    return out.astype(np.float32)
